# version 08
# baseline (speedup 1.0000x reference)
import numpy as np
import ml_dtypes
import concourse.bass as bass
import concourse.mybir as mybir
from concourse.bass_utils import run_bass_kernel_spmd

F32 = mybir.dt.float32
BF16 = mybir.dt.bfloat16
I32 = mybir.dt.int32
AF = mybir.ActivationFunctionType
ALU = mybir.AluOpType
AX = mybir.AxisListType

NCORES = 8
B, T, D = 16, 2048, 1024
SEQ_PER_CORE = B // NCORES
R = SEQ_PER_CORE * T
DFF = 3072
NEXP = 8
DEXP = 1024
H0, M0, DK0 = 4, 64, 256
H1, DH1 = 16, 64
CHUNK = 128

COMPUTE_ENGINES = ("tensor", "vector", "scalar", "gpsimd")
ALL_ENGINES = ("tensor", "vector", "scalar", "gpsimd", "sync")


class Rec:
    __slots__ = ("idx", "eng", "fn", "reads", "writes", "inc", "is_dma",
                 "is_memset", "edges", "planned_inc", "cum", "tag")

    def __init__(self, idx, eng, fn, reads, writes, inc, is_dma, is_memset, tag):
        self.idx = idx
        self.eng = eng
        self.fn = fn
        self.reads = reads
        self.writes = writes
        self.inc = inc
        self.is_dma = is_dma
        self.is_memset = is_memset
        self.edges = set()
        self.planned_inc = False
        self.cum = 0
        self.tag = tag


class MK:

    def __init__(self, nc):
        self.nc = nc
        self.recs = []
        self.last_writer = {}
        self.readers = {}
        self.collective_ctr = 0
        self._const_tiles = {}
        self._stack = None
        self.barriers = []

    def barrier(self):
        self.barriers.append(len(self.recs))

    @staticmethod
    def key(handle):
        return id(handle)

    def emit(self, eng, fn, reads=(), writes=(), dma=False, memset=False, tag=""):
        idx = len(self.recs)
        rec = Rec(idx, eng, fn, [self.key(b) for b in reads],
                  [self.key(b) for b in writes], 16 if dma else 1, dma, memset, tag)
        for k in rec.reads:
            lw = self.last_writer.get(k)
            if lw is not None and self._needs_edge(self.recs[lw], rec):
                rec.edges.add(lw)
        for k in rec.writes:
            lw = self.last_writer.get(k)
            if lw is not None and self._needs_edge(self.recs[lw], rec, war=True):
                rec.edges.add(lw)
            for rd in self.readers.get(k, ()):
                if rd != idx and self._needs_edge(self.recs[rd], rec, war=True):
                    rec.edges.add(rd)
        for k in rec.reads:
            self.readers.setdefault(k, []).append(idx)
        for k in rec.writes:
            self.last_writer[k] = idx
            self.readers[k] = []
        self.recs.append(rec)
        return rec

    @staticmethod
    def _needs_edge(src, dst, war=False):
        if src.eng != dst.eng:
            return True
        if src.is_dma or dst.is_dma:
            return True
        if src.eng == "tensor":
            return False
        if war:
            return True
        if src.is_memset:
            return True
        return False

    def dma(self, eng, out, in_, reads, writes, tag=""):
        self.emit(eng, lambda e, out=out, in_=in_: e.dma_start(out=out, in_=in_),
                  reads=reads, writes=writes, dma=True, tag=tag)

    def build_block(self):
        recs = self.recs
        for r in recs:
            if r.is_dma:
                r.planned_inc = True
        for r in recs:
            for e in r.edges:
                recs[e].planned_inc = True
        for b in self.barriers:
            seen = set()
            for i in range(b - 1, -1, -1):
                r = recs[i]
                if r.eng not in seen:
                    seen.add(r.eng)
                    if r.inc > 0:
                        r.planned_inc = True
                if len(seen) == len(ALL_ENGINES):
                    break
        cums = {e: 0 for e in ALL_ENGINES}
        barrier_snap = {}
        bi = 0
        for r in recs:
            while bi < len(self.barriers) and self.barriers[bi] <= r.idx:
                barrier_snap[self.barriers[bi]] = dict(cums)
                bi += 1
            if r.planned_inc:
                cums[r.eng] += r.inc
            r.cum = cums[r.eng]
        while bi < len(self.barriers):
            barrier_snap[self.barriers[bi]] = dict(cums)
            bi += 1
        dma_cum_so_far = {e: 0 for e in ALL_ENGINES}
        dma_issued_before = []
        for r in recs:
            dma_issued_before.append(dict(dma_cum_so_far))
            if r.is_dma and r.planned_inc:
                dma_cum_so_far[r.eng] = r.cum
        waits = []
        bi = 0
        cur_barrier = None
        for r in recs:
            while bi < len(self.barriers) and self.barriers[bi] <= r.idx:
                cur_barrier = barrier_snap[self.barriers[bi]]
                bi += 1
            w = {}
            if cur_barrier is not None:
                for e2, v2 in cur_barrier.items():
                    if v2 > 0:
                        w[e2] = v2
            for e in r.edges:
                p = recs[e]
                if p.is_dma:
                    v = dma_issued_before[r.idx][p.eng]
                    assert v >= p.cum
                else:
                    v = p.cum
                if w.get(p.eng, 0) < v:
                    w[p.eng] = v
            waits.append(w)
        self._waits = waits

        sems = {}
        block = None

        nc = self.nc
        from contextlib import ExitStack
        stack = ExitStack()
        self._stack = stack
        for e in ALL_ENGINES:
            sems[e] = stack.enter_context(nc.semaphore(f"sem_{e}"))
        self.sems = sems
        block = stack.enter_context(nc.Block())
        total = {e: cums[e] for e in ALL_ENGINES}

        def run_engine(obj, name):
            last_waited = {e: 0 for e in ALL_ENGINES}
            for r, w in zip(recs, waits):
                if r.eng != name:
                    continue
                for e, v in sorted(w.items()):
                    if v > last_waited[e]:
                        obj.wait_ge(sems[e], v)
                        last_waited[e] = v
                inst = r.fn(obj)
                if r.planned_inc and r.inc > 0:
                    if r.is_dma or r.eng == "tensor":
                        inst.then_inc(sems[r.eng], r.inc)
                    else:
                        obj.drain(fusable=True).then_inc(sems[r.eng], r.inc)
            for e in ALL_ENGINES:
                if total[e] > 0 and total[e] > last_waited[e]:
                    obj.wait_ge(sems[e], total[e])

        @block.gpsimd
        def _(g):
            run_engine(g, "gpsimd")

        @block.vector
        def _(v):
            run_engine(v, "vector")

        @block.scalar
        def _(s):
            run_engine(s, "scalar")

        @block.tensor
        def _(t):
            run_engine(t, "tensor")

        @block.sync
        def _(s):
            run_engine(s, "sync")

        stack.close()

    def collective(self, kind, op, ins_aps, outs_aps, reads, writes):
        nc = self.nc
        self.collective_ctr += 1
        name = f"csem{self.collective_ctr}"
        sem = nc.alloc_semaphore(name)
        core_ids = list(range(NCORES))

        def fn(g, kind=kind, op=op, ins_aps=ins_aps, outs_aps=outs_aps, sem=sem):
            inst = g.collective_compute(kind, op, replica_groups=[core_ids],
                                        ins=ins_aps, outs=outs_aps)
            inst.then_inc(sem, 1)
            return _FakeInst()

        cidx = len(self.recs)
        rec = self.emit("gpsimd", fn, reads=reads, writes=(), tag="coll")
        rec.inc = 0
        for k in rec.reads:
            lst = self.readers.get(k)
            if lst and cidx in lst:
                lst.remove(cidx)

        def wfn(g, sem=sem):
            g.wait_ge(sem, 1)
            return g.nop()

        self.emit("gpsimd", wfn, reads=reads, writes=writes, tag="collwait")


class _FakeInst:
    def then_inc(self, sem, v):
        return self


def f32c(x):
    return np.ascontiguousarray(x, dtype=np.float32)



def build_program(phases=("tin", "l0attn", "l0ffn", "l1attn", "l1ffn", "tout"),
                  stream_io=False, debug_out=(), inject=()):
    nc = bass.Bass("TRN2", target_bir_lowering=False)
    mk = MK(nc)
    ctx = {}
    ctx["nc"] = nc
    ctx["mk"] = mk

    def ein(name, shape):
        return nc.dram_tensor(name, shape, F32, kind="ExternalInput")

    X0 = ein("x0", [R, D])
    W = {}
    for name, shape in [
        ("L0_Wq", [D, D]), ("L0_Wk", [D, D]), ("L0_Wv", [D, D]),
        ("L0_Wg", [D, D]), ("L0_Wo", [D, D]),
        ("L0_sk_w1", [D, 16]), ("L0_sk_w2", [16, 256]), ("L0_sk_b2", [256]),
        ("L0_sv_w1", [D, 16]), ("L0_sv_w2", [16, 256]), ("L0_sv_b2", [256]),
        ("L0_gnorm_w", [256]),
        ("L0_w_gate", [DFF, D]), ("L0_w_up", [DFF, D]), ("L0_w_down", [D, DFF]),
        ("L1_Wz", [D, D]), ("L1_bz", [D]), ("L1_ln_g", [D]), ("L1_ln_b", [D]),
        ("L1_Wq", [D, D]), ("L1_bq", [D]), ("L1_Wr", [D, D]), ("L1_br", [D]),
        ("L1_Wh", [D, D]), ("L1_bh", [D]),
        ("L1_Wphi", [DH1, DH1]), ("L1_bphi", [DH1]),
        ("L1_Wo", [D, D]), ("L1_bo", [D]),
        ("L1_gate_w", [NEXP, D]), ("L1_w1", [NEXP, DEXP, D]), ("L1_w2", [NEXP, D, DEXP]),
    ]:
        W[name] = ein(name, shape)
    ctx["W"] = W
    ctx["X0"] = X0

    def internal(name, shape, dt=F32):
        return nc.dram_tensor(name, shape, dt)

    ctx["XF0"] = internal("XF0", [D, R])
    ctx["XB0"] = internal("XB0", [D, R], BF16)
    ctx["XF1"] = internal("XF1", [D, R])
    ctx["XB1"] = internal("XB1", [D, R], BF16)
    ctx["XF2"] = internal("XF2", [D, R])
    ctx["XB2"] = internal("XB2", [D, R], BF16)
    ctx["XF3"] = internal("XF3", [D, R])
    ctx["XB3"] = internal("XB3", [D, R], BF16)
    ctx["XF4"] = internal("XF4", [D, R])

    YOUT = nc.dram_tensor("y", [R, D], F32, kind="ExternalOutput")
    ctx["YOUT"] = YOUT

    ctx["eyef32"] = nc.inline_tensor(np.eye(128, dtype=np.float32), name="eyef32")
    ctx["eyebf16"] = nc.inline_tensor(
        np.eye(128, dtype=np.float32).astype(ml_dtypes.bfloat16), name="eyebf16")

    from contextlib import ExitStack
    with ExitStack() as stack:
        ctx["stack"] = stack
        e = stack.enter_context

        ctx["s_eye32"] = e(nc.sbuf_tensor("s_eye32", [128, 128], F32))
        ctx["s_eye16"] = e(nc.sbuf_tensor("s_eye16", [128, 128], BF16))
        ctx["s_ones"] = e(nc.sbuf_tensor("s_ones", [128, 8], BF16))
        ctx["s_ones32"] = e(nc.sbuf_tensor("s_ones32", [128, 8], F32))
        ctx["s_const"] = e(nc.sbuf_tensor("s_const", [128, 16], F32))

        mk.dma("sync", ctx["s_eye32"][:], ctx["eyef32"][:],
               reads=[ctx["eyef32"]], writes=[ctx["s_eye32"]], tag="ld_eye32")
        mk.dma("sync", ctx["s_eye16"][:], ctx["eyebf16"][:],
               reads=[ctx["eyebf16"]], writes=[ctx["s_eye16"]], tag="ld_eye16")
        mk.emit("vector", lambda v: v.memset(ctx["s_ones"][:], 1.0),
                writes=[ctx["s_ones"]], memset=True)
        mk.emit("vector", lambda v: v.memset(ctx["s_ones32"][:], 1.0),
                writes=[ctx["s_ones32"]], memset=True)

        const_vals = {}
        def constcol(val):
            if val in const_vals:
                return const_vals[val]
            i = len(const_vals)
            assert i < 16
            ap = ctx["s_const"][:, i:i + 1]
            mk.emit("vector", lambda v, ap=ap, val=val: v.memset(ap, float(val)),
                    writes=[ctx["s_const"]], memset=True)
            const_vals[val] = ap
            return ap
        ctx["constcol"] = constcol

        for name in inject:
            dst = ctx[name]
            shp = [int(s) for s in dst.ap().shape]
            dt = dst.ap().dtype
            src = nc.dram_tensor("in_" + name, shp, dt, kind="ExternalInput")
            mk.dma("sync", dst[:], src[:], reads=[src], writes=[dst],
                   tag="inject_" + name)

        main_stack = stack

        def run_phase(fn, *a):
            from contextlib import ExitStack as _ES
            with _ES() as ps:
                ctx["stack"] = ps
                fn(ctx, *a)
            ctx["stack"] = main_stack
            mk.barrier()

        if "tin" in phases:
            run_phase(phase_transpose_in)
        if "l0attn" in phases:
            run_phase(phase_l0_attn)
        if "l0ffn" in phases:
            run_phase(phase_l0_ffn)
        if "l1attn" in phases:
            run_phase(phase_l1_attn)
        if "l1ffn" in phases:
            run_phase(phase_l1_ffn)
        if "tout" in phases:
            if "l1ffn" in phases:
                tsrc = "XF4"
            elif "l1attn" in phases:
                tsrc = "XF3"
            else:
                tsrc = "XF0"
            run_phase(phase_transpose_out, tsrc)

        for name in debug_out:
            src = ctx[name]
            shp = [int(s) for s in src.ap().shape]
            dt = src.ap().dtype
            od = nc.dram_tensor("dbg_" + name, shp, dt, kind="ExternalOutput")
            mk.dma("sync", od[:], src[:], reads=[src], writes=[od], tag="dbg")

        mk.build_block()
    return nc



def phase_transpose_in(ctx):
    nc, mk = ctx["nc"], ctx["mk"]
    X0, XF0, XB0 = ctx["X0"], ctx["XF0"], ctx["XB0"]
    e = ctx["stack"].enter_context
    s_in = [e(nc.sbuf_tensor(f"ti_in{i}", [128, D], F32)) for i in range(2)]
    s_fm = [e(nc.sbuf_tensor(f"ti_fm{i}", [128, 8 * 128], F32)) for i in range(2)]
    s_fmb = [e(nc.sbuf_tensor(f"ti_fmb{i}", [128, 8 * 128], BF16)) for i in range(2)]
    p_tr = [e(nc.psum_tensor(f"ti_ps{i}", [128, 128], F32)) for i in range(2)]
    eye = ctx["s_eye32"]

    ntile = R // 128
    for it in range(ntile):
        bi = it % 2
        sin, sfm, sfmb = s_in[bi], s_fm[bi], s_fmb[bi]
        mk.dma("sync", sin[:], X0[it * 128:(it + 1) * 128, :],
               reads=[X0], writes=[sin], tag=f"tin_ld{it}")
        for c in range(8):
            ps = p_tr[c % 2]
            mk.emit("tensor",
                    lambda t, sin=sin, ps=ps, c=c: t.transpose(
                        ps[:], sin[:, c * 128:(c + 1) * 128], eye[:]),
                    reads=[sin, eye], writes=[ps], tag=f"tin_tr{it}_{c}")
            mk.emit("vector",
                    lambda v, ps=ps, sfm=sfm, c=c: v.tensor_copy(
                        sfm[:, c * 128:(c + 1) * 128], ps[:]),
                    reads=[ps], writes=[sfm], tag=f"tin_cp{it}_{c}")
        mk.emit("scalar", lambda s, sfm=sfm, sfmb=sfmb: s.copy(sfmb[:], sfm[:]),
                reads=[sfm], writes=[sfmb], tag=f"tin_cast{it}")
        dst32 = XF0[:].rearrange("(c p) t -> p c t", p=128)[:, :, it * 128:(it + 1) * 128]
        dst16 = XB0[:].rearrange("(c p) t -> p c t", p=128)[:, :, it * 128:(it + 1) * 128]
        src = s_fm[bi][:].rearrange("p (c t) -> p c t", c=8)
        srcb = s_fmb[bi][:].rearrange("p (c t) -> p c t", c=8)
        mk.dma("sync", dst32, src, reads=[sfm], writes=[XF0], tag=f"tin_st32_{it}")
        mk.dma("scalar", dst16, srcb, reads=[sfmb], writes=[XB0], tag=f"tin_st16_{it}")



def phase_transpose_out(ctx, src="XF3"):
    nc, mk = ctx["nc"], ctx["mk"]
    XF, YOUT = ctx[src], ctx["YOUT"]
    e = ctx["stack"].enter_context
    s_fm = [e(nc.sbuf_tensor(f"to_fm{i}", [128, 8 * 128], F32)) for i in range(2)]
    s_tm = [e(nc.sbuf_tensor(f"to_tm{i}", [128, D], F32)) for i in range(2)]
    p_tr = [e(nc.psum_tensor(f"to_ps{i}", [128, 128], F32)) for i in range(2)]
    eye = ctx["s_eye32"]

    ntile = R // 128
    for it in range(ntile):
        bi = it % 2
        sfm, stm = s_fm[bi], s_tm[bi]
        srcv = XF[:].rearrange("(c p) t -> p c t", p=128)[:, :, it * 128:(it + 1) * 128]
        mk.dma("sync", sfm[:].rearrange("p (c t) -> p c t", c=8), srcv,
               reads=[XF], writes=[sfm], tag=f"to_ld{it}")
        for c in range(8):
            ps = p_tr[c % 2]
            mk.emit("tensor",
                    lambda t, sfm=sfm, ps=ps, c=c: t.transpose(
                        ps[:], sfm[:, c * 128:(c + 1) * 128], eye[:]),
                    reads=[sfm, eye], writes=[ps], tag=f"to_tr{it}_{c}")
            mk.emit("vector",
                    lambda v, ps=ps, stm=stm, c=c: v.tensor_copy(
                        stm[:, c * 128:(c + 1) * 128], ps[:]),
                    reads=[ps], writes=[stm], tag=f"to_cp{it}_{c}")
        mk.dma("sync", YOUT[it * 128:(it + 1) * 128, :], stm[:],
               reads=[stm], writes=[YOUT], tag=f"to_st{it}")



def phase_l0_attn(ctx):
    nc, mk = ctx["nc"], ctx["mk"]
    W = ctx["W"]
    XB0, XF0, XF1, XB1 = ctx["XB0"], ctx["XF0"], ctx["XF1"], ctx["XB1"]
    e = ctx["stack"].enter_context
    eye32, eye16 = ctx["s_eye32"], ctx["s_eye16"]
    constcol = ctx["constcol"]

    inv_freq = (1.0 / (np.float32(10000.0) ** (np.arange(128, dtype=np.float32)
                                               / np.float32(128.0)))).astype(np.float32)
    ang = (np.arange(T, dtype=np.float32)[None, :] * inv_freq[:, None]).astype(np.float32)
    d_cos = nc.inline_tensor(np.cos(ang).astype(np.float32), name="rope_cos")
    d_sin = nc.inline_tensor(np.sin(ang).astype(np.float32), name="rope_sin")
    ut = np.triu(np.ones((128, 128), dtype=np.float32))
    d_ut32 = nc.inline_tensor(ut, name="ut32")
    d_ut16 = nc.inline_tensor(ut.astype(ml_dtypes.bfloat16), name="ut16")

    s_cos = e(nc.sbuf_tensor("a_cos", [128, T], F32))
    s_sin = e(nc.sbuf_tensor("a_sin", [128, T], F32))
    s_ut32 = e(nc.sbuf_tensor("a_ut32", [128, 128], F32))
    s_ut16 = e(nc.sbuf_tensor("a_ut16", [128, 128], BF16))
    s_row1 = e(nc.sbuf_tensor("a_row1", [1, 512], BF16))
    s_row1f = e(nc.sbuf_tensor("a_row1f", [1, 512], F32))
    s_col1 = e(nc.sbuf_tensor("a_col1", [128, 1], BF16))
    s_col1f = e(nc.sbuf_tensor("a_col1f", [128, 1], F32))
    mk.dma("sync", s_cos[:], d_cos[:], reads=[d_cos], writes=[s_cos])
    mk.dma("sync", s_sin[:], d_sin[:], reads=[d_sin], writes=[s_sin])
    mk.dma("sync", s_ut32[:], d_ut32[:], reads=[d_ut32], writes=[s_ut32])
    mk.dma("sync", s_ut16[:], d_ut16[:], reads=[d_ut16], writes=[s_ut16])
    for (buf, val) in ((s_row1, 1.0), (s_row1f, 1.0), (s_col1, 1.0), (s_col1f, 1.0)):
        mk.emit("vector", lambda v, buf=buf, val=val: v.memset(buf[:], val),
                writes=[buf], memset=True)

    p0 = e(nc.psum_tensor("a_p0", [128, 512], F32))
    p1 = e(nc.psum_tensor("a_p1", [128, 512], F32))
    p2 = e(nc.psum_tensor("a_p2", [128, 512], F32))
    p3 = e(nc.psum_tensor("a_p3", [1, 512], F32))
    p4 = e(nc.psum_tensor("a_p4", [128, 256], F32))
    p5 = e(nc.psum_tensor("a_p5", [128, 512], F32))
    p6 = e(nc.psum_tensor("a_p6", [128, 512], F32))
    p7 = e(nc.psum_tensor("a_p7", [128, 512], F32))

    ps_small = p4

    s_skw1 = e(nc.sbuf_tensor("a_skw1", [128, 8, 16], BF16))
    s_svw1 = e(nc.sbuf_tensor("a_svw1", [128, 8, 16], BF16))
    s_skw2 = e(nc.sbuf_tensor("a_skw2", [16, 256], BF16))
    s_svw2 = e(nc.sbuf_tensor("a_svw2", [16, 256], BF16))
    s_skb2 = e(nc.sbuf_tensor("a_skb2", [1, 256], F32))
    s_svb2 = e(nc.sbuf_tensor("a_svb2", [1, 256], F32))
    s_v2x128 = e(nc.sbuf_tensor("a_v2x128", [2, 128], F32))
    s_gnc = e(nc.sbuf_tensor("a_gnc", [128, 2], F32))

    mk.dma("gpsimd", s_skw1[:], W["L0_sk_w1"][:].rearrange("(c p) l -> p c l", p=128),
           reads=[W["L0_sk_w1"]], writes=[s_skw1])
    mk.dma("gpsimd", s_svw1[:], W["L0_sv_w1"][:].rearrange("(c p) l -> p c l", p=128),
           reads=[W["L0_sv_w1"]], writes=[s_svw1])
    mk.dma("gpsimd", s_skw2[:], W["L0_sk_w2"][:], reads=[W["L0_sk_w2"]], writes=[s_skw2])
    mk.dma("gpsimd", s_svw2[:], W["L0_sv_w2"][:], reads=[W["L0_sv_w2"]], writes=[s_svw2])
    mk.dma("sync", s_skb2[:], W["L0_sk_b2"][:].rearrange("(a b) -> a b", a=1),
           reads=[W["L0_sk_b2"]], writes=[s_skb2])
    mk.dma("sync", s_svb2[:], W["L0_sv_b2"][:].rearrange("(a b) -> a b", a=1),
           reads=[W["L0_sv_b2"]], writes=[s_svb2])

    mk.dma("sync", s_v2x128[:], W["L0_gnorm_w"][:].rearrange("(a b) -> a b", a=2),
           reads=[W["L0_gnorm_w"]], writes=[s_v2x128])
    mk.emit("tensor", lambda t: t.transpose(ps_small[0:128, 0:2], s_v2x128[:], eye32[0:2, 0:2]),
            reads=[s_v2x128, eye32], writes=[ps_small])
    mk.emit("vector", lambda v: v.tensor_copy(s_gnc[:], ps_small[0:128, 0:2]),
            reads=[ps_small], writes=[s_gnc])

    s_wq = e(nc.sbuf_tensor("a_wq", [128, 8, 256], BF16))
    s_wk = e(nc.sbuf_tensor("a_wk", [128, 8, 256], BF16))
    s_wv = e(nc.sbuf_tensor("a_wv", [128, 8, 256], BF16))
    s_wg = e(nc.sbuf_tensor("a_wg", [128, 8, 256], BF16))
    s_wo = e(nc.sbuf_tensor("a_wo", [128, 8, 1024], BF16))
    mk.dma("gpsimd", s_wo[:], W["L0_Wo"][:].rearrange("(c p) n -> p c n", p=128),
           reads=[W["L0_Wo"]], writes=[s_wo])

    s_xb = [e(nc.sbuf_tensor(f"a_xb{i}", [128, 8, 512], BF16)) for i in range(2)]
    s_qu = e(nc.sbuf_tensor("a_qu", [16, T], BF16))
    s_vu = e(nc.sbuf_tensor("a_vu", [16, T], BF16))
    s_wkt = e(nc.sbuf_tensor("a_wkt", [128, 16, 256], BF16))
    s_wvt = e(nc.sbuf_tensor("a_wvt", [128, 16, 256], BF16))
    s_wvf = e(nc.sbuf_tensor("a_wvf", [64, 4, T], BF16))
    s_q = e(nc.sbuf_tensor("a_q", [128, 2, T], BF16))
    s_k = e(nc.sbuf_tensor("a_k", [128, 2, T], BF16))
    s_v = e(nc.sbuf_tensor("a_v", [128, 16, 256], BF16))
    s_g = e(nc.sbuf_tensor("a_g", [128, 2, T], BF16))
    s_o = e(nc.sbuf_tensor("a_o", [128, 2, T], BF16))
    s_ogst = e(nc.sbuf_tensor("a_ogst", [128, 512], BF16))
    s_t1 = e(nc.sbuf_tensor("a_t1", [128, 512], F32))
    s_t2 = e(nc.sbuf_tensor("a_t2", [128, 512], F32))
    s_t3 = e(nc.sbuf_tensor("a_t3", [128, 512], F32))
    s_t4 = e(nc.sbuf_tensor("a_t4", [128, 512], F32))

    s_S = e(nc.sbuf_tensor("a_S", [128, 128], BF16))
    s_rzk = e(nc.sbuf_tensor("a_rzk", [128, 64], F32))
    s_lnt = e(nc.sbuf_tensor("a_lnt", [128, 64], F32))
    s_rzv = e(nc.sbuf_tensor("a_rzv", [128, 64], F32))
    s_log = e(nc.sbuf_tensor("a_log", [128, 64], F32))
    s_nmax = e(nc.sbuf_tensor("a_nmax", [128, 1], F32))
    s_exp = e(nc.sbuf_tensor("a_exp", [128, 64], F32))
    s_sum = e(nc.sbuf_tensor("a_sum", [128, 1], F32))
    s_rrec = e(nc.sbuf_tensor("a_rrec", [128, 1], F32))
    s_pz0 = e(nc.sbuf_tensor("a_pz0", [128, 64], F32))
    s_pz = e(nc.sbuf_tensor("a_pz", [128, 64], F32))
    s_pzT = e(nc.sbuf_tensor("a_pzT", [64, 128], BF16))
    s_ktm = e(nc.sbuf_tensor("a_ktm", [128, 256], BF16))
    s_A = e(nc.sbuf_tensor("a_A", [128, 128], BF16))
    s_Kst = e(nc.sbuf_tensor("a_Kst", [128, 2, 64], F32))
    s_Kstb = [e(nc.sbuf_tensor(f"a_Kstb{i}", [128, 2, 64], BF16)) for i in range(2)]
    s_Vst = e(nc.sbuf_tensor("a_Vst", [64, 256], F32))
    s_Vstb = [e(nc.sbuf_tensor(f"a_Vstb{i}", [64, 256], BF16)) for i in range(2)]
    s_cyf = e(nc.sbuf_tensor("a_cyf", [1, 128], F32))
    s_cyb = [e(nc.sbuf_tensor(f"a_cyb{i}", [1, 128], BF16)) for i in range(2)]

    s_rms = e(nc.sbuf_tensor("a_rms", [1, 512], F32))
    s_mean = e(nc.sbuf_tensor("a_mean", [1, 512], F32))
    s_mr = e(nc.sbuf_tensor("a_mr", [1, 512], F32))
    s_rstd = e(nc.sbuf_tensor("a_rstd", [1, 512], F32))
    s_xr = e(nc.sbuf_tensor("a_xr", [128, 512], F32))
    s_x1b = e(nc.sbuf_tensor("a_x1b", [128, 512], BF16))
    s_x1b2 = e(nc.sbuf_tensor("a_x1b2", [128, 512], BF16))
    s_res8 = e(nc.sbuf_tensor("a_res8", [128, 8, 512], F32))
    s_amax = e(nc.sbuf_tensor("a_amax", [128, 1], F32))
    s_amax1 = e(nc.sbuf_tensor("a_amax1", [1, 1], F32))
    AMAXL = nc.dram_tensor("AMAXL", [1, 1], F32)
    XR = nc.dram_tensor("XR", [D, R], F32)
    OG = nc.dram_tensor("OG", [D, R], BF16)
    ctx["OG"] = OG
    ctx["XR"] = XR
    ctx["AMAXL"] = AMAXL

    mk.emit("vector", lambda v: v.memset(s_amax[:], 0.0), writes=[s_amax], memset=True)

    scale0 = float(DK0) ** -0.5
    xb_view = XB0[:].rearrange("(c p) t -> p c t", p=128)
    xf_view = XF0[:].rearrange("(c p) t -> p c t", p=128)
    xr_view = XR[:].rearrange("(c p) t -> p c t", p=128)
    og_view = OG[:].rearrange("(c p) t -> p c t", p=128)
    xf1_view = XF1[:].rearrange("(c p) t -> p c t", p=128)
    xb1_view = XB1[:].rearrange("(c p) t -> p c t", p=128)

    def load_head_weights(h):
        for (wname, dst) in (("L0_Wq", s_wq), ("L0_Wk", s_wk), ("L0_Wv", s_wv),
                             ("L0_Wg", s_wg)):
            src = W[wname][:].rearrange("(c p) n -> p c n", p=128)[:, :, h * 256:(h + 1) * 256]
            mk.dma("gpsimd", dst[:], src, reads=[W[wname]], writes=[dst],
                   tag=f"ldw{wname}{h}")

    for seq in range(SEQ_PER_CORE):
        t0seq = seq * T

        for tt in range(4):
            xb = s_xb[tt % 2]
            sl = slice(t0seq + tt * 512, t0seq + (tt + 1) * 512)
            mk.dma("sync", xb[:], xb_view[:, :, sl], reads=[XB0], writes=[xb])
            for (w1, u) in ((s_skw1, s_qu), (s_svw1, s_vu)):
                for kc in range(8):
                    mk.emit("tensor", lambda t, w1=w1, xb=xb, kc=kc: t.matmul(
                        p0[0:16, 0:512], w1[:, kc, :], xb[:, kc, :],
                        start=(kc == 0), stop=(kc == 7)),
                        reads=[w1, xb], writes=[p0], tag="sk_u")
                mk.emit("vector", lambda v, u=u, tt=tt: v.tensor_copy(
                    u[:, tt * 512:(tt + 1) * 512], p0[0:16, 0:512]),
                    reads=[p0], writes=[u], tag="sk_uev")
        for tile in range(16):
            tsl = slice(tile * 128, (tile + 1) * 128)
            for (u, binj, dst) in ((s_qu, s_skb2, s_wkt), (s_vu, s_svb2, s_wvt)):
                w2 = s_skw2 if dst is s_wkt else s_svw2
                mk.emit("tensor", lambda t, u=u, w2=w2, tsl=tsl: t.matmul(
                    p1[:, 0:256], u[:, tsl], w2[:], start=True, stop=False),
                    reads=[u, w2], writes=[p1], tag="wk_mm")
                mk.emit("tensor", lambda t, binj=binj: t.matmul(
                    p1[:, 0:256], s_row1f[0:1, 0:128], binj[:],
                    start=False, stop=True, skip_group_check=True),
                    reads=[s_row1f, binj], writes=[p1], tag="wk_b")
                mk.emit("vector", lambda v: v.tensor_scalar(
                    s_t1[:, 0:256], p1[:, 0:256], 32.0, -32.0, ALU.min, ALU.max),
                    reads=[p1], writes=[s_t1], tag="wk_clip")
                mk.emit("scalar", lambda s, dst=dst, tile=tile: s.activation(
                    dst[:, tile, :], s_t1[:, 0:256], AF.Exp),
                    reads=[s_t1], writes=[dst], tag="wk_exp")
        for hh in range(4):
            for tt in range(4):
                tsl = slice(tt * 512, (tt + 1) * 512)
                mk.emit("tensor", lambda t, hh=hh, tsl=tsl: t.matmul(
                    p1[0:64, 0:512], s_svw2[:, hh * 64:(hh + 1) * 64], s_vu[:, tsl],
                    start=True, stop=False),
                    reads=[s_svw2, s_vu], writes=[p1], tag="wvf_mm")
                mk.emit("tensor", lambda t, hh=hh: t.matmul(
                    p1[0:64, 0:512], s_svb2[0:1, hh * 64:(hh + 1) * 64], s_row1f[0:1, :],
                    start=False, stop=True, skip_group_check=True),
                    reads=[s_svb2, s_row1f], writes=[p1], tag="wvf_b")
                mk.emit("vector", lambda v: v.tensor_scalar(
                    s_t1[0:64, 0:512], p1[0:64, 0:512], 32.0, -32.0, ALU.min, ALU.max),
                    reads=[p1], writes=[s_t1], tag="wvf_clip")
                mk.emit("scalar", lambda s, hh=hh, tsl=tsl: s.activation(
                    s_wvf[:, hh, tsl], s_t1[0:64, 0:512], AF.Exp),
                    reads=[s_t1], writes=[s_wvf], tag="wvf_exp")

        for h in range(H0):
            load_head_weights(h)
            msl = slice(h * 64, (h + 1) * 64)
            wvf_l = s_wvf[0:64, h, :]

            for tt in range(4):
                xb = s_xb[tt % 2]
                sl = slice(t0seq + tt * 512, t0seq + (tt + 1) * 512)
                lsl = slice(tt * 512, (tt + 1) * 512)
                mk.dma("sync", xb[:], xb_view[:, :, sl], reads=[XB0], writes=[xb])
                for (wsl, dst) in ((s_wq, s_q), (s_wk, s_k)):
                    for mc in range(2):
                        pp = (p0, p1)[mc]
                        for kc in range(8):
                            mk.emit("tensor", lambda t, wsl=wsl, xb=xb, kc=kc, mc=mc, pp=pp:
                                    t.matmul(pp[:, 0:512], wsl[:, kc, mc * 128:(mc + 1) * 128],
                                             xb[:, kc, :], start=(kc == 0), stop=(kc == 7)),
                                    reads=[wsl, xb], writes=[pp], tag="qk_mm")
                    csl = s_cos[:, lsl]
                    ssl = s_sin[:, lsl]
                    mk.emit("vector", lambda v, csl=csl: v.tensor_tensor(
                        s_t1[:], p0[:, 0:512], csl, ALU.mult),
                        reads=[p0, s_cos], writes=[s_t1], tag="rope1")
                    mk.emit("vector", lambda v, ssl=ssl: v.tensor_tensor(
                        s_t2[:], p1[:, 0:512], ssl, ALU.mult),
                        reads=[p1, s_sin], writes=[s_t2], tag="rope2")
                    mk.emit("vector", lambda v, csl=csl: v.tensor_tensor(
                        s_t3[:], p1[:, 0:512], csl, ALU.mult),
                        reads=[p1, s_cos], writes=[s_t3], tag="rope3")
                    mk.emit("vector", lambda v, ssl=ssl: v.tensor_tensor(
                        s_t4[:], p0[:, 0:512], ssl, ALU.mult),
                        reads=[p0, s_sin], writes=[s_t4], tag="rope4")
                    mk.emit("gpsimd", lambda g, dst=dst, lsl=lsl: g.tensor_sub(
                        dst[:, 0, lsl], s_t1[:], s_t2[:]),
                        reads=[s_t1, s_t2], writes=[dst], tag="rope5")
                    mk.emit("gpsimd", lambda g, dst=dst, lsl=lsl: g.tensor_add(
                        dst[:, 1, lsl], s_t3[:], s_t4[:]),
                        reads=[s_t3, s_t4], writes=[dst], tag="rope6")
                for vt in range(4):
                    gt = tt * 4 + vt
                    for kc in range(8):
                        mk.emit("tensor", lambda t, xb=xb, kc=kc, vt=vt: t.matmul(
                            p7[:, 0:256], xb[:, kc, vt * 128:(vt + 1) * 128],
                            s_wv[:, kc, :], start=(kc == 0), stop=(kc == 7)),
                            reads=[xb, s_wv], writes=[p7], tag="v_mm")
                    mk.emit("vector", lambda v, gt=gt: v.tensor_copy(
                        s_v[:, gt, :], p7[:, 0:256]),
                        reads=[p7], writes=[s_v], tag="v_ev")

            for buf in (s_Kst, s_Kstb[0], s_Kstb[1], s_Vst, s_Vstb[0], s_Vstb[1],
                        s_cyf, s_cyb[0], s_cyb[1]):
                mk.emit("vector", lambda v, buf=buf: v.memset(buf[:], 0.0),
                        writes=[buf], memset=True)

            for c in range(16):
                tsl = slice(c * 128, (c + 1) * 128)
                gt0 = seq
                for mc in range(2):
                    mk.emit("tensor", lambda t, mc=mc, tsl=tsl: t.matmul(
                        p0[:, 0:128], s_k[:, mc, tsl], s_q[:, mc, tsl],
                        start=(mc == 0), stop=(mc == 1)),
                        reads=[s_k, s_q], writes=[p0], tag="scanS")
                mk.emit("vector", lambda v: v.scalar_tensor_tensor(
                    s_S[:], p0[:, 0:128], 1.0, s_ut32[:], ALU.mult, ALU.mult),
                    reads=[p0, s_ut32], writes=[s_S], tag="scanSev")
                for (wtm, cof, dst0) in ((s_wkt, 0, 64), (s_wvt, 64, 128)):
                    mk.emit("tensor", lambda t, wtm=wtm, dst0=dst0, c=c: t.matmul(
                        p1[:, dst0:dst0 + 64], s_ut16[:], wtm[:, c, msl],
                        start=True, stop=False),
                        reads=[s_ut16, wtm], writes=[p1], tag="scanZ")
                    cyp = s_cyb[(c + 1) % 2]
                    mk.emit("tensor", lambda t, cof=cof, dst0=dst0, cyp=cyp: t.matmul(
                        p1[:, dst0:dst0 + 64], s_row1[0:1, 0:128],
                        cyp[0:1, cof:cof + 64], start=False, stop=True,
                        skip_group_check=True),
                        reads=[s_row1, cyp], writes=[p1], tag="scanZc")
                _m1 = constcol(-1.0)[0:128, :]
                mk.emit("scalar", lambda s: s.activation(
                    s_lnt[:], p1[:, 64:128], AF.Ln),
                    reads=[p1], writes=[s_lnt], tag="lzk")
                mk.emit("scalar", lambda s, _m1=_m1: s.activation(
                    s_rzk[:], s_lnt[:], AF.Exp, scale=_m1),
                    reads=[s_lnt, ctx["s_const"]], writes=[s_rzk], tag="rzk")
                mk.emit("scalar", lambda s: s.activation(
                    s_lnt[:], p1[:, 128:192], AF.Ln),
                    reads=[p1], writes=[s_lnt], tag="lzv")
                mk.emit("scalar", lambda s, _m1=_m1: s.activation(
                    s_rzv[:], s_lnt[:], AF.Exp, scale=_m1),
                    reads=[s_lnt, ctx["s_const"]], writes=[s_rzv], tag="rzv")
                mk.emit("tensor", lambda t, c=c: t.matmul(
                    p3[0:1, 0:64], s_col1[:], s_wkt[:, c, msl],
                    start=(c == 0), stop=False, skip_group_check=True),
                    reads=[s_col1, s_wkt], writes=[p3], tag="cyk")
                mk.emit("tensor", lambda t, c=c: t.matmul(
                    p3[0:1, 64:128], s_col1[:], s_wvt[:, c, msl],
                    start=False, stop=False, skip_group_check=True),
                    reads=[s_col1, s_wvt], writes=[p3], tag="cyv")
                cyc = s_cyb[c % 2]
                mk.emit("vector", lambda v: v.tensor_copy(s_cyf[:], p3[0:1, 0:128]),
                        reads=[p3], writes=[s_cyf], tag="cysnapf")
                mk.emit("vector", lambda v, cyc=cyc: v.tensor_copy(cyc[:], s_cyf[:]),
                        reads=[s_cyf], writes=[cyc], tag="cysnapb")
                p4b = p4[:].bitcast(BF16)
                for mc in range(2):
                    mk.emit("tensor", lambda t, mc=mc, tsl=tsl, p4b=p4b: t.transpose(
                        p4b[:, 256 + mc * 128:256 + (mc + 1) * 128],
                        s_k[:, mc, tsl], eye16[:]),
                        reads=[s_k, eye16], writes=[p4], tag="ktmT")
                mk.emit("vector", lambda v, p4b=p4b: v.tensor_copy(
                    s_ktm[:], p4b[:, 256:512]),
                    reads=[p4], writes=[s_ktm], tag="ktmEv")
                for dt in range(2):
                    mk.emit("tensor", lambda t, dt=dt, c=c: t.matmul(
                        p5[:, dt * 64:(dt + 1) * 64],
                        s_ktm[:, dt * 128:(dt + 1) * 128], s_wkt[:, c, msl],
                        start=True, stop=True),
                        reads=[s_ktm, s_wkt], writes=[p5], tag="kstU")
                mk.emit("tensor", lambda t, c=c: t.matmul(
                    p5[0:64, 128:384], s_wvt[:, c, msl], s_v[:, c, :],
                    start=True, stop=True),
                    reads=[s_wvt, s_v], writes=[p5], tag="vstU")
                mk.emit("vector", lambda v: v.tensor_add(
                    s_Kst[:].rearrange("p d m -> p (d m)"),
                    s_Kst[:].rearrange("p d m -> p (d m)"), p5[:, 0:128]),
                    reads=[p5, s_Kst], writes=[s_Kst], tag="kstA")
                mk.emit("vector", lambda v: v.tensor_add(
                    s_Vst[:], s_Vst[:], p5[0:64, 128:384]),
                    reads=[p5, s_Vst], writes=[s_Vst], tag="vstA")
                kbc = s_Kstb[c % 2]
                vbc = s_Vstb[c % 2]
                mk.emit("vector", lambda v, kbc=kbc: v.tensor_copy(
                    kbc[:].rearrange("p d m -> p (d m)"),
                    s_Kst[:].rearrange("p d m -> p (d m)")),
                    reads=[s_Kst], writes=[kbc], tag="kstC")
                mk.emit("vector", lambda v, vbc=vbc: v.tensor_copy(vbc[:], s_Vst[:]),
                        reads=[s_Vst], writes=[vbc], tag="vstC")

                mk.emit("tensor", lambda t, c=c: t.matmul(
                    p1[:, 0:64], s_S[:], s_wkt[:, c, msl], start=True, stop=False),
                    reads=[s_S, s_wkt], writes=[p1], tag="logI")
                kbp = s_Kstb[(c + 1) % 2]
                for mc in range(2):
                    mk.emit("tensor", lambda t, mc=mc, tsl=tsl, kbp=kbp: t.matmul(
                        p1[:, 0:64], s_q[:, mc, tsl], kbp[:, mc, :],
                        start=False, stop=(mc == 1), skip_group_check=True),
                        reads=[s_q, kbp], writes=[p1], tag="logX")
                mk.emit("vector", lambda v: v.scalar_tensor_tensor(
                    s_log[:], p1[:, 0:64], scale0, s_rzk[:], ALU.mult, ALU.mult),
                    reads=[p1, s_rzk], writes=[s_log], tag="logEv")
                mk.emit("vector", lambda v: v.reduce_max(
                    s_nmax[:], s_log[:], AX.X, negate=True),
                    reads=[s_log], writes=[s_nmax], tag="smax")
                mk.emit("scalar", lambda s: s.activation(
                    s_exp[:], s_log[:], AF.Exp, bias=s_nmax[:]),
                    reads=[s_log, s_nmax], writes=[s_exp], tag="sexp")
                mk.emit("vector", lambda v: v.reduce_sum(s_sum[:], s_exp[:], AX.X),
                        reads=[s_exp], writes=[s_sum], tag="ssum")
                mk.emit("vector", lambda v: v.reciprocal(s_rrec[:], s_sum[:]),
                        reads=[s_sum], writes=[s_rrec], tag="srec")
                mk.emit("vector", lambda v: v.tensor_scalar(
                    s_pz0[:], s_exp[:], s_rrec[:], None, ALU.mult),
                    reads=[s_exp, s_rrec], writes=[s_pz0], tag="spz0")
                mk.emit("vector", lambda v: v.tensor_tensor(
                    s_pz[:], s_pz0[:], s_rzv[:], ALU.mult),
                    reads=[s_pz0, s_rzv], writes=[s_pz], tag="spz")
                mk.emit("tensor", lambda t: t.transpose(
                    p4[0:64, 0:128], s_pz[:], eye32[:]),
                    reads=[s_pz, eye32], writes=[p4], tag="pzT")
                mk.emit("vector", lambda v: v.tensor_copy(s_pzT[:], p4[0:64, 0:128]),
                        reads=[p4], writes=[s_pzT], tag="pzTev")
                mk.emit("tensor", lambda t, wvf_l=wvf_l, tsl=tsl: t.matmul(
                    p2[:, 0:128], wvf_l[:, tsl], s_pzT[:], start=True, stop=True),
                    reads=[s_wvf, s_pzT], writes=[p2], tag="scanA")
                mk.emit("vector", lambda v: v.scalar_tensor_tensor(
                    s_A[:], p2[:, 0:128], 1.0, s_ut32[:], ALU.mult, ALU.mult),
                    reads=[p2, s_ut32], writes=[s_A], tag="scanAev")
                for dt in range(2):
                    mk.emit("tensor", lambda t, dt=dt, c=c: t.matmul(
                        p2[:, 128 + dt * 128:128 + (dt + 1) * 128],
                        s_v[:, c, dt * 128:(dt + 1) * 128], s_A[:],
                        start=True, stop=False),
                        reads=[s_v, s_A], writes=[p2], tag="oIntra")
                    vbp = s_Vstb[(c + 1) % 2]
                    mk.emit("tensor", lambda t, dt=dt, vbp=vbp: t.matmul(
                        p2[:, 128 + dt * 128:128 + (dt + 1) * 128],
                        vbp[:, dt * 128:(dt + 1) * 128], s_pzT[:],
                        start=False, stop=True, skip_group_check=True),
                        reads=[vbp, s_pzT], writes=[p2], tag="oInter")
                mk.emit("vector", lambda v, tsl=tsl: v.tensor_copy(
                    s_o[:, :, tsl], p2[:, 128:384].rearrange("p (d t) -> p d t", d=2)),
                    reads=[p2], writes=[s_o], tag="oEv")
                if ctx.get("dbg_chunk") == (seq, h, c):
                    for nm, buf in (("DCS", s_S), ("DCLOG", s_log), ("DCNM", s_nmax),
                                    ("DCEXP", s_exp), ("DCSUM", s_sum), ("DCRREC", s_rrec),
                                    ("DCPZ", s_pz), ("DCPZT", s_pzT), ("DCA", s_A),
                                    ("DCRZK", s_rzk), ("DCRZV", s_rzv), ("DCKTM", s_ktm)):
                        shp = [int(sz) for sz in buf[:].shape]
                        dd = nc.dram_tensor(nm, shp, buf[:].dtype)
                        ctx[nm] = dd
                        mk.dma("sync", dd[:], buf[:], reads=[buf], writes=[dd], tag="dcdump")
            for tt in range(4):
                xb = s_xb[tt % 2]
                sl = slice(t0seq + tt * 512, t0seq + (tt + 1) * 512)
                lsl = slice(tt * 512, (tt + 1) * 512)
                mk.dma("sync", xb[:], xb_view[:, :, sl], reads=[XB0], writes=[xb])
                for mc in range(2):
                    for kc in range(8):
                        mk.emit("tensor", lambda t, xb=xb, kc=kc, mc=mc: t.matmul(
                            p0[:, 0:512], s_wg[:, kc, mc * 128:(mc + 1) * 128],
                            xb[:, kc, :], start=(kc == 0), stop=(kc == 7)),
                            reads=[s_wg, xb], writes=[p0], tag="g_mm")
                    mk.emit("vector", lambda v, mc=mc, lsl=lsl: v.tensor_copy(
                        s_g[:, mc, lsl], p0[:, 0:512]),
                        reads=[p0], writes=[s_g], tag="g_ev")

            for tt in range(4):
                lsl = slice(tt * 512, (tt + 1) * 512)
                for mc in range(2):
                    mk.emit("scalar", lambda s, mc=mc, lsl=lsl: s.activation(
                        s_t1[:], s_o[:, mc, lsl], AF.Square),
                        reads=[s_o], writes=[s_t1], tag="rmsSq")
                    mk.emit("tensor", lambda t, mc=mc: t.matmul(
                        p6[0:1, 0:512], s_col1f[:], s_t1[:],
                        start=(mc == 0), stop=(mc == 1)),
                        reads=[s_col1f, s_t1], writes=[p6], tag="rmsMM")
                _b = constcol(1e-5)[0:1, :]
                _s = constcol(1.0 / 256.0)[0:1, :]
                mk.emit("scalar", lambda s, _b=_b, _s=_s: s.activation(
                    s_rstd[0:1, :], p6[0:1, 0:512], AF.Sqrt, bias=_b, scale=_s),
                    reads=[p6, ctx["s_const"]], writes=[s_rstd], tag="rmsSB")
                mk.emit("vector", lambda v: v.reciprocal(
                    s_rms[0:1, :], s_rstd[0:1, :]),
                    reads=[s_rstd], writes=[s_rms], tag="rmsRec")
                mk.emit("tensor", lambda t: t.matmul(
                    p6[:, 0:512], s_row1f[0:1, 0:128], s_rms[0:1, :],
                    start=True, stop=True),
                    reads=[s_row1f, s_rms], writes=[p6], tag="rmsBc")
                for mc in range(2):
                    mk.emit("scalar", lambda s, mc=mc, lsl=lsl: s.activation(
                        s_t1[:], s_g[:, mc, lsl], AF.Silu),
                        reads=[s_g], writes=[s_t1], tag="gSilu")
                    mk.emit("gpsimd", lambda v, mc=mc: v.tensor_scalar(
                        s_t2[:], s_t1[:], s_gnc[:, mc:mc + 1], None, ALU.mult),
                        reads=[s_t1, s_gnc], writes=[s_t2], tag="gGn")
                    mk.emit("vector", lambda v, mc=mc, lsl=lsl: v.tensor_tensor(
                        s_t3[:], s_o[:, mc, lsl], p6[:, 0:512], ALU.mult),
                        reads=[s_o, p6], writes=[s_t3], tag="gOn")
                    mk.emit("gpsimd", lambda v: v.tensor_tensor(
                        s_ogst[:], s_t3[:], s_t2[:], ALU.mult),
                        reads=[s_t3, s_t2], writes=[s_ogst], tag="gOg")
                    gsl = slice(t0seq + tt * 512, t0seq + (tt + 1) * 512)
                    mk.dma("sync", og_view[:, h * 2 + mc, gsl], s_ogst[:],
                           reads=[s_ogst], writes=[OG], tag="gOgSt")

            if ctx.get("dbg_head") == (seq, h):
                for nm, buf in (("DBGQ", s_q), ("DBGK", s_k), ("DBGV", s_v),
                                ("DBGO", s_o), ("DBGG", s_g), ("DBGWKT", s_wkt),
                                ("DBGWVT", s_wvt), ("DBGWVF", s_wvf),
                                ("DBGKST", s_Kst), ("DBGVST", s_Vst)):
                    shp = [int(s) for s in buf[:].shape]
                    dt = buf[:].dtype
                    dd = nc.dram_tensor(nm, shp, dt)
                    ctx[nm] = dd
                    mk.dma("sync", dd[:], buf[:], reads=[buf], writes=[dd],
                           tag="dbgdump")

        for tt in range(4):
            sl = slice(t0seq + tt * 512, t0seq + (tt + 1) * 512)
            lsl = slice(tt * 512, (tt + 1) * 512)
            ogb = s_xb[tt % 2]
            mk.dma("sync", ogb[:], og_view[:, :, sl], reads=[OG], writes=[ogb],
                   tag="wo_ogld")
            for fc in range(8):
                ps = (p0, p1)[fc % 2]
                xr = (s_xr, s_t4)[fc % 2]
                for kc in range(8):
                    mk.emit("tensor", lambda t, fc=fc, kc=kc, ogb=ogb, ps=ps: t.matmul(
                        ps[:, 0:512], s_wo[:, kc, fc * 128:(fc + 1) * 128],
                        ogb[:, kc, :], start=(kc == 0), stop=(kc == 7)),
                        reads=[s_wo, ogb], writes=[ps], tag="wo_mm")
                mk.dma("scalar", xr[:], xf_view[:, fc, sl], reads=[XF0],
                       writes=[xr], tag="wo_xf")
                mk.emit("vector", lambda v, fc=fc, ps=ps, xr=xr: v.tensor_add(
                    s_res8[:, fc, :], ps[:, 0:512], xr[:]),
                    reads=[ps, xr], writes=[s_res8], tag="wo_res")
            for fc in range(8):
                t2 = (s_t1, s_t2)[fc % 2]
                mk.emit("scalar", lambda s, fc=fc, t2=t2: s.activation(
                    t2[:], s_res8[:, fc, :], AF.Square),
                    reads=[s_res8], writes=[t2], tag="lnSq")
                mk.emit("tensor", lambda t, fc=fc: t.matmul(
                    p6[0:1, 0:512], s_col1f[:], s_res8[:, fc, :],
                    start=(fc == 0), stop=(fc == 7), skip_group_check=True),
                    reads=[s_col1f, s_res8], writes=[p6], tag="lnS1")
                mk.emit("tensor", lambda t, fc=fc, t2=t2: t.matmul(
                    p5[0:1, 0:512], s_col1f[:], t2[:],
                    start=(fc == 0), stop=(fc == 7), skip_group_check=True),
                    reads=[s_col1f, t2], writes=[p5], tag="lnS2")
            mk.emit("vector", lambda v: v.tensor_scalar(
                s_mean[0:1, :], p6[0:1, 0:512], 1.0 / D, None, ALU.mult),
                reads=[p6], writes=[s_mean], tag="lnMean")
            mk.emit("scalar", lambda s: s.activation(s_mr[0:1, :], s_mean[0:1, :],
                                                     AF.Square),
                    reads=[s_mean], writes=[s_mr], tag="lnMs")
            mk.emit("vector", lambda v: v.scalar_tensor_tensor(
                s_rstd[0:1, :], p5[0:1, 0:512], 1.0 / D, s_mr[0:1, :],
                ALU.mult, ALU.subtract),
                reads=[p5, s_mr], writes=[s_rstd], tag="lnVar")
            _b2 = constcol(1e-5)[0:1, :]
            mk.emit("scalar", lambda s, _b2=_b2: s.activation(
                s_rstd[0:1, :], s_rstd[0:1, :], AF.Sqrt, bias=_b2),
                reads=[s_rstd, ctx["s_const"]], writes=[s_rstd], tag="lnSqrt")
            mk.emit("vector", lambda v: v.reciprocal(s_rstd[0:1, :], s_rstd[0:1, :]),
                    reads=[s_rstd], writes=[s_rstd], tag="lnRec")
            mk.emit("vector", lambda v: v.tensor_mul(
                s_mr[0:1, :], s_mean[0:1, :], s_rstd[0:1, :]),
                reads=[s_mean, s_rstd], writes=[s_mr], tag="lnMr")
            mk.emit("tensor", lambda t: t.matmul(
                p6[:, 0:512], s_row1f[0:1, 0:128], s_rstd[0:1, 0:512],
                start=True, stop=True),
                reads=[s_row1f, s_rstd], writes=[p6], tag="lnBc1")
            mk.emit("tensor", lambda t: t.matmul(
                p5[:, 0:512], s_row1f[0:1, 0:128], s_mr[0:1, 0:512],
                start=True, stop=True),
                reads=[s_row1f, s_mr], writes=[p5], tag="lnBc2")
            for fc in range(8):
                t1 = (s_t1, s_t2)[fc % 2]
                xb16 = (s_x1b, s_x1b2)[fc % 2]
                mk.emit("vector", lambda v, fc=fc, t1=t1: v.tensor_tensor(
                    t1[:], s_res8[:, fc, :], p6[:, 0:512], ALU.mult),
                    reads=[s_res8, p6], writes=[t1], tag="ln_ap1")
                mk.emit("vector", lambda v, t1=t1: v.tensor_tensor(
                    t1[:], t1[:], p5[:, 0:512], ALU.subtract),
                    reads=[t1, p5], writes=[t1], tag="ln_ap2")
                mk.dma("sync", xf1_view[:, fc, sl], t1[:], reads=[t1],
                       writes=[XF1], tag="ln_st32")
                mk.emit("scalar", lambda s, t1=t1, xb16=xb16: s.copy(xb16[:], t1[:]),
                        reads=[t1], writes=[xb16], tag="ln_cp16")
                mk.dma("scalar", xb1_view[:, fc, sl], xb16[:],
                       reads=[xb16], writes=[XB1], tag="ln_st16")
                mk.emit("vector", lambda v, t1=t1: v.reduce_max(
                    s_t3[:, 0:1], t1[:], AX.X, apply_absolute_value=True),
                    reads=[t1], writes=[s_t3], tag="amax1")
                mk.emit("vector", lambda v: v.tensor_max(
                    s_amax[:], s_amax[:], s_t3[:, 0:1]),
                    reads=[s_t3, s_amax], writes=[s_amax], tag="amax2")

    mk.emit("tensor", lambda t: t.transpose(ps_small[0:1, 0:128], s_amax[:], eye32[:]),
            reads=[s_amax, eye32], writes=[ps_small], tag="amaxT")
    mk.emit("vector", lambda v: v.reduce_max(
        s_amax1[:], ps_small[0:1, 0:128], AX.X),
        reads=[ps_small], writes=[s_amax1], tag="amaxR")
    mk.dma("sync", AMAXL[:], s_amax1[:], reads=[s_amax1], writes=[AMAXL], tag="amaxSt")


def emit_ln_stage(ctx, bufs, feed_pre, feed, XFres, XRs, XFout, XBout,
                  amax_buf=None, bias_cols=None, tt_range=None):
    nc, mk = ctx["nc"], ctx["mk"]
    constcol = ctx["constcol"]
    e = bufs.get("e") or ctx["stack"].enter_context
    pp, p5, p6 = bufs["pp"], bufs["p5"], bufs["p6"]
    s_mean, s_mr, s_rstd = bufs["s_mean"], bufs["s_mr"], bufs["s_rstd"]
    s_col1f, s_fones = bufs["s_col1f"], bufs["s_fones"]
    ck = bufs.get("cache_key")
    cache = ctx.setdefault("_ln_cache", {})
    if ck is not None and ck in cache:
        s_res, s_xr, s_t1, s_t2, s_x1b = cache[ck]
    else:
        ln_uid = ctx.get("_ln_uid", 0)
        ctx["_ln_uid"] = ln_uid + 1
        s_res = e(nc.sbuf_tensor(f"lnres{ln_uid}", [128, 8, 512], F32))
        s_xr = [e(nc.sbuf_tensor(f"lnxr{ln_uid}_{i}", [128, 512], F32)) for i in range(2)]
        s_t1 = [e(nc.sbuf_tensor(f"lnt1{ln_uid}_{i}", [128, 512], F32)) for i in range(2)]
        s_t2 = [e(nc.sbuf_tensor(f"lnt2{ln_uid}_{i}", [128, 512], F32)) for i in range(2)]
        s_x1b = [e(nc.sbuf_tensor(f"lnxb{ln_uid}_{i}", [128, 512], BF16)) for i in range(2)]
        if ck is not None:
            cache[ck] = (s_res, s_xr, s_t1, s_t2, s_x1b)
    xfres_v = XFres[:].rearrange("(c p) t -> p c t", p=128)
    xfo_v = XFout[:].rearrange("(c p) t -> p c t", p=128)
    if XBout is not None:
        xbo_v = XBout[:].rearrange("(c p) t -> p c t", p=128)
    _beps = constcol(1e-5)[0:1, :]

    for tt in (tt_range if tt_range is not None else range(R // 512)):
        sl = slice(tt * 512, (tt + 1) * 512)
        if feed_pre is not None:
            feed_pre(tt)
        for fc in range(8):
            ps = pp[fc % 2]
            yap = feed(tt, fc, ps)
            if yap is not None:
                psap, psbuf = yap
            else:
                psap, psbuf = ps[:, 0:512], ps
            xr = s_xr[fc % 2]
            mk.dma("scalar", xr[:], xfres_v[:, fc, sl], reads=[XFres],
                   writes=[xr], tag="lnh_xf")
            if bias_cols is not None:
                bc = bias_cols[:, fc:fc + 1]
                mk.emit("vector", lambda v, psap=psap, bc=bc, xr=xr, fc=fc:
                        v.scalar_tensor_tensor(
                            s_res[:, fc, :], psap, bc, xr[:], ALU.add, ALU.add),
                        reads=[psbuf, xr, bufs["s_biascols"]], writes=[s_res],
                        tag="lnh_res")
            else:
                mk.emit("vector", lambda v, psap=psap, xr=xr, fc=fc: v.tensor_add(
                    s_res[:, fc, :], psap, xr[:]),
                    reads=[psbuf, xr], writes=[s_res], tag="lnh_res")
        for fc in range(8):
            t2 = s_t2[fc % 2]
            mk.emit("scalar", lambda s, fc=fc, t2=t2: s.activation(
                t2[:], s_res[:, fc, :], AF.Square),
                reads=[s_res], writes=[t2], tag="lnh_sq")
            mk.emit("tensor", lambda t, fc=fc: t.matmul(
                p6[0:1, 0:512], s_col1f[:], s_res[:, fc, :],
                start=(fc == 0), stop=(fc == 7), skip_group_check=True),
                reads=[s_col1f, s_res], writes=[p6], tag="lnh_S1")
            mk.emit("tensor", lambda t, fc=fc, t2=t2: t.matmul(
                p5[0:1, 0:512], s_col1f[:], t2[:],
                start=(fc == 0), stop=(fc == 7), skip_group_check=True),
                reads=[s_col1f, t2], writes=[p5], tag="lnh_S2")
        mk.emit("vector", lambda v: v.tensor_scalar(
            s_mean[0:1, :], p6[0:1, 0:512], 1.0 / D, None, ALU.mult),
            reads=[p6], writes=[s_mean], tag="lnh_mean")
        mk.emit("scalar", lambda s: s.activation(s_mr[0:1, :], s_mean[0:1, :],
                                                 AF.Square),
                reads=[s_mean], writes=[s_mr], tag="lnh_ms")
        mk.emit("vector", lambda v: v.scalar_tensor_tensor(
            s_rstd[0:1, :], p5[0:1, 0:512], 1.0 / D, s_mr[0:1, :],
            ALU.mult, ALU.subtract),
            reads=[p5, s_mr], writes=[s_rstd], tag="lnh_var")
        mk.emit("scalar", lambda s, _beps=_beps: s.activation(
            s_rstd[0:1, :], s_rstd[0:1, :], AF.Sqrt, bias=_beps),
            reads=[s_rstd, ctx["s_const"]], writes=[s_rstd], tag="lnh_sqrt")
        mk.emit("vector", lambda v: v.reciprocal(s_rstd[0:1, :], s_rstd[0:1, :]),
                reads=[s_rstd], writes=[s_rstd], tag="lnh_rec")
        mk.emit("vector", lambda v: v.tensor_mul(
            s_mr[0:1, :], s_mean[0:1, :], s_rstd[0:1, :]),
            reads=[s_mean, s_rstd], writes=[s_mr], tag="lnh_mr")
        mk.emit("tensor", lambda t: t.matmul(
            p6[:, 0:512], s_fones[0:1, 0:128], s_rstd[0:1, 0:512],
            start=True, stop=True),
            reads=[s_fones, s_rstd], writes=[p6], tag="lnh_bc1")
        mk.emit("tensor", lambda t: t.matmul(
            p5[:, 0:512], s_fones[0:1, 0:128], s_mr[0:1, 0:512],
            start=True, stop=True),
            reads=[s_fones, s_mr], writes=[p5], tag="lnh_bc2")
        for fc in range(8):
            t1 = s_t1[fc % 2]
            t2 = s_t2[fc % 2]
            xb = s_x1b[fc % 2]
            mk.emit("vector", lambda v, fc=fc, t1=t1: v.tensor_tensor(
                t1[:], s_res[:, fc, :], p6[:, 0:512], ALU.mult),
                reads=[s_res, p6], writes=[t1], tag="lnh_ap1")
            mk.emit("vector", lambda v, t1=t1, t2=t2: v.tensor_tensor(
                t2[:], t1[:], p5[:, 0:512], ALU.subtract),
                reads=[t1, p5], writes=[t2], tag="lnh_ap2")
            mk.dma("sync", xfo_v[:, fc, sl], t2[:], reads=[t2], writes=[XFout],
                   tag="lnh_st32")
            if XBout is not None:
                mk.emit("scalar", lambda s, t2=t2, xb=xb: s.copy(xb[:], t2[:]),
                        reads=[t2], writes=[xb], tag="lnh_cp16")
                mk.dma("scalar", xbo_v[:, fc, sl], xb[:], reads=[xb],
                       writes=[XBout], tag="lnh_st16")
            if amax_buf is not None:
                mk.emit("vector", lambda v, t1=t1, t2=t2: v.reduce_max(
                    t1[:, 0:1], t2[:], AX.X, apply_absolute_value=True),
                    reads=[t2], writes=[t1], tag="lnh_am1")
                mk.emit("vector", lambda v, t1=t1: v.tensor_max(
                    amax_buf[:], amax_buf[:], t1[:, 0:1]),
                    reads=[t1, amax_buf], writes=[amax_buf], tag="lnh_am2")


def emit_partition_amax(ctx, s_amax, s_out1, ps, eye32, dram_out):
    nc, mk = ctx["nc"], ctx["mk"]
    mk.emit("tensor", lambda t: t.transpose(ps[0:1, 0:128], s_amax[:], eye32[:]),
            reads=[s_amax, eye32], writes=[ps], tag="pam_T")
    mk.emit("vector", lambda v: v.reduce_max(s_out1[:], ps[0:1, 0:128], AX.X),
            reads=[ps], writes=[s_out1], tag="pam_R")
    mk.dma("sync", dram_out[:], s_out1[:], reads=[s_out1], writes=[dram_out],
           tag="pam_st")


def phase_l0_ffn(ctx):
    nc, mk = ctx["nc"], ctx["mk"]
    W = ctx["W"]
    XF1, XB1, XF2, XB2 = ctx["XF1"], ctx["XB1"], ctx["XF2"], ctx["XB2"]
    AMAXL = ctx["AMAXL"]
    e = ctx["stack"].enter_context
    eye32, eye16 = ctx["s_eye32"], ctx["s_eye16"]
    constcol = ctx["constcol"]
    QMAX = 32767.0

    UPRE = nc.dram_tensor("UPRE", [DFF, R], BF16)
    GPRE = nc.dram_tensor("GPRE", [DFF, R], BF16)
    GS = nc.dram_tensor("GS", [DFF, R], BF16)
    OPRE = nc.dram_tensor("OPRE", [DFF, R], BF16)
    XQ = nc.dram_tensor("XQ", [D, R], BF16)
    XR2 = nc.dram_tensor("XR2", [D, R], F32)
    AMG = nc.dram_tensor("AMG", [1, 8], F32)
    AMGo = nc.dram_tensor("AMGo", [1, 8], F32)
    for nm in ("UPRE", "GPRE", "GS", "OPRE", "XQ"):
        ctx[nm] = locals()[nm]

    s_fones = e(nc.sbuf_tensor("f_ones", [1, 512], F32))
    s_col1f = e(nc.sbuf_tensor("f_col1f", [128, 1], F32))
    s_scl = e(nc.sbuf_tensor("f_scl", [1, 8], F32))
    s_cols = e(nc.sbuf_tensor("f_cols", [128, 8], F32))
    s_am_u = e(nc.sbuf_tensor("f_amu", [128, 1], F32))
    s_am_g = e(nc.sbuf_tensor("f_amg", [128, 1], F32))
    s_am_s = e(nc.sbuf_tensor("f_ams", [128, 1], F32))
    s_am1 = e(nc.sbuf_tensor("f_am1", [1, 2], F32))
    s_wdT = e(nc.sbuf_tensor("f_wdT", [128, 24, 1024], BF16))

    p0 = e(nc.psum_tensor("f_p0", [128, 512], F32))
    p1 = e(nc.psum_tensor("f_p1", [128, 512], F32))
    p4 = e(nc.psum_tensor("f_p4", [128, 256], F32))
    p5 = e(nc.psum_tensor("f_p5", [128, 512], F32))
    p6 = e(nc.psum_tensor("f_p6", [128, 512], F32))

    mk.emit("vector", lambda v: v.memset(s_fones[:], 1.0), writes=[s_fones], memset=True)
    mk.emit("vector", lambda v: v.memset(s_col1f[:], 1.0), writes=[s_col1f], memset=True)
    for b in (s_am_u, s_am_g, s_am_s):
        mk.emit("vector", lambda v, b=b: v.memset(b[:], 0.0), writes=[b], memset=True)

    p4b = p4[:].bitcast(BF16)

    mk.dma("sync", AMG[0:1, 0:1], AMAXL[:], reads=[AMAXL], writes=[AMG], tag="am_cp")
    mk.collective("AllReduce", ALU.max, [AMG[:]], [AMGo[:]],
                  reads=[AMG], writes=[AMGo])
    mk.dma("sync", s_scl[0:1, 0:1], AMGo[0:1, 0:1], reads=[AMGo], writes=[s_scl],
           tag="am_ld")

    def emit_scale_pair(src_slice, dst_scl_col, dst_inv_col, qmax):
        mk.emit("vector", lambda v: v.tensor_scalar(
            s_scl[0:1, 2:3], src_slice, 1.0 / qmax, 1e-12, ALU.mult, ALU.max),
            reads=[s_scl], writes=[s_scl], tag="scl_s")
        mk.emit("vector", lambda v: v.reciprocal(s_scl[0:1, 3:4], s_scl[0:1, 2:3]),
                reads=[s_scl], writes=[s_scl], tag="scl_i")
        mk.emit("tensor", lambda t: t.matmul(
            p4[0:128, 0:2], s_fones[0:1, 0:128], s_scl[0:1, 2:4],
            start=True, stop=True),
            reads=[s_fones, s_scl], writes=[p4], tag="scl_bc")
        mk.emit("vector", lambda v: v.tensor_copy(dst_scl_col, p4[0:128, 0:1]),
                reads=[p4], writes=[s_cols], tag="scl_c1")
        mk.emit("vector", lambda v: v.tensor_copy(dst_inv_col, p4[0:128, 1:2]),
                reads=[p4], writes=[s_cols], tag="scl_c2")

    emit_scale_pair(s_scl[0:1, 0:1], s_cols[:, 0:1], s_cols[:, 1:2], QMAX)
    if ctx.get("ffn_stop") == "scales":
        return


    from contextlib import ExitStack as _ES
    xf1_v = XF1[:].rearrange("(c p) t -> p c t", p=128)
    with _ES() as sub:
        se = sub.enter_context
        s_xq = se(nc.sbuf_tensor("f_xq", [128, 8, R], BF16))
        s_x1t = se(nc.sbuf_tensor("f_x1t", [128, 8, 512], F32))
        s_xi = se(nc.sbuf_tensor("f_xi", [128, 8, 512], I32))
        s_wt = [se(nc.sbuf_tensor(f"f_wt{i}", [128, 8, 128], BF16)) for i in range(2)]
        s_sp2 = [se(nc.sbuf_tensor(f"f_sp{i}", [128, 512], BF16)) for i in range(2)]
        s_wraw = [se(nc.sbuf_tensor(f"f_wraw{i}", [128, DFF], F32)) for i in range(2)]
        s_am32 = se(nc.sbuf_tensor("f_am32", [128, 96], F32))
        s_rs32 = se(nc.sbuf_tensor("f_rs32", [128, 96], F32))
        s_wi = se(nc.sbuf_tensor("f_wi", [128, DFF], I32))
        s_wf = se(nc.sbuf_tensor("f_wf", [128, DFF], F32))
        s_wq16 = se(nc.sbuf_tensor("f_wq16", [128, DFF], BF16))

        def dequant(tile_f32, width):
            nblk = width // 32
            v3 = tile_f32.rearrange("p (b w) -> p b w", w=32)
            mk.emit("vector", lambda v, v3=v3, nblk=nblk: v.tensor_reduce(
                s_am32[:, 0:nblk], v3, AX.X, ALU.max, apply_absolute_value=True),
                reads=[s_wraw[0], s_wraw[1]], writes=[s_am32], tag="dq_am")
            mk.emit("vector", lambda v, nblk=nblk: v.tensor_scalar(
                s_rs32[:, 0:nblk], s_am32[:, 0:nblk], 1.0 / 7.0, 1e-12, ALU.mult, ALU.max),
                reads=[s_am32], writes=[s_rs32], tag="dq_s")
            mk.emit("vector", lambda v, nblk=nblk: v.reciprocal(
                s_am32[:, 0:nblk], s_rs32[:, 0:nblk]),
                reads=[s_rs32], writes=[s_am32], tag="dq_rs")
            mk.emit("vector", lambda v, v3=v3, nblk=nblk, width=width: v.tensor_tensor(
                s_wi[:, 0:width].rearrange("p (b w) -> p b w", w=32), v3,
                s_am32[:, 0:nblk].unsqueeze(2).broadcast_to((128, nblk, 32)), ALU.mult),
                reads=[s_wraw[0], s_wraw[1], s_am32], writes=[s_wi], tag="dq_r")
            mk.emit("gpsimd", lambda g, width=width: g.tensor_copy(
                s_wf[:, 0:width], s_wi[:, 0:width]),
                reads=[s_wi], writes=[s_wf], tag="dq_f")
            mk.emit("gpsimd", lambda g, nblk=nblk, width=width: g.tensor_tensor(
                s_wq16[:, 0:width].rearrange("p (b w) -> p b w", w=32),
                s_wf[:, 0:width].rearrange("p (b w) -> p b w", w=32),
                s_rs32[:, 0:nblk].unsqueeze(2).broadcast_to((128, nblk, 32)), ALU.mult),
                reads=[s_wf, s_rs32], writes=[s_wq16], tag="dq_q")

        for ot in range(8):
            wr = s_wraw[ot % 2]
            mk.dma("gpsimd", wr[:], W["L0_w_down"][ot * 128:(ot + 1) * 128, :],
                   reads=[W["L0_w_down"]], writes=[wr], tag="wd_ld")
            dequant(wr[:], DFF)
            for kc in range(24):
                mk.emit("tensor", lambda t, kc=kc, p4b=p4b: t.transpose(
                    p4b[:, 256:384], s_wq16[:, kc * 128:(kc + 1) * 128], eye16[:]),
                    reads=[s_wq16, eye16], writes=[p4], tag="wd_T")
                mk.emit("vector", lambda v, kc=kc, ot=ot, p4b=p4b: v.tensor_copy(
                    s_wdT[:, kc, ot * 128:(ot + 1) * 128], p4b[:, 256:384]),
                    reads=[p4], writes=[s_wdT], tag="wd_ev")

        if ctx.get("ffn_stop") == "wdown":
            return
        for tt in range(R // 512):
            sl = slice(tt * 512, (tt + 1) * 512)
            mk.dma("sync", s_x1t[:], xf1_v[:, :, sl], reads=[XF1], writes=[s_x1t],
                   tag="xq_ld")
            mk.emit("vector", lambda v: v.tensor_scalar(
                s_xi[:], s_x1t[:], s_cols[:, 1:2], None, ALU.mult),
                reads=[s_x1t, s_cols], writes=[s_xi], tag="xq_i")
            mk.emit("gpsimd", lambda v: v.tensor_copy(s_x1t[:], s_xi[:]),
                    reads=[s_xi], writes=[s_x1t], tag="xq_f")
            mk.emit("gpsimd", lambda v, sl=sl: v.tensor_scalar(
                s_xq[:, :, sl], s_x1t[:], s_cols[:, 0:1], None, ALU.mult),
                reads=[s_x1t, s_cols], writes=[s_xq], tag="xq_q")

        if ctx.get("ffn_stop") == "xq":
            return
        for (wname, PRE, amcol) in (("L0_w_up", UPRE, s_am_u), ("L0_w_gate", GPRE, s_am_g)):
            pre_v = PRE[:].rearrange("(c p) t -> p c t", p=128)
            for ot in range(24):
                wr = s_wraw[ot % 2]
                wt = s_wt[ot % 2]
                mk.dma("gpsimd", wr[:, 0:1024], W[wname][ot * 128:(ot + 1) * 128, :],
                       reads=[W[wname]], writes=[wr], tag="w_ld")
                dequant(wr[:, 0:1024], 1024)
                for kc in range(8):
                    mk.emit("tensor", lambda t, kc=kc, p4b=p4b: t.transpose(
                        p4b[:, 256:384], s_wq16[:, kc * 128:(kc + 1) * 128], eye16[:]),
                        reads=[s_wq16, eye16], writes=[p4], tag="w_T")
                    mk.emit("vector", lambda v, kc=kc, wt=wt, p4b=p4b: v.tensor_copy(
                        wt[:, kc, :], p4b[:, 256:384]),
                        reads=[p4], writes=[wt], tag="w_ev")
                for tt in range(R // 512):
                    sl = slice(tt * 512, (tt + 1) * 512)
                    ps = (p0, p1)[tt % 2]
                    if not ctx.get("w_nomm"):
                        for kc in range(8):
                            mk.emit("tensor", lambda t, kc=kc, sl=sl, ps=ps, wt=wt: t.matmul(
                                ps[:, 0:512], wt[:, kc, :], s_xq[:, kc, sl],
                                start=(kc == 0), stop=(kc == 7)),
                                reads=[wt, s_xq], writes=[ps], tag="w_mm")
                    if not ctx.get("w_noamax"):
                        mk.emit("vector", lambda v, ps=ps: v.reduce_max(
                            s_x1t[:, 0, 0:1], ps[:, 0:512], AX.X, apply_absolute_value=True),
                            reads=[ps], writes=[s_x1t], tag="w_am1")
                        mk.emit("vector", lambda v, amcol=amcol: v.tensor_max(
                            amcol[:], amcol[:], s_x1t[:, 0, 0:1]),
                            reads=[s_x1t, amcol], writes=[amcol], tag="w_am2")
                    if not ctx.get("w_nosp"):
                        sp = s_sp2[tt % 2]
                        mk.emit("vector", lambda v, ps=ps, sp=sp: v.tensor_copy(sp[:], ps[:, 0:512]),
                                reads=[ps], writes=[sp], tag="w_sp")
                        mk.dma("sync", pre_v[:, ot, sl], sp[:], reads=[sp],
                               writes=[PRE], tag="w_st")
                if ctx.get("ffn_stop") == f"W{ot}_{wname}":
                    return
    if ctx.get("ffn_stop") == "W":
        return
    mk.barrier()

    emit_partition_amax(ctx, s_am_u, s_am1[0:1, 0:1], p4, eye32, AMG[0:1, 0:1])
    emit_partition_amax(ctx, s_am_g, s_am1[0:1, 1:2], p4, eye32, AMG[0:1, 1:2])
    mk.collective("AllReduce", ALU.max, [AMG[:]], [AMGo[:]],
                  reads=[AMG], writes=[AMGo])
    mk.dma("sync", s_scl[0:1, 4:6], AMGo[0:1, 0:2], reads=[AMGo], writes=[s_scl], tag="m2_ld")
    emit_scale_pair(s_scl[0:1, 4:5], s_cols[:, 2:3], s_cols[:, 3:4], QMAX)
    emit_scale_pair(s_scl[0:1, 5:6], s_cols[:, 4:5], s_cols[:, 5:6], QMAX)

    gpre_v = GPRE[:].rearrange("(c p) t -> p c t", p=128)
    upre_v = UPRE[:].rearrange("(c p) t -> p c t", p=128)
    gs_v = GS[:].rearrange("(c p) t -> p c t", p=128)
    opre_v = OPRE[:].rearrange("(c p) t -> p c t", p=128)
    with _ES() as sub:
        se = sub.enter_context
        s_in1 = [se(nc.sbuf_tensor(f"f_in1{i}", [128, 4, 512], BF16)) for i in range(2)]
        s_in2 = [se(nc.sbuf_tensor(f"f_in2{i}", [128, 4, 512], BF16)) for i in range(2)]
        s_out = [se(nc.sbuf_tensor(f"f_out{i}", [128, 4, 512], BF16)) for i in range(2)]
        s_f1 = [se(nc.sbuf_tensor(f"f_f1{i}", [128, 4, 512], F32)) for i in range(2)]
        s_f2 = [se(nc.sbuf_tensor(f"f_f2{i}", [128, 4, 512], F32)) for i in range(2)]
        s_i1 = [se(nc.sbuf_tensor(f"f_i1{i}", [128, 4, 512], I32)) for i in range(2)]

        def qdq_from(dstbuf, srcbuf, inv_col, scl_col, i1):
            mk.emit("vector", lambda v, srcbuf=srcbuf, inv_col=inv_col, i1=i1:
                    v.tensor_scalar(i1[:], srcbuf[:], inv_col, None, ALU.mult),
                    reads=[srcbuf, s_cols], writes=[i1], tag="qd_i")
            mk.emit("gpsimd", lambda v, dstbuf=dstbuf, i1=i1: v.tensor_copy(
                dstbuf[:], i1[:]),
                reads=[i1], writes=[dstbuf], tag="qd_f")
            mk.emit("gpsimd", lambda v, dstbuf=dstbuf, scl_col=scl_col: v.tensor_scalar(
                dstbuf[:], dstbuf[:], scl_col, None, ALU.mult),
                reads=[dstbuf, s_cols], writes=[dstbuf], tag="qd_q")

        for it in range(48):
            gt, tt = divmod(it, 8)
            sl = slice(tt * 512, (tt + 1) * 512)
            ocs = slice(gt * 4, (gt + 1) * 4)
            pi = it % 2
            in1, f1, f2, i1, out = s_in1[pi], s_f1[pi], s_f2[pi], s_i1[pi], s_out[pi]
            mk.dma("sync", in1[:], gpre_v[:, ocs, sl], reads=[GPRE],
                   writes=[in1], tag="x_ld")
            qdq_from(f1, in1, s_cols[:, 5:6], s_cols[:, 4:5], i1)
            mk.emit("scalar", lambda s, f1=f1, f2=f2: s.activation(
                f2[:], f1[:], AF.Sigmoid),
                reads=[f1], writes=[f2], tag="x_sig")
            mk.emit("vector", lambda v, f2=f2, i1=i1: v.tensor_scalar(
                i1[:], f2[:], 65536.0, None, ALU.mult),
                reads=[f2], writes=[i1], tag="x_si")
            mk.emit("gpsimd", lambda v, f2=f2, i1=i1: v.tensor_copy(f2[:], i1[:]),
                    reads=[i1], writes=[f2], tag="x_sf")
            mk.emit("vector", lambda v, f2=f2: v.tensor_scalar(
                f2[:], f2[:], 65535.0, None, ALU.min),
                reads=[f2], writes=[f2], tag="x_sc")
            mk.emit("gpsimd", lambda v, f1=f1, f2=f2: v.tensor_tensor(
                f2[:], f1[:], f2[:], ALU.mult),
                reads=[f1, f2], writes=[f2], tag="x_gs1")
            mk.emit("vector", lambda v, f2=f2, out=out: v.tensor_scalar(
                out[:], f2[:], float(2.0 ** -16), None, ALU.mult),
                reads=[f2], writes=[out], tag="x_gs2")
            mk.emit("vector", lambda v, f1=f1, out=out: v.reduce_max(
                f1[:, 0, 0:1], out[:].rearrange("p a b -> p (a b)"),
                AX.X, apply_absolute_value=True),
                reads=[out], writes=[f1], tag="x_am1")
            mk.emit("vector", lambda v, f1=f1: v.tensor_max(
                s_am_s[:], s_am_s[:], f1[:, 0, 0:1]),
                reads=[f1, s_am_s], writes=[s_am_s], tag="x_am2")
            mk.dma("scalar", gs_v[:, ocs, sl], out[:], reads=[out],
                   writes=[GS], tag="x_st")

        if ctx.get("ffn_stop") == "X":
            return
        mk.barrier()
        emit_partition_amax(ctx, s_am_s, s_am1[0:1, 0:1], p4, eye32, AMG[0:1, 0:1])
        mk.collective("AllReduce", ALU.max, [AMG[:]], [AMGo[:]],
                      reads=[AMG], writes=[AMGo])
        mk.dma("sync", s_scl[0:1, 6:7], AMGo[0:1, 0:1], reads=[AMGo], writes=[s_scl],
               tag="m3_ld")
        emit_scale_pair(s_scl[0:1, 6:7], s_cols[:, 6:7], s_cols[:, 7:8], QMAX)
        mk.emit("vector", lambda v: v.memset(s_am_s[:], 0.0), writes=[s_am_s],
                memset=True)

        for it in range(48):
            gt, tt = divmod(it, 8)
            sl = slice(tt * 512, (tt + 1) * 512)
            ocs = slice(gt * 4, (gt + 1) * 4)
            pi = it % 2
            in1, in2, f1, f2, i1, out = (s_in1[pi], s_in2[pi], s_f1[pi],
                                         s_f2[pi], s_i1[pi], s_out[pi])
            mk.dma("sync", in1[:], gs_v[:, ocs, sl], reads=[GS],
                   writes=[in1], tag="y_ld1")
            mk.dma("scalar", in2[:], upre_v[:, ocs, sl], reads=[UPRE],
                   writes=[in2], tag="y_ld2")
            qdq_from(f1, in1, s_cols[:, 7:8], s_cols[:, 6:7], i1)
            qdq_from(f2, in2, s_cols[:, 3:4], s_cols[:, 2:3], i1)
            mk.emit("gpsimd", lambda v, f1=f1, f2=f2: v.tensor_tensor(
                f1[:], f1[:], f2[:], ALU.mult),
                reads=[f1, f2], writes=[f1], tag="y_op")
            mk.emit("gpsimd", lambda v, f1=f1, out=out: v.tensor_copy(out[:], f1[:]),
                    reads=[f1], writes=[out], tag="y_cp")
            mk.emit("vector", lambda v, f1=f1, f2=f2: v.reduce_max(
                f2[:, 0, 0:1], f1[:].rearrange("p a b -> p (a b)"),
                AX.X, apply_absolute_value=True),
                reads=[f1], writes=[f2], tag="y_am1")
            mk.emit("vector", lambda v, f2=f2: v.tensor_max(
                s_am_s[:], s_am_s[:], f2[:, 0, 0:1]),
                reads=[f2, s_am_s], writes=[s_am_s], tag="y_am2")
            mk.dma("scalar", opre_v[:, ocs, sl], out[:], reads=[out],
                   writes=[OPRE], tag="y_st")

        if ctx.get("ffn_stop") == "Y":
            return
        mk.barrier()
        emit_partition_amax(ctx, s_am_s, s_am1[0:1, 0:1], p4, eye32, AMG[0:1, 0:1])
        mk.collective("AllReduce", ALU.max, [AMG[:]], [AMGo[:]],
                      reads=[AMG], writes=[AMGo])
        mk.dma("sync", s_scl[0:1, 7:8], AMGo[0:1, 0:1], reads=[AMGo], writes=[s_scl],
               tag="m4_ld")
        emit_scale_pair(s_scl[0:1, 7:8], s_cols[:, 6:7], s_cols[:, 7:8], QMAX)

        s_o3 = se(nc.sbuf_tensor("f_o3", [128, 24, 512], BF16))
        bufs = dict(pp=(p0, p1), p5=p5, p6=p6, e=se,
                    s_mean=se(nc.sbuf_tensor("f_lmean", [1, 512], F32)),
                    s_mr=se(nc.sbuf_tensor("f_lmr", [1, 512], F32)),
                    s_rstd=se(nc.sbuf_tensor("f_lrstd", [1, 512], F32)),
                    s_col1f=s_col1f, s_fones=s_fones)

        def feed_pre(tt):
            sl = slice(tt * 512, (tt + 1) * 512)
            for gt in range(6):
                ocs = slice(gt * 4, (gt + 1) * 4)
                pi = gt % 2
                in1, f1, i1 = s_in1[pi], s_f1[pi], s_i1[pi]
                mk.dma("sync", in1[:], opre_v[:, ocs, sl], reads=[OPRE],
                       writes=[in1], tag="z_ld")
                qdq_from(f1, in1, s_cols[:, 7:8], s_cols[:, 6:7], i1)
                mk.emit("vector", lambda v, ocs=ocs, f1=f1: v.tensor_copy(
                    s_o3[:, ocs, :], f1[:]),
                    reads=[f1], writes=[s_o3], tag="z_cp")

        def feed(tt, fc, ps):
            for kc in range(24):
                mk.emit("tensor", lambda t, kc=kc, fc=fc, ps=ps: t.matmul(
                    ps[:, 0:512], s_wdT[:, kc, fc * 128:(fc + 1) * 128],
                    s_o3[:, kc, :], start=(kc == 0), stop=(kc == 23)),
                    reads=[s_wdT, s_o3], writes=[ps], tag="z_mm")

        emit_ln_stage(ctx, bufs, feed_pre, feed, XF1, XR2, XF2, XB2)


def phase_l1_attn(ctx):
    nc, mk = ctx["nc"], ctx["mk"]
    W = ctx["W"]
    XF2, XB2, XF3, XB3 = ctx["XF2"], ctx["XB2"], ctx["XF3"], ctx["XB3"]
    e = ctx["stack"].enter_context
    eye32, eye16 = ctx["s_eye32"], ctx["s_eye16"]
    constcol = ctx["constcol"]
    TWO_PI = float(2.0 * np.pi)

    ZPRE = nc.dram_tensor("ZPRE", [D, R], F32)
    ZB = nc.dram_tensor("ZB", [D, R], BF16)
    QB = nc.dram_tensor("QB", [D, R], BF16)
    RB = nc.dram_tensor("RB", [D, R], BF16)
    HB = nc.dram_tensor("HB", [D, R], BF16)
    QC = nc.dram_tensor("QC", [2 * D, R], BF16)
    RC = nc.dram_tensor("RC", [2 * D, R], BF16)
    OH = nc.dram_tensor("OH", [D, R], BF16)
    XR3 = nc.dram_tensor("XR3", [D, R], F32)
    for nm in ("ZPRE", "ZB", "QB", "RB", "HB", "QC", "RC", "OH"):
        ctx[nm] = locals()[nm]

    xb2_v = XB2[:].rearrange("(c p) t -> p c t", p=128)
    zpre_v = ZPRE[:].rearrange("(c p) t -> p c t", p=128)
    zb_v = ZB[:].rearrange("(c p) t -> p c t", p=128)

    s_fones = e(nc.sbuf_tensor("l_ones", [1, 512], F32))
    s_col1f = e(nc.sbuf_tensor("l_col1f", [128, 1], F32))
    s_w = e(nc.sbuf_tensor("l_w", [128, 8, 1024], BF16))
    s_bc = e(nc.sbuf_tensor("l_bc", [128, 8], F32))
    s_lng = e(nc.sbuf_tensor("l_lng", [128, 8], F32))
    s_lnb = e(nc.sbuf_tensor("l_lnb", [128, 8], F32))
    s_v8 = e(nc.sbuf_tensor("l_v8", [8, 128], F32))
    s_xt = [e(nc.sbuf_tensor(f"l_xt{i}", [128, 8, 512], BF16)) for i in range(2)]
    s_t1 = e(nc.sbuf_tensor("l_t1", [128, 512], F32))
    s_t2 = e(nc.sbuf_tensor("l_t2", [128, 512], F32))
    s_t3 = e(nc.sbuf_tensor("l_t3", [128, 512], F32))
    s_i1 = e(nc.sbuf_tensor("l_i1", [128, 512], I32))
    s_b16 = e(nc.sbuf_tensor("l_b16", [128, 512], BF16))
    s_b16b = e(nc.sbuf_tensor("l_b16b", [128, 512], BF16))
    s_mean = e(nc.sbuf_tensor("l_mean", [1, 512], F32))
    s_mr = e(nc.sbuf_tensor("l_mr", [1, 512], F32))
    s_rstd = e(nc.sbuf_tensor("l_rstd", [1, 512], F32))
    s_xr = e(nc.sbuf_tensor("l_xr", [128, 512], F32))
    s_x1b = e(nc.sbuf_tensor("l_x1b", [128, 512], BF16))
    s_wphi = e(nc.sbuf_tensor("l_wphi", [128, 64], BF16))
    s_bphic = e(nc.sbuf_tensor("l_bphic", [128, 1], F32))

    p0 = e(nc.psum_tensor("l_p0", [128, 512], F32))
    p1 = e(nc.psum_tensor("l_p1", [128, 512], F32))
    p4 = e(nc.psum_tensor("l_p4", [128, 256], F32))
    p5 = e(nc.psum_tensor("l_p5", [128, 512], F32))
    p6 = e(nc.psum_tensor("l_p6", [128, 512], F32))
    p7 = e(nc.psum_tensor("l_p7", [128, 512], F32))
    p4b = p4[:].bitcast(BF16)

    mk.emit("vector", lambda v: v.memset(s_fones[:], 1.0), writes=[s_fones], memset=True)
    mk.emit("vector", lambda v: v.memset(s_col1f[:], 1.0), writes=[s_col1f], memset=True)

    def load_cols(wname, dst):
        mk.dma("sync", s_v8[:], W[wname][:].rearrange("(a b) -> a b", a=8),
               reads=[W[wname]], writes=[s_v8], tag="lc_ld")
        mk.emit("tensor", lambda t: t.transpose(p4[0:128, 0:8], s_v8[:], eye32[0:8, 0:8]),
                reads=[s_v8, eye32], writes=[p4], tag="lc_T")
        mk.emit("vector", lambda v, dst=dst: v.tensor_copy(dst[:], p4[0:128, 0:8]),
                reads=[p4], writes=[dst], tag="lc_ev")

    def load_weight(wname):
        mk.dma("gpsimd", s_w[:], W[wname][:].rearrange("(c p) n -> p c n", p=128),
               reads=[W[wname]], writes=[s_w], tag="lw_" + wname)

    load_cols("L1_ln_g", s_lng)
    load_cols("L1_ln_b", s_lnb)

    load_weight("L1_Wz")
    load_cols("L1_bz", s_bc)
    s_zres = e(nc.sbuf_tensor("l_zres", [128, 8, 512], F32))
    _beps = constcol(1e-5)[0:1, :]
    for tt in range(R // 512):
        sl = slice(tt * 512, (tt + 1) * 512)
        xt = s_xt[tt % 2]
        mk.dma("sync", xt[:], xb2_v[:, :, sl], reads=[XB2], writes=[xt], tag="z_xld")
        for fc in range(8):
            ps = (p0, p1)[fc % 2]
            for kc in range(8):
                mk.emit("tensor", lambda t, kc=kc, fc=fc, ps=ps, xt=xt: t.matmul(
                    ps[:, 0:512], s_w[:, kc, fc * 128:(fc + 1) * 128], xt[:, kc, :],
                    start=(kc == 0), stop=(kc == 7)),
                    reads=[s_w, xt], writes=[ps], tag="z_mm")
            mk.emit("vector", lambda v, fc=fc, ps=ps: v.tensor_scalar(
                s_zres[:, fc, :], ps[:, 0:512], s_bc[:, fc:fc + 1], None, ALU.add),
                reads=[ps, s_bc], writes=[s_zres], tag="z_bias")
        for fc in range(8):
            t2 = (s_t2, s_t3)[fc % 2]
            mk.emit("scalar", lambda s, fc=fc, t2=t2: s.activation(
                t2[:], s_zres[:, fc, :], AF.Square),
                reads=[s_zres], writes=[t2], tag="z_sq")
            mk.emit("tensor", lambda t, fc=fc: t.matmul(
                p6[0:1, 0:512], s_col1f[:], s_zres[:, fc, :],
                start=(fc == 0), stop=(fc == 7), skip_group_check=True),
                reads=[s_col1f, s_zres], writes=[p6], tag="z_S1")
            mk.emit("tensor", lambda t, fc=fc, t2=t2: t.matmul(
                p5[0:1, 0:512], s_col1f[:], t2[:],
                start=(fc == 0), stop=(fc == 7), skip_group_check=True),
                reads=[s_col1f, t2], writes=[p5], tag="z_S2")
        mk.emit("vector", lambda v: v.tensor_scalar(
            s_mean[0:1, :], p6[0:1, 0:512], 1.0 / D, None, ALU.mult),
            reads=[p6], writes=[s_mean], tag="z_mean")
        mk.emit("scalar", lambda s: s.activation(s_mr[0:1, :], s_mean[0:1, :], AF.Square),
                reads=[s_mean], writes=[s_mr], tag="z_ms")
        mk.emit("vector", lambda v: v.scalar_tensor_tensor(
            s_rstd[0:1, :], p5[0:1, 0:512], 1.0 / D, s_mr[0:1, :], ALU.mult, ALU.subtract),
            reads=[p5, s_mr], writes=[s_rstd], tag="z_var")
        mk.emit("scalar", lambda s, _beps=_beps: s.activation(
            s_rstd[0:1, :], s_rstd[0:1, :], AF.Sqrt, bias=_beps),
            reads=[s_rstd, ctx["s_const"]], writes=[s_rstd], tag="z_sqrt")
        mk.emit("vector", lambda v: v.reciprocal(s_rstd[0:1, :], s_rstd[0:1, :]),
                reads=[s_rstd], writes=[s_rstd], tag="z_rec")
        mk.emit("vector", lambda v: v.tensor_mul(
            s_mr[0:1, :], s_mean[0:1, :], s_rstd[0:1, :]),
            reads=[s_mean, s_rstd], writes=[s_mr], tag="z_mr")
        mk.emit("tensor", lambda t: t.matmul(
            p6[:, 0:512], s_fones[0:1, 0:128], s_rstd[0:1, 0:512], start=True, stop=True),
            reads=[s_fones, s_rstd], writes=[p6], tag="z_bc1")
        mk.emit("tensor", lambda t: t.matmul(
            p5[:, 0:512], s_fones[0:1, 0:128], s_mr[0:1, 0:512], start=True, stop=True),
            reads=[s_fones, s_mr], writes=[p5], tag="z_bc2")
        for fc in range(8):
            t1 = (s_t1, s_t2)[fc % 2]
            xb = (s_x1b, s_b16b)[fc % 2]
            mk.emit("vector", lambda v, fc=fc, t1=t1: v.tensor_tensor(
                t1[:], s_zres[:, fc, :], p6[:, 0:512], ALU.mult),
                reads=[s_zres, p6], writes=[t1], tag="z_ap1")
            mk.emit("vector", lambda v, t1=t1: v.tensor_tensor(
                t1[:], t1[:], p5[:, 0:512], ALU.subtract),
                reads=[t1, p5], writes=[t1], tag="z_ap2")
            mk.emit("vector", lambda v, fc=fc, t1=t1, xb=xb: v.tensor_scalar(
                xb[:], t1[:], s_lng[:, fc:fc + 1], s_lnb[:, fc:fc + 1],
                ALU.mult, ALU.add),
                reads=[t1, s_lng, s_lnb], writes=[xb], tag="z_ap3")
            mk.dma("sync", zb_v[:, fc, sl], xb[:], reads=[xb], writes=[ZB],
                   tag="z_st2")

    for (wname, bname, DST) in (("L1_Wq", "L1_bq", QB), ("L1_Wr", "L1_br", RB),
                                ("L1_Wh", "L1_bh", HB)):
        load_weight(wname)
        load_cols(bname, s_bc)
        dst_v = DST[:].rearrange("(c p) t -> p c t", p=128)
        for tt in range(R // 512):
            sl = slice(tt * 512, (tt + 1) * 512)
            xt = s_xt[tt % 2]
            mk.dma("sync", xt[:], zb_v[:, :, sl], reads=[ZB], writes=[xt],
                   tag="q_xld")
            for fc in range(8):
                ps = (p0, p1)[fc % 2]
                for kc in range(8):
                    mk.emit("tensor", lambda t, kc=kc, fc=fc, ps=ps, xt=xt: t.matmul(
                        ps[:, 0:512], s_w[:, kc, fc * 128:(fc + 1) * 128], xt[:, kc, :],
                        start=(kc == 0), stop=(kc == 7)),
                        reads=[s_w, xt], writes=[ps], tag="q_mm")
                mk.emit("vector", lambda v, fc=fc, ps=ps: v.tensor_scalar(
                    s_b16[:], ps[:, 0:512], s_bc[:, fc:fc + 1], None, ALU.add),
                    reads=[ps, s_bc], writes=[s_b16], tag="q_ev")
                mk.dma("sync", dst_v[:, fc, sl], s_b16[:], reads=[s_b16], writes=[DST],
                       tag="q_st")

    mk.dma("gpsimd", s_wphi[0:64, :], W["L1_Wphi"][:], reads=[W["L1_Wphi"]],
           writes=[s_wphi], tag="phi_w")
    mk.dma("gpsimd", s_wphi[64:128, :], W["L1_Wphi"][:], reads=[W["L1_Wphi"]],
           writes=[s_wphi], tag="phi_w2")
    mk.dma("sync", s_v8[0:1, 0:64], W["L1_bphi"][:].rearrange("(a b) -> a b", a=1),
           reads=[W["L1_bphi"]], writes=[s_v8], tag="phib_ld")
    mk.emit("tensor", lambda t: t.transpose(p4[0:64, 0:1], s_v8[0:1, 0:64],
                                            eye32[0:1, 0:1]),
            reads=[s_v8, eye32], writes=[p4], tag="phib_T")
    mk.emit("vector", lambda v: v.tensor_copy(s_bphic[0:64, :], p4[0:64, 0:1]),
            reads=[p4], writes=[s_bphic], tag="phib_c1")
    mk.emit("tensor", lambda t: t.transpose(p4[0:64, 1:2], s_v8[0:1, 0:64],
                                            eye32[0:1, 0:1]),
            reads=[s_v8, eye32], writes=[p4], tag="phib_T2")
    mk.emit("vector", lambda v: v.tensor_copy(s_bphic[64:128, :], p4[0:64, 1:2]),
            reads=[p4], writes=[s_bphic], tag="phib_c2")

    qb_v = QB[:].rearrange("(c p) t -> p c t", p=128)
    rb_v = RB[:].rearrange("(c p) t -> p c t", p=128)
    qc_v = QC[:].rearrange("(c x p) t -> x c p t", x=2, p=64)
    rc_v = RC[:].rearrange("(c x p) t -> x c p t", x=2, p=64)
    s_b16_2 = e(nc.sbuf_tensor("l_b16_2", [128, 512], BF16))
    s_t1_2 = e(nc.sbuf_tensor("l_t1_2", [128, 512], F32))
    s_t2_2 = e(nc.sbuf_tensor("l_t2_2", [128, 512], F32))
    s_t3_2 = e(nc.sbuf_tensor("l_t3_2", [128, 512], F32))
    s_i1_2 = e(nc.sbuf_tensor("l_i1_2", [128, 512], I32))
    s_b16b_2 = e(nc.sbuf_tensor("l_b16b_2", [128, 512], BF16))
    qc_sets = [(s_b16, s_t1, s_t2, s_t3, s_i1, s_b16b),
               (s_b16_2, s_t1_2, s_t2_2, s_t3_2, s_i1_2, s_b16b_2)]
    qit = 0
    for (SRC_v, SRCbuf, DST_v, DSTbuf) in ((qb_v, QB, qc_v, QC), (rb_v, RB, rc_v, RC)):
        for c in range(8):
            for tt in range(R // 512):
                qit += 1
                b16, t1, t2, t3, i1, b16b = qc_sets[qit % 2]
                pphi = (p0, p7)[qit % 2]
                sl = slice(tt * 512, (tt + 1) * 512)
                mk.dma("sync", b16[:], SRC_v[:, c, sl], reads=[SRCbuf],
                       writes=[b16], tag="qc_ld")
                for half in range(2):
                    mk.emit("tensor", lambda t, half=half, b16=b16, pphi=pphi: t.matmul(
                        pphi[64 * half:64 * half + 64, 0:512],
                        s_wphi[64 * half:64 * half + 64, :],
                        b16[64 * half:64 * half + 64, :], start=True, stop=True),
                        reads=[s_wphi, b16], writes=[pphi], tag="qc_phi")
                mk.emit("vector", lambda v, t1=t1, pphi=pphi: v.tensor_scalar(
                    t1[:], pphi[:, 0:512], s_bphic[:, 0:1], None, ALU.add),
                    reads=[pphi, s_bphic], writes=[t1], tag="qc_phib")
                mk.emit("vector", lambda v, t1=t1, i1=i1: v.tensor_scalar(
                    i1[:], t1[:], 1.0 / TWO_PI, 0.25, ALU.mult, ALU.add),
                    reads=[t1], writes=[i1], tag="qc_k2")
                mk.emit("vector", lambda v, t2=t2, i1=i1: v.tensor_copy(t2[:], i1[:]),
                        reads=[i1], writes=[t2], tag="qc_k2f")
                mk.emit("vector", lambda v, t2=t2, t3=t3, t1=t1: v.scalar_tensor_tensor(
                    t3[:], t2[:], -TWO_PI, t1[:], ALU.mult, ALU.add),
                    reads=[t2, t1], writes=[t3], tag="qc_carg")
                _bhalfpi = constcol(float(np.pi / 2.0))
                mk.emit("scalar", lambda s, _b=_bhalfpi, t2=t2, t3=t3: s.activation(
                    t2[:], t3[:], AF.Sin, bias=_b),
                    reads=[t3, ctx["s_const"]], writes=[t2], tag="qc_cos")
                mk.emit("vector", lambda v, b16b=b16b, b16=b16, t2=t2: v.tensor_tensor(
                    b16b[:], b16[:], t2[:], ALU.mult),
                    reads=[b16, t2], writes=[b16b], tag="qc_mc")
                mk.dma("sync", DST_v[0, 2 * c, :, sl], b16b[0:64, :],
                       reads=[b16b], writes=[DSTbuf], tag="qc_stc0")
                mk.dma("sync", DST_v[0, 2 * c + 1, :, sl], b16b[64:128, :],
                       reads=[b16b], writes=[DSTbuf], tag="qc_stc1")
                mk.emit("vector", lambda v, t1=t1, i1=i1: v.tensor_scalar(
                    i1[:], t1[:], 1.0 / TWO_PI, None, ALU.mult),
                    reads=[t1], writes=[i1], tag="qc_k1")
                mk.emit("vector", lambda v, t2=t2, i1=i1: v.tensor_copy(t2[:], i1[:]),
                        reads=[i1], writes=[t2], tag="qc_k1f")
                mk.emit("vector", lambda v, t2=t2, t3=t3, t1=t1: v.scalar_tensor_tensor(
                    t3[:], t2[:], -TWO_PI, t1[:], ALU.mult, ALU.add),
                    reads=[t2, t1], writes=[t3], tag="qc_sarg")
                mk.emit("scalar", lambda s, t2=t2, t3=t3: s.activation(
                    t2[:], t3[:], AF.Sin),
                    reads=[t3], writes=[t2], tag="qc_sin")
                mk.emit("vector", lambda v, b16b=b16b, b16=b16, t2=t2: v.tensor_tensor(
                    b16b[:], b16[:], t2[:], ALU.mult),
                    reads=[b16, t2], writes=[b16b], tag="qc_ms")
                mk.dma("scalar", DST_v[1, 2 * c, :, sl], b16b[0:64, :],
                       reads=[b16b], writes=[DSTbuf], tag="qc_sts0")
                mk.dma("scalar", DST_v[1, 2 * c + 1, :, sl], b16b[64:128, :],
                       reads=[b16b], writes=[DSTbuf], tag="qc_sts1")

    s_qc = [e(nc.sbuf_tensor(f"l_qc{i}", [128, T], BF16)) for i in range(2)]
    s_rc = [e(nc.sbuf_tensor(f"l_rc{i}", [128, T], BF16)) for i in range(2)]
    s_hh = [e(nc.sbuf_tensor(f"l_hh{i}", [64, T], BF16)) for i in range(2)]
    s_rct = [e(nc.sbuf_tensor(f"l_rct{i}", [128, 16, 128], BF16)) for i in range(2)]
    s_hht = [e(nc.sbuf_tensor(f"l_hht{i}", [128, 16, 64], BF16)) for i in range(2)]
    s_G = [e(nc.sbuf_tensor(f"l_G{i}", [128, 128], BF16)) for i in range(2)]
    s_M = [e(nc.sbuf_tensor(f"l_M{i}", [128, 64], BF16)) for i in range(2)]
    s_inrm = [e(nc.sbuf_tensor(f"l_inrm{i}", [1, T], F32)) for i in range(2)]
    s_qcs = [e(nc.sbuf_tensor(f"l_qcs{i}", [128, 512], BF16)) for i in range(2)]
    oh_v = OH[:].rearrange("(h p) t -> p h t", p=64)
    hb_hv = HB[:].rearrange("(h p) t -> p h t", p=64)
    qc_hv = QC[:].rearrange("(h p) t -> p h t", p=128)
    rc_hv = RC[:].rearrange("(h p) t -> p h t", p=128)

    for seq in range(SEQ_PER_CORE):
        t0seq = seq * T
        ssl = slice(t0seq, t0seq + T)
        for h in range(H1):
            hp = (seq * H1 + h) % 2
            qc, rc, hh = s_qc[hp], s_rc[hp], s_hh[hp]
            rct, hht, G, M = s_rct[hp], s_hht[hp], s_G[hp], s_M[hp]
            inrm, qcs = s_inrm[hp], s_qcs[hp]
            pG = (p5, p6)[hp]
            mk.dma("sync", qc[:], qc_hv[:, h, ssl], reads=[QC], writes=[qc],
                   tag="gh_qld")
            mk.dma("sync", rc[:], rc_hv[:, h, ssl], reads=[RC], writes=[rc],
                   tag="gh_rld")
            mk.dma("scalar", hh[:], hb_hv[:, h, ssl], reads=[HB], writes=[hh],
                   tag="gh_hld")
            for c in range(16):
                tsl = slice(c * 128, (c + 1) * 128)
                pt = p4 if hp == 0 else p1
                ptb = pt[:].bitcast(BF16)
                mk.emit("tensor", lambda t, tsl=tsl, ptb=ptb, rc=rc: t.transpose(
                    ptb[:, 0:128], rc[:, tsl], eye16[:]),
                    reads=[rc, eye16], writes=[pt], tag="gh_rT")
                mk.emit("vector", lambda v, c=c, ptb=ptb, rct=rct: v.tensor_copy(
                    rct[:, c, :], ptb[:, 0:128]),
                    reads=[pt], writes=[rct], tag="gh_rTe")
                mk.emit("tensor", lambda t, tsl=tsl, ptb=ptb, hh=hh: t.transpose(
                    ptb[:, 128:192], hh[:, tsl], eye16[0:64, 0:64]),
                    reads=[hh, eye16], writes=[pt], tag="gh_hT")
                mk.emit("vector", lambda v, c=c, ptb=ptb, hht=hht: v.tensor_copy(
                    hht[:, c, :], ptb[:, 128:192]),
                    reads=[pt], writes=[hht], tag="gh_hTe")
            for c in range(16):
                mk.emit("tensor", lambda t, c=c, rct=rct, pG=pG: t.matmul(
                    pG[:, 0:128], rct[:, c, :], rct[:, c, :],
                    start=(c == 0), stop=(c == 15), skip_group_check=True),
                    reads=[rct], writes=[pG], tag="gh_G")
                mk.emit("tensor", lambda t, c=c, rct=rct, hht=hht, pG=pG: t.matmul(
                    pG[:, 128:192], rct[:, c, :], hht[:, c, :],
                    start=False, stop=(c == 15), skip_group_check=True),
                    reads=[rct, hht], writes=[pG], tag="gh_M")
            mk.emit("vector", lambda v, G=G, pG=pG: v.tensor_copy(G[:], pG[:, 0:128]),
                    reads=[pG], writes=[G], tag="gh_Ge")
            mk.emit("vector", lambda v, M=M, pG=pG: v.tensor_copy(M[:, 0:64], pG[:, 128:192]),
                    reads=[pG], writes=[M], tag="gh_Me")
            for tq in range(4):
                qsl = slice(tq * 512, (tq + 1) * 512)
                pW = (p0, p7)[hp]
                mk.emit("tensor", lambda t, qsl=qsl, G=G, qc=qc, pW=pW: t.matmul(
                    pW[:, 0:512], G[:], qc[:, qsl], start=True, stop=True),
                    reads=[G, qc], writes=[pW], tag="gh_W")
                t1l = (s_t1, s_t2)[hp]
                mk.emit("vector", lambda v, qsl=qsl, qc=qc, pW=pW, t1l=t1l: v.tensor_tensor(
                    t1l[:], pW[:, 0:512], qc[:, qsl], ALU.mult),
                    reads=[pW, qc], writes=[t1l], tag="gh_WQ")
                mk.emit("tensor", lambda t, t1l=t1l, pW=pW: t.matmul(
                    pW[0:1, 0:512], s_col1f[:], t1l[:], start=True, stop=True),
                    reads=[s_col1f, t1l], writes=[pW], tag="gh_n2")
                mrow = (s_mean, s_mr)[hp]
                mk.emit("scalar", lambda s, mrow=mrow, pW=pW: s.activation(
                    mrow[0:1, :], pW[0:1, 0:512], AF.Sqrt),
                    reads=[pW], writes=[mrow], tag="gh_sq")
                mk.emit("vector", lambda v, mrow=mrow: v.tensor_scalar(
                    mrow[0:1, :], mrow[0:1, :], 1e-12, None, ALU.max),
                    reads=[mrow], writes=[mrow], tag="gh_mx")
                mk.emit("vector", lambda v, tq=tq, qsl=qsl, mrow=mrow, inrm=inrm:
                        v.reciprocal(inrm[0:1, qsl], mrow[0:1, :]),
                        reads=[mrow], writes=[inrm], tag="gh_in")
                mk.emit("tensor", lambda t, qsl=qsl, inrm=inrm, pW=pW: t.matmul(
                    pW[:, 0:512], s_fones[0:1, 0:128], inrm[0:1, qsl],
                    start=True, stop=True),
                    reads=[s_fones, inrm], writes=[pW], tag="gh_bc")
                mk.emit("vector", lambda v, qsl=qsl, qc=qc, qcs=qcs, pW=pW: v.tensor_tensor(
                    qcs[:], qc[:, qsl], pW[:, 0:512], ALU.mult),
                    reads=[qc, pW], writes=[qcs], tag="gh_qs")
                mk.emit("tensor", lambda t, M=M, qcs=qcs, pW=pW: t.matmul(
                    pW[0:64, 0:512], M[:, 0:64], qcs[:], start=True, stop=True),
                    reads=[M, qcs], writes=[pW], tag="gh_o")
                ob = (s_b16, s_b16b)[hp]
                mk.emit("vector", lambda v, ob=ob, pW=pW: v.tensor_copy(
                    ob[0:64, :], pW[0:64, 0:512]),
                    reads=[pW], writes=[ob], tag="gh_oe")
                mk.dma("sync", oh_v[:, h, t0seq + tq * 512:t0seq + (tq + 1) * 512],
                       ob[0:64, :], reads=[ob], writes=[OH], tag="gh_ost")

    load_weight("L1_Wo")
    load_cols("L1_bo", s_bc)
    oh_cv = OH[:].rearrange("(c p) t -> p c t", p=128)
    bufs = dict(pp=(p0, p1), p5=p5, p6=p6,
                s_mean=s_mean, s_mr=s_mr, s_rstd=s_rstd,
                s_col1f=s_col1f, s_fones=s_fones, s_biascols=s_bc)

    def feed_pre(tt):
        sl = slice(tt * 512, (tt + 1) * 512)
        xt = s_xt[tt % 2]
        mk.dma("sync", xt[:], oh_cv[:, :, sl], reads=[OH], writes=[xt],
               tag="wo_old")

    def feed(tt, fc, ps):
        xt = s_xt[tt % 2]
        for kc in range(8):
            mk.emit("tensor", lambda t, kc=kc, fc=fc, ps=ps, xt=xt: t.matmul(
                ps[:, 0:512], s_w[:, kc, fc * 128:(fc + 1) * 128], xt[:, kc, :],
                start=(kc == 0), stop=(kc == 7)),
                reads=[s_w, xt], writes=[ps], tag="wo_mm")

    emit_ln_stage(ctx, bufs, feed_pre, feed, XF2, XR3, XF3, XB3, bias_cols=s_bc)


def phase_l1_ffn(ctx):
    nc, mk = ctx["nc"], ctx["mk"]
    W = ctx["W"]
    XF3, XB3, XF4 = ctx["XF3"], ctx["XB3"], ctx["XF4"]
    e = ctx["stack"].enter_context
    eye32, eye16 = ctx["s_eye32"], ctx["s_eye16"]
    constcol = ctx["constcol"]
    CAP = float(B * T // NEXP)

    RLOC = nc.dram_tensor("RLOC", [R, NEXP], F32)
    RGL = nc.dram_tensor("RGL", [B * T, NEXP], F32)
    W1T = nc.dram_tensor("W1T", [NEXP, D, DEXP], BF16)
    W2T = nc.dram_tensor("W2T", [NEXP, DEXP, D], BF16)
    XR4 = nc.dram_tensor("XR4", [D, R], F32)
    SSEL = nc.dram_tensor("SSEL", [R, NEXP], F32)
    ctx["RLOC"] = RLOC
    ctx["RGL"] = RGL
    ctx["SSEL"] = SSEL

    tle = np.zeros((128, 8, 8), np.float32)
    tlt = np.zeros((128, 8, 8), np.float32)
    for j in range(8):
        tle[:, j, j + 1:] = 1.0
        tlt[:, j, :j] = 1.0
    d_tle = nc.inline_tensor(tle.reshape(128, 64), name="tmpl_le")
    d_tlt = nc.inline_tensor(tlt.reshape(128, 64), name="tmpl_lt")
    sel1h = np.zeros((8, 8 * 128), np.float32)
    for ex in range(8):
        sel1h[ex, ex * 128:(ex + 1) * 128] = 1.0
    d_sel1h = nc.inline_tensor(sel1h.astype(ml_dtypes.bfloat16), name="sel1h")

    s_tle = e(nc.sbuf_tensor("m_tle", [128, 8, 8], F32))
    s_tlt = e(nc.sbuf_tensor("m_tlt", [128, 8, 8], F32))
    s_sel1h = e(nc.sbuf_tensor("m_sel1h", [8, 8 * 128], BF16))
    mk.dma("sync", s_tle[:].rearrange("p a b -> p (a b)"), d_tle[:],
           reads=[d_tle], writes=[s_tle])
    mk.dma("sync", s_tlt[:].rearrange("p a b -> p (a b)"), d_tlt[:],
           reads=[d_tlt], writes=[s_tlt])
    mk.dma("sync", s_sel1h[:], d_sel1h[:], reads=[d_sel1h], writes=[s_sel1h])

    s_fones = e(nc.sbuf_tensor("m_ones", [1, 512], F32))
    s_col1f = e(nc.sbuf_tensor("m_col1f", [128, 1], F32))
    mk.emit("vector", lambda v: v.memset(s_fones[:], 1.0), writes=[s_fones], memset=True)
    mk.emit("vector", lambda v: v.memset(s_col1f[:], 1.0), writes=[s_col1f], memset=True)

    s_neg = e(nc.sbuf_tensor("m_neg", [128, 1], F32))
    s_t1 = e(nc.sbuf_tensor("m_t1", [128, 512], F32))
    s_t2 = e(nc.sbuf_tensor("m_t2", [128, 512], F32))
    s_selT = e(nc.sbuf_tensor("m_selT", [8, R], BF16))
    p0 = e(nc.psum_tensor("m_p0", [128, 512], F32))
    p1 = e(nc.psum_tensor("m_p1", [128, 512], F32))
    p4 = e(nc.psum_tensor("m_p4", [128, 256], F32))
    p5 = e(nc.psum_tensor("m_p5", [128, 512], F32))
    p6 = e(nc.psum_tensor("m_p6", [128, 512], F32))
    p4b = p4[:].bitcast(BF16)
    s_w1T = e(nc.sbuf_tensor("m_w1T", [128, 8, 1024], BF16))
    s_w2T = e(nc.sbuf_tensor("m_w2T", [128, 8, 1024], BF16))
    s_wl = s_w1T
    for ex in range(NEXP):
        for (WSRC, DSTT) in ((W["L1_w1"], W1T), (W["L1_w2"], W2T)):
            mk.dma("gpsimd", s_wl[:],
                   WSRC[ex].rearrange("(c p) n -> p c n", p=128),
                   reads=[WSRC], writes=[s_wl], tag="wt_ld")
            for oc in range(8):
                for kc in range(8):
                    pt = (p4, p5)[kc % 2]
                    ptb = pt[:].bitcast(BF16)
                    mk.emit("tensor", lambda t, oc=oc, kc=kc, ptb=ptb: t.transpose(
                        ptb[:, 256:384], s_wl[:, oc, kc * 128:(kc + 1) * 128], eye16[:]),
                        reads=[s_wl, eye16], writes=[pt], tag="wt_T")
                    mk.emit("vector", lambda v, oc=oc, kc=kc, ptb=ptb: v.tensor_copy(
                        s_w2T[:, kc, oc * 128:(oc + 1) * 128], ptb[:, 256:384]),
                        reads=[pt], writes=[s_w2T], tag="wt_e")
            mk.dma("scalar", DSTT[ex].rearrange("(c p) n -> p c n", p=128), s_w2T[:],
                   reads=[s_w2T], writes=[DSTT], tag="wt_st")

    from contextlib import ExitStack as _ES2
    with _ES2() as rsub:
        re_ = rsub.enter_context
        s_gwT = re_(nc.sbuf_tensor("m_gwT", [128, 8, 8], F32))
        s_gw = re_(nc.sbuf_tensor("m_gw", [8, D], F32))
        s_probs = re_(nc.sbuf_tensor("m_probs", [128, 32, 8], F32))
        s_S = re_(nc.sbuf_tensor("m_S", [128, 32, 8], F32))
        s_cnt = re_(nc.sbuf_tensor("m_cnt", [128, 32, 8], F32))
        s_R = re_(nc.sbuf_tensor("m_R", [128, 32, 8], F32))
        s_asg = re_(nc.sbuf_tensor("m_asg", [128, 32, 8], F32))
        s_m1 = re_(nc.sbuf_tensor("m_m1", [128, 32, 8], F32))
        s_m2 = re_(nc.sbuf_tensor("m_m2", [128, 32, 8], F32))


        mk.emit("vector", lambda v: v.memset(s_neg[:], -1e9), writes=[s_neg], memset=True)

        mk.dma("sync", s_gw[:], W["L1_gate_w"][:], reads=[W["L1_gate_w"]], writes=[s_gw])
        for kc in range(8):
            mk.emit("tensor", lambda t, kc=kc: t.transpose(
                p4[0:128, 0:8], s_gw[:, kc * 128:(kc + 1) * 128], eye32[0:8, 0:8]),
                reads=[s_gw, eye32], writes=[p4], tag="gl_T")
            mk.emit("vector", lambda v, kc=kc: v.tensor_copy(s_gwT[:, kc, :], p4[0:128, 0:8]),
                    reads=[p4], writes=[s_gwT], tag="gl_Te")
        xf3_v = XF3[:].rearrange("(c p) t -> p c t", p=128)
        s_xf8 = [re_(nc.sbuf_tensor(f"m_xf8{i}", [128, 8, 128], F32)) for i in range(2)]
        for tile in range(32):
            tsl = slice(tile * 128, (tile + 1) * 128)
            xf8 = s_xf8[tile % 2]
            mk.dma("sync", xf8[:], xf3_v[:, :, tsl], reads=[XF3], writes=[xf8],
                   tag="gl_x")
            for kc in range(8):
                mk.emit("tensor", lambda t, kc=kc, xf8=xf8: t.matmul(
                    p0[0:128, 0:8], xf8[:, kc, :], s_gwT[:, kc, :],
                    start=(kc == 0), stop=(kc == 7)),
                    reads=[xf8, s_gwT], writes=[p0], tag="gl_mm")
            mk.emit("vector", lambda v: v.reduce_max(
                s_t1[:, 0:1], p0[0:128, 0:8], AX.X, negate=True),
                reads=[p0], writes=[s_t1], tag="gl_mx")
            mk.emit("scalar", lambda s: s.activation(
                s_t2[:, 0:8], p0[0:128, 0:8], AF.Exp, bias=s_t1[:, 0:1]),
                reads=[p0, s_t1], writes=[s_t2], tag="gl_ex")
            mk.emit("vector", lambda v: v.reduce_sum(s_t1[:, 0:1], s_t2[:, 0:8], AX.X),
                    reads=[s_t2], writes=[s_t1], tag="gl_sm")
            mk.emit("vector", lambda v: v.reciprocal(s_t1[:, 1:2], s_t1[:, 0:1]),
                    reads=[s_t1], writes=[s_t1], tag="gl_rc")
            mk.emit("vector", lambda v, tile=tile: v.tensor_scalar(
                s_probs[:, tile, :], s_t2[:, 0:8], s_t1[:, 1:2], None, ALU.mult),
                reads=[s_t2, s_t1], writes=[s_probs], tag="gl_p")
        mk.emit("vector", lambda v: v.memset(s_S[:], 0.0), writes=[s_S], memset=True)
        mk.emit("vector", lambda v: v.memset(s_cnt[:], 0.0), writes=[s_cnt], memset=True)
        for j in range(8):
            pj = s_probs[:, :, j:j + 1].broadcast_to((128, 32, 8))
            mk.emit("vector", lambda v, pj=pj: v.tensor_tensor(
                s_m1[:], s_probs[:], pj, ALU.is_le),
                reads=[s_probs], writes=[s_m1], tag="rt_le")
            mk.emit("vector", lambda v, pj=pj: v.tensor_tensor(
                s_m2[:], s_probs[:], pj, ALU.is_lt),
                reads=[s_probs], writes=[s_m2], tag="rt_lt")
            tleb = s_tle[:, j:j + 1, :].broadcast_to((128, 32, 8))
            tltb = s_tlt[:, j:j + 1, :].broadcast_to((128, 32, 8))
            mk.emit("vector", lambda v, tleb=tleb: v.tensor_tensor(
                s_m1[:], s_m1[:], tleb, ALU.mult),
                reads=[s_m1, s_tle], writes=[s_m1], tag="rt_c1")
            mk.emit("vector", lambda g, tltb=tltb: g.tensor_tensor(
                s_m2[:], s_m2[:], tltb, ALU.mult),
                reads=[s_m2, s_tlt], writes=[s_m2], tag="rt_c2")
            mk.emit("vector", lambda v: v.tensor_add(s_m1[:], s_m1[:], s_m2[:]),
                    reads=[s_m1, s_m2], writes=[s_m1], tag="rt_c3")
            mk.emit("vector", lambda v, pj=pj: v.tensor_tensor(
                s_m2[:], s_m1[:], pj, ALU.mult),
                reads=[s_m1, s_probs], writes=[s_m2], tag="rt_sp")
            mk.emit("vector", lambda v: v.tensor_add(s_S[:], s_S[:], s_m2[:]),
                    reads=[s_S, s_m2], writes=[s_S], tag="rt_sa")
            mk.emit("vector", lambda g: g.tensor_add(s_cnt[:], s_cnt[:], s_m1[:]),
                    reads=[s_cnt, s_m1], writes=[s_cnt], tag="rt_ca")
        mk.emit("vector", lambda v: v.tensor_scalar(
            s_asg[:], s_S[:], 0.9, None, ALU.is_lt),
            reads=[s_S], writes=[s_asg], tag="rt_as")
        mk.emit("vector", lambda v: v.tensor_sub(s_R[:], s_probs[:], s_cnt[:]),
                reads=[s_probs, s_cnt], writes=[s_R], tag="rt_r1")
        mk.emit("vector", lambda v: v.tensor_scalar(
            s_R[:], s_R[:], -1.0, None, ALU.add),
            reads=[s_R], writes=[s_R], tag="rt_r2")
        mk.emit("vector", lambda v: v.tensor_scalar(
            s_m2[:], s_asg[:], 1e9, -1e9, ALU.mult, ALU.add),
            reads=[s_asg], writes=[s_m2], tag="rt_r3a")
        mk.emit("vector", lambda v: v.tensor_add(s_R[:], s_R[:], s_m2[:]),
                reads=[s_R, s_m2], writes=[s_R], tag="rt_r3b")
        rloc_v = RLOC[:].rearrange("(k p) e -> p k e", p=128)
        mk.dma("sync", rloc_v, s_R[:], reads=[s_R], writes=[RLOC], tag="rt_st")
        mk.collective("AllGather", ALU.bypass, [RLOC[:]], [RGL[:]],
                      reads=[RLOC], writes=[RGL])

        s_rgl = re_(nc.sbuf_tensor("m_rgl", [128, 256, 8], F32))
        s_lo = re_(nc.sbuf_tensor("m_lo", [1, 8], F32))
        s_hi = re_(nc.sbuf_tensor("m_hi", [1, 8], F32))
        s_mid = re_(nc.sbuf_tensor("m_mid", [1, 8], F32))
        s_cnt8 = re_(nc.sbuf_tensor("m_cnt8", [1, 8], F32))
        s_pred = re_(nc.sbuf_tensor("m_pred", [1, 8], F32))
        s_midc = re_(nc.sbuf_tensor("m_midc", [128, 8], F32))
        s_cmp = re_(nc.sbuf_tensor("m_cmp", [128, 256], F32))
        s_c256 = re_(nc.sbuf_tensor("m_c256", [1, 256], F32))
        rgl_v = RGL[:].rearrange("(k p) e -> p k e", p=128)
        mk.dma("sync", s_rgl[:], rgl_v, reads=[RGL], writes=[s_rgl], tag="bi_ld")
        mk.emit("vector", lambda v: v.memset(s_lo[:], -9.0), writes=[s_lo], memset=True)
        mk.emit("vector", lambda v: v.memset(s_hi[:], 0.0), writes=[s_hi], memset=True)
        for it in range(40):
            mk.emit("vector", lambda v: v.tensor_add(s_mid[:], s_lo[:], s_hi[:]),
                    reads=[s_lo, s_hi], writes=[s_mid], tag="bi_m1")
            mk.emit("vector", lambda v: v.tensor_scalar(
                s_mid[:], s_mid[:], 0.5, None, ALU.mult),
                reads=[s_mid], writes=[s_mid], tag="bi_m2")
            mk.emit("tensor", lambda t: t.matmul(
                p4[0:128, 0:8], s_fones[0:1, 0:128], s_mid[:], start=True, stop=True),
                reads=[s_fones, s_mid], writes=[p4], tag="bi_bc")
            mk.emit("vector", lambda v: v.tensor_copy(s_midc[:], p4[0:128, 0:8]),
                    reads=[p4], writes=[s_midc], tag="bi_bce")
            for ex in range(8):
                mk.emit("vector", lambda v, ex=ex: v.tensor_scalar(
                    s_cmp[:], s_rgl[:, :, ex], s_midc[:, ex:ex + 1], None, ALU.is_ge),
                    reads=[s_rgl, s_midc], writes=[s_cmp], tag="bi_cmp")
                mk.emit("tensor", lambda t: t.matmul(
                    p6[0:1, 0:256], s_col1f[:], s_cmp[:], start=True, stop=True),
                    reads=[s_col1f, s_cmp], writes=[p6], tag="bi_mm")
                mk.emit("vector", lambda v, ex=ex: v.reduce_sum(
                    s_cnt8[0:1, ex:ex + 1], p6[0:1, 0:256], AX.X),
                    reads=[p6], writes=[s_cnt8], tag="bi_rs")
            mk.emit("vector", lambda v: v.tensor_scalar(
                s_pred[:], s_cnt8[:], CAP, None, ALU.is_ge),
                reads=[s_cnt8], writes=[s_pred], tag="bi_pr")
            mk.emit("vector", lambda v: v.tensor_sub(s_cnt8[:], s_mid[:], s_lo[:]),
                    reads=[s_mid, s_lo], writes=[s_cnt8], tag="bi_d1")
            mk.emit("vector", lambda v: v.tensor_mul(s_cnt8[:], s_cnt8[:], s_pred[:]),
                    reads=[s_cnt8, s_pred], writes=[s_cnt8], tag="bi_d2")
            mk.emit("vector", lambda v: v.tensor_add(s_lo[:], s_lo[:], s_cnt8[:]),
                    reads=[s_lo, s_cnt8], writes=[s_lo], tag="bi_lo")
            mk.emit("vector", lambda v: v.tensor_sub(s_cnt8[:], s_hi[:], s_mid[:]),
                    reads=[s_hi, s_mid], writes=[s_cnt8], tag="bi_d3")
            mk.emit("vector", lambda v: v.tensor_mul(s_cnt8[:], s_cnt8[:], s_pred[:]),
                    reads=[s_cnt8, s_pred], writes=[s_cnt8], tag="bi_d4")
            mk.emit("vector", lambda v: v.tensor_add(s_hi[:], s_mid[:], s_cnt8[:]),
                    reads=[s_mid, s_cnt8], writes=[s_hi], tag="bi_hi")
        mk.emit("tensor", lambda t: t.matmul(
            p4[0:128, 0:8], s_fones[0:1, 0:128], s_lo[:], start=True, stop=True),
            reads=[s_fones, s_lo], writes=[p4], tag="kp_bc")
        mk.emit("vector", lambda v: v.tensor_copy(s_midc[:], p4[0:128, 0:8]),
                reads=[p4], writes=[s_midc], tag="kp_bce")
        mk.emit("vector", lambda v: v.tensor_tensor(
            s_m1[:], s_R[:], s_midc[:, 0:8].unsqueeze(1).broadcast_to((128, 32, 8)),
            ALU.is_ge),
            reads=[s_R, s_midc], writes=[s_m1], tag="kp_ge")
        mk.emit("vector", lambda v: v.tensor_mul(s_m1[:], s_m1[:], s_asg[:]),
                reads=[s_m1, s_asg], writes=[s_m1], tag="kp_a")
        mk.emit("vector", lambda v: v.tensor_mul(s_m1[:], s_m1[:], s_probs[:]),
                reads=[s_m1, s_probs], writes=[s_m1], tag="kp_p")
        for tile in range(32):
            mk.emit("tensor", lambda t, tile=tile: t.transpose(
                p4[0:8, 0:128], s_m1[:, tile, :], eye32[:]),
                reads=[s_m1, eye32], writes=[p4], tag="sl_T")
            mk.emit("vector", lambda v, tile=tile: v.tensor_copy(
                s_selT[:, tile * 128:(tile + 1) * 128], p4[0:8, 0:128]),
                reads=[p4], writes=[s_selT], tag="sl_Te")


    mk.barrier()

    s_xb = [e(nc.sbuf_tensor(f"m_xb{i}", [128, 8, 512], BF16)) for i in range(2)]
    s_h = [e(nc.sbuf_tensor(f"m_h{i}", [128, 8, 512], BF16)) for i in range(2)]
    psel = [e(nc.psum_tensor(f"m_psel{i}", [128, 512], F32)) for i in range(2)]
    s_y = e(nc.sbuf_tensor("m_y", [128, 8, R // 2], F32))
    xb3_v = XB3[:].rearrange("(c p) t -> p c t", p=128)
    for half in range(2):
        mk.emit("vector", lambda v: v.memset(s_y[:], 0.0), writes=[s_y], memset=True)
        for ex in range(NEXP):
            mk.dma("gpsimd", s_w1T[:], W1T[ex].rearrange("(c p) n -> p c n", p=128),
                   reads=[W1T], writes=[s_w1T], tag="de_w1")
            mk.dma("scalar", s_w2T[:], W2T[ex].rearrange("(c p) n -> p c n", p=128),
                   reads=[W2T], writes=[s_w2T], tag="de_w2")
            for tt in range(4):
                gtt = half * 4 + tt
                sl = slice(gtt * 512, (gtt + 1) * 512)
                lsl = slice(tt * 512, (tt + 1) * 512)
                xb = s_xb[tt % 2]
                sh = s_h[tt % 2]
                pse = psel[tt % 2]
                tg = s_t1 if tt % 2 == 0 else s_t2
                mk.dma("sync", xb[:], xb3_v[:, :, sl], reads=[XB3], writes=[xb],
                       tag="de_x")
                mk.emit("tensor", lambda t, ex=ex, sl=sl, pse=pse: t.matmul(
                    pse[:, 0:512], s_sel1h[:, ex * 128:(ex + 1) * 128], s_selT[:, sl],
                    start=True, stop=True),
                    reads=[s_sel1h, s_selT], writes=[pse], tag="de_sb")
                for fc in range(8):
                    ps = (p0, p1)[fc % 2]
                    for kc in range(8):
                        mk.emit("tensor", lambda t, kc=kc, fc=fc, ps=ps, xb=xb: t.matmul(
                            ps[:, 0:512], s_w1T[:, kc, fc * 128:(fc + 1) * 128],
                            xb[:, kc, :], start=(kc == 0), stop=(kc == 7)),
                            reads=[s_w1T, xb], writes=[ps], tag="de_h")
                    mk.emit("scalar", lambda s, ps=ps, tg=tg: s.activation(
                        tg[:], ps[:, 0:512], AF.Gelu),
                        reads=[ps], writes=[tg], tag="de_g")
                    mk.emit("vector", lambda v, fc=fc, sh=sh, pse=pse, tg=tg:
                            v.tensor_tensor(
                                sh[:, fc, :], tg[:], pse[:, 0:512], ALU.mult),
                            reads=[tg, pse], writes=[sh], tag="de_hs")
                for fc in range(8):
                    ps = (p0, p1)[fc % 2]
                    for kc in range(8):
                        mk.emit("tensor", lambda t, kc=kc, fc=fc, ps=ps, sh=sh: t.matmul(
                            ps[:, 0:512], s_w2T[:, kc, fc * 128:(fc + 1) * 128],
                            sh[:, kc, :], start=(kc == 0), stop=(kc == 7)),
                            reads=[s_w2T, sh], writes=[ps], tag="de_y")
                    mk.emit("vector", lambda v, fc=fc, lsl=lsl, ps=ps: v.tensor_add(
                        s_y[:, fc, lsl], s_y[:, fc, lsl], ps[:, 0:512]),
                        reads=[s_y, ps], writes=[s_y], tag="de_ya")
        bufs = dict(pp=(p0, p1), p5=p5, p6=p6, e=e, cache_key="moe",
                    s_mean=e(nc.sbuf_tensor(f"m_me{half}", [1, 512], F32)),
                    s_mr=e(nc.sbuf_tensor(f"m_mr{half}", [1, 512], F32)),
                    s_rstd=e(nc.sbuf_tensor(f"m_rs{half}", [1, 512], F32)),
                    s_col1f=s_col1f, s_fones=s_fones)

        def feed(tt, fc, ps, half=half):
            lsl = slice((tt - half * 4) * 512, (tt - half * 4 + 1) * 512)
            return (s_y[:, fc, lsl], s_y)

        emit_ln_stage(ctx, bufs, None, feed, XF3, XR4, XF4, None,
                      tt_range=range(half * 4, (half + 1) * 4))


def shard_inputs(inputs):
    x = f32c(inputs["L0_attn_hidden_states"]).reshape(B * T, D)
    wmap = {
        "L0_Wq": "L0_attn_Wq", "L0_Wk": "L0_attn_Wk", "L0_Wv": "L0_attn_Wv",
        "L0_Wg": "L0_attn_Wg", "L0_Wo": "L0_attn_Wo",
        "L0_sk_w1": "L0_attn_sk_w1", "L0_sk_w2": "L0_attn_sk_w2",
        "L0_sk_b2": "L0_attn_sk_b2", "L0_sv_w1": "L0_attn_sv_w1",
        "L0_sv_w2": "L0_attn_sv_w2", "L0_sv_b2": "L0_attn_sv_b2",
        "L0_gnorm_w": "L0_attn_gnorm_w",
        "L0_w_gate": "L0_ffn_w_gate", "L0_w_up": "L0_ffn_w_up",
        "L0_w_down": "L0_ffn_w_down",
        "L1_Wz": "L1_attn_Wz", "L1_bz": "L1_attn_bz",
        "L1_ln_g": "L1_attn_ln_g", "L1_ln_b": "L1_attn_ln_b",
        "L1_Wq": "L1_attn_Wq", "L1_bq": "L1_attn_bq",
        "L1_Wr": "L1_attn_Wr", "L1_br": "L1_attn_br",
        "L1_Wh": "L1_attn_Wh", "L1_bh": "L1_attn_bh",
        "L1_Wphi": "L1_attn_Wphi", "L1_bphi": "L1_attn_bphi",
        "L1_Wo": "L1_attn_Wo", "L1_bo": "L1_attn_bo",
        "L1_gate_w": "L1_ffn_gate_w", "L1_w1": "L1_ffn_w1", "L1_w2": "L1_ffn_w2",
    }
    weights = {k: f32c(inputs[v]) for k, v in wmap.items()}
    in_maps = []
    for c in range(NCORES):
        m = dict(weights)
        m["x0"] = x[c * R:(c + 1) * R]
        in_maps.append(m)
    return in_maps


def kernel(**inputs):
    assert inputs["L1_attn_key_in"].shape[-1] == D
    assert inputs["L1_attn_value"].shape[-1] == D
    in_maps = shard_inputs(inputs)
    nc = build_program()
    res = run_bass_kernel_spmd(nc, in_maps, core_ids=list(range(NCORES)))
    out = np.concatenate([np.asarray(r["y"]) for r in res.results], axis=0)
    return out.reshape(B, T, D)


# version 09 (final)
# speedup vs baseline: 1.3737x; 1.3737x over previous
import numpy as np
import ml_dtypes
import concourse.bass as bass
import concourse.mybir as mybir
from concourse.bass_utils import run_bass_kernel_spmd

F32 = mybir.dt.float32
BF16 = mybir.dt.bfloat16
I32 = mybir.dt.int32
AF = mybir.ActivationFunctionType
ALU = mybir.AluOpType
AX = mybir.AxisListType

NCORES = 8
B, T, D = 16, 2048, 1024
SEQ_PER_CORE = B // NCORES
R = SEQ_PER_CORE * T
DFF = 3072
NEXP = 8
DEXP = 1024
H0, M0, DK0 = 4, 64, 256
H1, DH1 = 16, 64
CHUNK = 128

COMPUTE_ENGINES = ("tensor", "vector", "scalar", "gpsimd")
ALL_ENGINES = ("tensor", "vector", "scalar", "gpsimd", "sync")


class Rec:
    __slots__ = ("idx", "eng", "fn", "reads", "writes", "inc", "is_dma",
                 "is_memset", "edges", "planned_inc", "cum", "tag")

    def __init__(self, idx, eng, fn, reads, writes, inc, is_dma, is_memset, tag):
        self.idx = idx
        self.eng = eng
        self.fn = fn
        self.reads = reads
        self.writes = writes
        self.inc = inc
        self.is_dma = is_dma
        self.is_memset = is_memset
        self.edges = set()
        self.planned_inc = False
        self.cum = 0
        self.tag = tag


class MK:

    def __init__(self, nc):
        self.nc = nc
        self.recs = []
        self.last_writer = {}
        self.readers = {}
        self.collective_ctr = 0
        self._const_tiles = {}
        self._stack = None
        self.barriers = []

    def barrier(self):
        self.barriers.append(len(self.recs))

    @staticmethod
    def key(handle):
        return id(handle)

    def emit(self, eng, fn, reads=(), writes=(), dma=False, memset=False, tag=""):
        idx = len(self.recs)
        rec = Rec(idx, eng, fn, [self.key(b) for b in reads],
                  [self.key(b) for b in writes], 16 if dma else 1, dma, memset, tag)
        for k in rec.reads:
            lw = self.last_writer.get(k)
            if lw is not None and self._needs_edge(self.recs[lw], rec):
                rec.edges.add(lw)
        for k in rec.writes:
            lw = self.last_writer.get(k)
            if lw is not None and self._needs_edge(self.recs[lw], rec, war=True):
                rec.edges.add(lw)
            for rd in self.readers.get(k, ()):
                if rd != idx and self._needs_edge(self.recs[rd], rec, war=True):
                    rec.edges.add(rd)
        for k in rec.reads:
            self.readers.setdefault(k, []).append(idx)
        for k in rec.writes:
            self.last_writer[k] = idx
            self.readers[k] = []
        self.recs.append(rec)
        return rec

    @staticmethod
    def _needs_edge(src, dst, war=False):
        if src.eng != dst.eng:
            return True
        if src.is_dma or dst.is_dma:
            return True
        if src.eng == "tensor":
            return False
        if war:
            return True
        if src.is_memset:
            return True
        return False

    def dma(self, eng, out, in_, reads, writes, tag=""):
        self.emit(eng, lambda e, out=out, in_=in_: e.dma_start(out=out, in_=in_),
                  reads=reads, writes=writes, dma=True, tag=tag)

    def build_block(self):
        recs = self.recs
        for r in recs:
            if r.is_dma:
                r.planned_inc = True
        for r in recs:
            for e in r.edges:
                recs[e].planned_inc = True
        for b in self.barriers:
            seen = set()
            for i in range(b - 1, -1, -1):
                r = recs[i]
                if r.eng not in seen:
                    seen.add(r.eng)
                    if r.inc > 0:
                        r.planned_inc = True
                if len(seen) == len(ALL_ENGINES):
                    break
        cums = {e: 0 for e in ALL_ENGINES}
        barrier_snap = {}
        bi = 0
        for r in recs:
            while bi < len(self.barriers) and self.barriers[bi] <= r.idx:
                barrier_snap[self.barriers[bi]] = dict(cums)
                bi += 1
            if r.planned_inc:
                cums[r.eng] += r.inc
            r.cum = cums[r.eng]
        while bi < len(self.barriers):
            barrier_snap[self.barriers[bi]] = dict(cums)
            bi += 1
        dma_cum_so_far = {e: 0 for e in ALL_ENGINES}
        dma_issued_before = []
        for r in recs:
            dma_issued_before.append(dict(dma_cum_so_far))
            if r.is_dma and r.planned_inc:
                dma_cum_so_far[r.eng] = r.cum
        waits = []
        bi = 0
        cur_barrier = None
        for r in recs:
            while bi < len(self.barriers) and self.barriers[bi] <= r.idx:
                cur_barrier = barrier_snap[self.barriers[bi]]
                bi += 1
            w = {}
            if cur_barrier is not None:
                for e2, v2 in cur_barrier.items():
                    if v2 > 0:
                        w[e2] = v2
            for e in r.edges:
                p = recs[e]
                if p.is_dma:
                    v = dma_issued_before[r.idx][p.eng]
                    assert v >= p.cum
                else:
                    v = p.cum
                if w.get(p.eng, 0) < v:
                    w[p.eng] = v
            waits.append(w)
        self._waits = waits

        sems = {}
        block = None

        nc = self.nc
        from contextlib import ExitStack
        stack = ExitStack()
        self._stack = stack
        for e in ALL_ENGINES:
            sems[e] = stack.enter_context(nc.semaphore(f"sem_{e}"))
        self.sems = sems
        block = stack.enter_context(nc.Block())
        total = {e: cums[e] for e in ALL_ENGINES}

        def run_engine(obj, name):
            last_waited = {e: 0 for e in ALL_ENGINES}
            for r, w in zip(recs, waits):
                if r.eng != name:
                    continue
                for e, v in sorted(w.items()):
                    if v > last_waited[e]:
                        obj.wait_ge(sems[e], v)
                        last_waited[e] = v
                inst = r.fn(obj)
                if r.planned_inc and r.inc > 0:
                    if r.is_dma or r.eng == "tensor":
                        inst.then_inc(sems[r.eng], r.inc)
                    else:
                        obj.drain(fusable=True).then_inc(sems[r.eng], r.inc)
            for e in ALL_ENGINES:
                if total[e] > 0 and total[e] > last_waited[e]:
                    obj.wait_ge(sems[e], total[e])

        @block.gpsimd
        def _(g):
            run_engine(g, "gpsimd")

        @block.vector
        def _(v):
            run_engine(v, "vector")

        @block.scalar
        def _(s):
            run_engine(s, "scalar")

        @block.tensor
        def _(t):
            run_engine(t, "tensor")

        @block.sync
        def _(s):
            run_engine(s, "sync")

        stack.close()

    def collective(self, kind, op, ins_aps, outs_aps, reads, writes):
        nc = self.nc
        self.collective_ctr += 1
        name = f"csem{self.collective_ctr}"
        sem = nc.alloc_semaphore(name)
        core_ids = list(range(NCORES))

        def fn(g, kind=kind, op=op, ins_aps=ins_aps, outs_aps=outs_aps, sem=sem):
            inst = g.collective_compute(kind, op, replica_groups=[core_ids],
                                        ins=ins_aps, outs=outs_aps)
            inst.then_inc(sem, 1)
            return _FakeInst()

        cidx = len(self.recs)
        rec = self.emit("gpsimd", fn, reads=reads, writes=(), tag="coll")
        rec.inc = 0
        for k in rec.reads:
            lst = self.readers.get(k)
            if lst and cidx in lst:
                lst.remove(cidx)

        def wfn(g, sem=sem):
            g.wait_ge(sem, 1)
            return g.nop()

        self.emit("gpsimd", wfn, reads=reads, writes=writes, tag="collwait")


class _FakeInst:
    def then_inc(self, sem, v):
        return self


def f32c(x):
    return np.ascontiguousarray(x, dtype=np.float32)



def build_program(phases=("tin", "l0attn", "l0ffn", "l1attn", "l1ffn", "tout"),
                  stream_io=False, debug_out=(), inject=()):
    nc = bass.Bass("TRN2", target_bir_lowering=False)
    mk = MK(nc)
    ctx = {}
    ctx["nc"] = nc
    ctx["mk"] = mk

    def ein(name, shape):
        return nc.dram_tensor(name, shape, F32, kind="ExternalInput")

    X0 = ein("x0", [R, D])
    W = {}
    for name, shape in [
        ("L0_Wq", [D, D]), ("L0_Wk", [D, D]), ("L0_Wv", [D, D]),
        ("L0_Wg", [D, D]), ("L0_Wo", [D, D]),
        ("L0_sk_w1", [D, 16]), ("L0_sk_w2", [16, 256]), ("L0_sk_b2", [256]),
        ("L0_sv_w1", [D, 16]), ("L0_sv_w2", [16, 256]), ("L0_sv_b2", [256]),
        ("L0_gnorm_w", [256]),
        ("L0_w_gate", [DFF, D]), ("L0_w_up", [DFF, D]), ("L0_w_down", [D, DFF]),
        ("L1_Wz", [D, D]), ("L1_bz", [D]), ("L1_ln_g", [D]), ("L1_ln_b", [D]),
        ("L1_Wq", [D, D]), ("L1_bq", [D]), ("L1_Wr", [D, D]), ("L1_br", [D]),
        ("L1_Wh", [D, D]), ("L1_bh", [D]),
        ("L1_Wphi", [DH1, DH1]), ("L1_bphi", [DH1]),
        ("L1_Wo", [D, D]), ("L1_bo", [D]),
        ("L1_gate_w", [NEXP, D]), ("L1_w1", [NEXP, DEXP, D]), ("L1_w2", [NEXP, D, DEXP]),
    ]:
        W[name] = ein(name, shape)
    ctx["W"] = W
    ctx["X0"] = X0

    def internal(name, shape, dt=F32):
        return nc.dram_tensor(name, shape, dt)

    ctx["XF0"] = internal("XF0", [D, R])
    ctx["XB0"] = internal("XB0", [D, R], BF16)
    ctx["XF1"] = internal("XF1", [D, R])
    ctx["XB1"] = internal("XB1", [D, R], BF16)
    ctx["XF2"] = internal("XF2", [D, R])
    ctx["XB2"] = internal("XB2", [D, R], BF16)
    ctx["XF3"] = internal("XF3", [D, R])
    ctx["XB3"] = internal("XB3", [D, R], BF16)
    ctx["XF4"] = internal("XF4", [D, R])

    YOUT = nc.dram_tensor("y", [R, D], F32, kind="ExternalOutput")
    ctx["YOUT"] = YOUT

    ctx["eyef32"] = nc.inline_tensor(np.eye(128, dtype=np.float32), name="eyef32")
    ctx["eyebf16"] = nc.inline_tensor(
        np.eye(128, dtype=np.float32).astype(ml_dtypes.bfloat16), name="eyebf16")

    from contextlib import ExitStack
    with ExitStack() as stack:
        ctx["stack"] = stack
        e = stack.enter_context

        ctx["s_eye32"] = e(nc.sbuf_tensor("s_eye32", [128, 128], F32))
        ctx["s_eye16"] = e(nc.sbuf_tensor("s_eye16", [128, 128], BF16))
        ctx["s_ones"] = e(nc.sbuf_tensor("s_ones", [128, 8], BF16))
        ctx["s_ones32"] = e(nc.sbuf_tensor("s_ones32", [128, 8], F32))
        ctx["s_const"] = e(nc.sbuf_tensor("s_const", [128, 16], F32))

        mk.dma("sync", ctx["s_eye32"][:], ctx["eyef32"][:],
               reads=[ctx["eyef32"]], writes=[ctx["s_eye32"]], tag="ld_eye32")
        mk.dma("sync", ctx["s_eye16"][:], ctx["eyebf16"][:],
               reads=[ctx["eyebf16"]], writes=[ctx["s_eye16"]], tag="ld_eye16")
        mk.emit("vector", lambda v: v.memset(ctx["s_ones"][:], 1.0),
                writes=[ctx["s_ones"]], memset=True)
        mk.emit("vector", lambda v: v.memset(ctx["s_ones32"][:], 1.0),
                writes=[ctx["s_ones32"]], memset=True)

        const_vals = {}
        def constcol(val):
            if val in const_vals:
                return const_vals[val]
            i = len(const_vals)
            assert i < 16
            ap = ctx["s_const"][:, i:i + 1]
            mk.emit("vector", lambda v, ap=ap, val=val: v.memset(ap, float(val)),
                    writes=[ctx["s_const"]], memset=True)
            const_vals[val] = ap
            return ap
        ctx["constcol"] = constcol

        for name in inject:
            dst = ctx[name]
            shp = [int(s) for s in dst.ap().shape]
            dt = dst.ap().dtype
            src = nc.dram_tensor("in_" + name, shp, dt, kind="ExternalInput")
            mk.dma("sync", dst[:], src[:], reads=[src], writes=[dst],
                   tag="inject_" + name)

        main_stack = stack

        def run_phase(fn, *a):
            from contextlib import ExitStack as _ES
            with _ES() as ps:
                ctx["stack"] = ps
                fn(ctx, *a)
            ctx["stack"] = main_stack
            mk.barrier()

        if "tin" in phases:
            run_phase(phase_transpose_in)
        if "l0attn" in phases:
            run_phase(phase_l0_attn)
        if "l0ffn" in phases:
            run_phase(phase_l0_ffn)
        if "l1attn" in phases:
            run_phase(phase_l1_attn)
        if "l1ffn" in phases:
            run_phase(phase_l1_ffn)
        if "tout" in phases:
            if "l1ffn" in phases:
                tsrc = "XF4"
            elif "l1attn" in phases:
                tsrc = "XF3"
            else:
                tsrc = "XF0"
            run_phase(phase_transpose_out, tsrc)

        for name in debug_out:
            src = ctx[name]
            shp = [int(s) for s in src.ap().shape]
            dt = src.ap().dtype
            od = nc.dram_tensor("dbg_" + name, shp, dt, kind="ExternalOutput")
            mk.dma("sync", od[:], src[:], reads=[src], writes=[od], tag="dbg")

        mk.build_block()
    return nc



def phase_transpose_in(ctx):
    nc, mk = ctx["nc"], ctx["mk"]
    X0, XF0, XB0 = ctx["X0"], ctx["XF0"], ctx["XB0"]
    e = ctx["stack"].enter_context
    s_in = [e(nc.sbuf_tensor(f"ti_in{i}", [128, D], F32)) for i in range(2)]
    s_fm = [e(nc.sbuf_tensor(f"ti_fm{i}", [128, 8 * 128], F32)) for i in range(2)]
    s_fmb = [e(nc.sbuf_tensor(f"ti_fmb{i}", [128, 8 * 128], BF16)) for i in range(2)]
    p_tr = [e(nc.psum_tensor(f"ti_ps{i}", [128, 128], F32)) for i in range(2)]
    eye = ctx["s_eye32"]

    ntile = R // 128
    for it in range(ntile):
        bi = it % 2
        sin, sfm, sfmb = s_in[bi], s_fm[bi], s_fmb[bi]
        mk.dma("sync", sin[:], X0[it * 128:(it + 1) * 128, :],
               reads=[X0], writes=[sin], tag=f"tin_ld{it}")
        for c in range(8):
            ps = p_tr[c % 2]
            mk.emit("tensor",
                    lambda t, sin=sin, ps=ps, c=c: t.transpose(
                        ps[:], sin[:, c * 128:(c + 1) * 128], eye[:]),
                    reads=[sin, eye], writes=[ps], tag=f"tin_tr{it}_{c}")
            mk.emit("vector",
                    lambda v, ps=ps, sfm=sfm, c=c: v.tensor_copy(
                        sfm[:, c * 128:(c + 1) * 128], ps[:]),
                    reads=[ps], writes=[sfm], tag=f"tin_cp{it}_{c}")
        mk.emit("scalar", lambda s, sfm=sfm, sfmb=sfmb: s.copy(sfmb[:], sfm[:]),
                reads=[sfm], writes=[sfmb], tag=f"tin_cast{it}")
        dst32 = XF0[:].rearrange("(c p) t -> p c t", p=128)[:, :, it * 128:(it + 1) * 128]
        dst16 = XB0[:].rearrange("(c p) t -> p c t", p=128)[:, :, it * 128:(it + 1) * 128]
        src = s_fm[bi][:].rearrange("p (c t) -> p c t", c=8)
        srcb = s_fmb[bi][:].rearrange("p (c t) -> p c t", c=8)
        mk.dma("sync", dst32, src, reads=[sfm], writes=[XF0], tag=f"tin_st32_{it}")
        mk.dma("scalar", dst16, srcb, reads=[sfmb], writes=[XB0], tag=f"tin_st16_{it}")



def phase_transpose_out(ctx, src="XF3"):
    nc, mk = ctx["nc"], ctx["mk"]
    XF, YOUT = ctx[src], ctx["YOUT"]
    e = ctx["stack"].enter_context
    s_fm = [e(nc.sbuf_tensor(f"to_fm{i}", [128, 8 * 128], F32)) for i in range(2)]
    s_tm = [e(nc.sbuf_tensor(f"to_tm{i}", [128, D], F32)) for i in range(2)]
    p_tr = [e(nc.psum_tensor(f"to_ps{i}", [128, 128], F32)) for i in range(2)]
    eye = ctx["s_eye32"]

    ntile = R // 128
    for it in range(ntile):
        bi = it % 2
        sfm, stm = s_fm[bi], s_tm[bi]
        srcv = XF[:].rearrange("(c p) t -> p c t", p=128)[:, :, it * 128:(it + 1) * 128]
        mk.dma("sync", sfm[:].rearrange("p (c t) -> p c t", c=8), srcv,
               reads=[XF], writes=[sfm], tag=f"to_ld{it}")
        for c in range(8):
            ps = p_tr[c % 2]
            mk.emit("tensor",
                    lambda t, sfm=sfm, ps=ps, c=c: t.transpose(
                        ps[:], sfm[:, c * 128:(c + 1) * 128], eye[:]),
                    reads=[sfm, eye], writes=[ps], tag=f"to_tr{it}_{c}")
            mk.emit("vector",
                    lambda v, ps=ps, stm=stm, c=c: v.tensor_copy(
                        stm[:, c * 128:(c + 1) * 128], ps[:]),
                    reads=[ps], writes=[stm], tag=f"to_cp{it}_{c}")
        mk.dma("sync", YOUT[it * 128:(it + 1) * 128, :], stm[:],
               reads=[stm], writes=[YOUT], tag=f"to_st{it}")



def phase_l0_attn(ctx):
    nc, mk = ctx["nc"], ctx["mk"]
    W = ctx["W"]
    XB0, XF0, XF1, XB1 = ctx["XB0"], ctx["XF0"], ctx["XF1"], ctx["XB1"]
    e = ctx["stack"].enter_context
    eye32, eye16 = ctx["s_eye32"], ctx["s_eye16"]
    constcol = ctx["constcol"]

    inv_freq = (1.0 / (np.float32(10000.0) ** (np.arange(128, dtype=np.float32)
                                               / np.float32(128.0)))).astype(np.float32)
    ang = (np.arange(T, dtype=np.float32)[None, :] * inv_freq[:, None]).astype(np.float32)
    d_cos = nc.inline_tensor(np.cos(ang).astype(np.float32), name="rope_cos")
    d_sin = nc.inline_tensor(np.sin(ang).astype(np.float32), name="rope_sin")
    ut = np.triu(np.ones((128, 128), dtype=np.float32))
    d_ut32 = nc.inline_tensor(ut, name="ut32")
    d_ut16 = nc.inline_tensor(ut.astype(ml_dtypes.bfloat16), name="ut16")

    s_cos = e(nc.sbuf_tensor("a_cos", [128, T], F32))
    s_sin = e(nc.sbuf_tensor("a_sin", [128, T], F32))
    s_ut32 = e(nc.sbuf_tensor("a_ut32", [128, 128], F32))
    s_ut16 = e(nc.sbuf_tensor("a_ut16", [128, 128], BF16))
    s_row1 = e(nc.sbuf_tensor("a_row1", [1, 512], BF16))
    s_row1f = e(nc.sbuf_tensor("a_row1f", [1, 512], F32))
    s_col1 = e(nc.sbuf_tensor("a_col1", [128, 1], BF16))
    s_col1f = e(nc.sbuf_tensor("a_col1f", [128, 1], F32))
    mk.dma("sync", s_cos[:], d_cos[:], reads=[d_cos], writes=[s_cos])
    mk.dma("sync", s_sin[:], d_sin[:], reads=[d_sin], writes=[s_sin])
    mk.dma("sync", s_ut32[:], d_ut32[:], reads=[d_ut32], writes=[s_ut32])
    mk.dma("sync", s_ut16[:], d_ut16[:], reads=[d_ut16], writes=[s_ut16])
    for (buf, val) in ((s_row1, 1.0), (s_row1f, 1.0), (s_col1, 1.0), (s_col1f, 1.0)):
        mk.emit("vector", lambda v, buf=buf, val=val: v.memset(buf[:], val),
                writes=[buf], memset=True)

    p0 = e(nc.psum_tensor("a_p0", [128, 512], F32))
    p1 = e(nc.psum_tensor("a_p1", [128, 512], F32))
    p2 = e(nc.psum_tensor("a_p2", [128, 512], F32))
    p3 = e(nc.psum_tensor("a_p3", [1, 512], F32))
    p4 = e(nc.psum_tensor("a_p4", [128, 256], F32))
    p5 = e(nc.psum_tensor("a_p5", [128, 512], F32))
    p6 = e(nc.psum_tensor("a_p6", [128, 512], F32))
    p7 = e(nc.psum_tensor("a_p7", [128, 512], F32))

    ps_small = p4

    s_skw1 = e(nc.sbuf_tensor("a_skw1", [128, 8, 16], BF16))
    s_svw1 = e(nc.sbuf_tensor("a_svw1", [128, 8, 16], BF16))
    s_skw2 = e(nc.sbuf_tensor("a_skw2", [16, 256], BF16))
    s_svw2 = e(nc.sbuf_tensor("a_svw2", [16, 256], BF16))
    s_skb2 = e(nc.sbuf_tensor("a_skb2", [1, 256], F32))
    s_svb2 = e(nc.sbuf_tensor("a_svb2", [1, 256], F32))
    s_v2x128 = e(nc.sbuf_tensor("a_v2x128", [2, 128], F32))
    s_gnc = e(nc.sbuf_tensor("a_gnc", [128, 2], F32))

    mk.dma("gpsimd", s_skw1[:], W["L0_sk_w1"][:].rearrange("(c p) l -> p c l", p=128),
           reads=[W["L0_sk_w1"]], writes=[s_skw1])
    mk.dma("gpsimd", s_svw1[:], W["L0_sv_w1"][:].rearrange("(c p) l -> p c l", p=128),
           reads=[W["L0_sv_w1"]], writes=[s_svw1])
    mk.dma("gpsimd", s_skw2[:], W["L0_sk_w2"][:], reads=[W["L0_sk_w2"]], writes=[s_skw2])
    mk.dma("gpsimd", s_svw2[:], W["L0_sv_w2"][:], reads=[W["L0_sv_w2"]], writes=[s_svw2])
    mk.dma("sync", s_skb2[:], W["L0_sk_b2"][:].rearrange("(a b) -> a b", a=1),
           reads=[W["L0_sk_b2"]], writes=[s_skb2])
    mk.dma("sync", s_svb2[:], W["L0_sv_b2"][:].rearrange("(a b) -> a b", a=1),
           reads=[W["L0_sv_b2"]], writes=[s_svb2])

    mk.dma("sync", s_v2x128[:], W["L0_gnorm_w"][:].rearrange("(a b) -> a b", a=2),
           reads=[W["L0_gnorm_w"]], writes=[s_v2x128])
    mk.emit("tensor", lambda t: t.transpose(ps_small[0:128, 0:2], s_v2x128[:], eye32[0:2, 0:2]),
            reads=[s_v2x128, eye32], writes=[ps_small])
    mk.emit("vector", lambda v: v.tensor_copy(s_gnc[:], ps_small[0:128, 0:2]),
            reads=[ps_small], writes=[s_gnc])

    s_wq = e(nc.sbuf_tensor("a_wq", [128, 8, 256], BF16))
    s_wk = e(nc.sbuf_tensor("a_wk", [128, 8, 256], BF16))
    s_wv = e(nc.sbuf_tensor("a_wv", [128, 8, 256], BF16))
    s_wg = e(nc.sbuf_tensor("a_wg", [128, 8, 256], BF16))
    s_wo = e(nc.sbuf_tensor("a_wo", [128, 8, 1024], BF16))
    mk.dma("gpsimd", s_wo[:], W["L0_Wo"][:].rearrange("(c p) n -> p c n", p=128),
           reads=[W["L0_Wo"]], writes=[s_wo])

    s_xb = [e(nc.sbuf_tensor(f"a_xb{i}", [128, 8, 512], BF16)) for i in range(2)]
    s_qu = e(nc.sbuf_tensor("a_qu", [16, T], BF16))
    s_vu = e(nc.sbuf_tensor("a_vu", [16, T], BF16))
    s_wkt = e(nc.sbuf_tensor("a_wkt", [128, 16, 256], BF16))
    s_wvt = e(nc.sbuf_tensor("a_wvt", [128, 16, 256], BF16))
    s_wvf = e(nc.sbuf_tensor("a_wvf", [64, 4, T], BF16))
    s_q = e(nc.sbuf_tensor("a_q", [128, 2, T], BF16))
    s_k = e(nc.sbuf_tensor("a_k", [128, 2, T], BF16))
    s_v = e(nc.sbuf_tensor("a_v", [128, 16, 256], BF16))
    s_g = e(nc.sbuf_tensor("a_g", [128, 2, T], BF16))
    s_o = e(nc.sbuf_tensor("a_o", [128, 2, T], BF16))
    s_ogst = e(nc.sbuf_tensor("a_ogst", [128, 512], BF16))
    s_t1 = e(nc.sbuf_tensor("a_t1", [128, 512], F32))
    s_t2 = e(nc.sbuf_tensor("a_t2", [128, 512], F32))
    s_t3 = e(nc.sbuf_tensor("a_t3", [128, 512], F32))
    s_t4 = e(nc.sbuf_tensor("a_t4", [128, 512], F32))

    s_S = e(nc.sbuf_tensor("a_S", [128, 128], BF16))
    s_rzk = e(nc.sbuf_tensor("a_rzk", [128, 64], F32))
    s_lnt = e(nc.sbuf_tensor("a_lnt", [128, 64], F32))
    s_rzv = e(nc.sbuf_tensor("a_rzv", [128, 64], F32))
    s_log = e(nc.sbuf_tensor("a_log", [128, 64], F32))
    s_nmax = e(nc.sbuf_tensor("a_nmax", [128, 1], F32))
    s_exp = e(nc.sbuf_tensor("a_exp", [128, 64], F32))
    s_sum = e(nc.sbuf_tensor("a_sum", [128, 1], F32))
    s_rrec = e(nc.sbuf_tensor("a_rrec", [128, 1], F32))
    s_pz0 = e(nc.sbuf_tensor("a_pz0", [128, 64], F32))
    s_pz = e(nc.sbuf_tensor("a_pz", [128, 64], F32))
    s_pzT = e(nc.sbuf_tensor("a_pzT", [64, 128], BF16))
    s_ktm = e(nc.sbuf_tensor("a_ktm", [128, 256], BF16))
    s_A = e(nc.sbuf_tensor("a_A", [128, 128], BF16))
    s_Kst = e(nc.sbuf_tensor("a_Kst", [128, 2, 64], F32))
    s_Kstb = [e(nc.sbuf_tensor(f"a_Kstb{i}", [128, 2, 64], BF16)) for i in range(2)]
    s_Vst = e(nc.sbuf_tensor("a_Vst", [64, 256], F32))
    s_Vstb = [e(nc.sbuf_tensor(f"a_Vstb{i}", [64, 256], BF16)) for i in range(2)]
    s_cyf = e(nc.sbuf_tensor("a_cyf", [1, 128], F32))
    s_cyb = [e(nc.sbuf_tensor(f"a_cyb{i}", [1, 128], BF16)) for i in range(2)]

    s_rms = e(nc.sbuf_tensor("a_rms", [1, 512], F32))
    s_mean = e(nc.sbuf_tensor("a_mean", [1, 512], F32))
    s_mr = e(nc.sbuf_tensor("a_mr", [1, 512], F32))
    s_rstd = e(nc.sbuf_tensor("a_rstd", [1, 512], F32))
    s_xr = e(nc.sbuf_tensor("a_xr", [128, 512], F32))
    s_x1b = e(nc.sbuf_tensor("a_x1b", [128, 512], BF16))
    s_x1b2 = e(nc.sbuf_tensor("a_x1b2", [128, 512], BF16))
    s_res8 = e(nc.sbuf_tensor("a_res8", [128, 8, 512], F32))
    s_amax = e(nc.sbuf_tensor("a_amax", [128, 1], F32))
    s_amax1 = e(nc.sbuf_tensor("a_amax1", [1, 1], F32))
    AMAXL = nc.dram_tensor("AMAXL", [1, 1], F32)
    XR = nc.dram_tensor("XR", [D, R], F32)
    OG = nc.dram_tensor("OG", [D, R], BF16)
    ctx["OG"] = OG
    ctx["XR"] = XR
    ctx["AMAXL"] = AMAXL

    mk.emit("vector", lambda v: v.memset(s_amax[:], 0.0), writes=[s_amax], memset=True)

    scale0 = float(DK0) ** -0.5
    xb_view = XB0[:].rearrange("(c p) t -> p c t", p=128)
    xf_view = XF0[:].rearrange("(c p) t -> p c t", p=128)
    xr_view = XR[:].rearrange("(c p) t -> p c t", p=128)
    og_view = OG[:].rearrange("(c p) t -> p c t", p=128)
    xf1_view = XF1[:].rearrange("(c p) t -> p c t", p=128)
    xb1_view = XB1[:].rearrange("(c p) t -> p c t", p=128)

    def load_head_weights(h):
        for (wname, dst) in (("L0_Wq", s_wq), ("L0_Wk", s_wk), ("L0_Wv", s_wv),
                             ("L0_Wg", s_wg)):
            src = W[wname][:].rearrange("(c p) n -> p c n", p=128)[:, :, h * 256:(h + 1) * 256]
            mk.dma("gpsimd", dst[:], src, reads=[W[wname]], writes=[dst],
                   tag=f"ldw{wname}{h}")

    for seq in range(SEQ_PER_CORE):
        t0seq = seq * T

        for tt in range(4):
            xb = s_xb[tt % 2]
            sl = slice(t0seq + tt * 512, t0seq + (tt + 1) * 512)
            mk.dma("sync", xb[:], xb_view[:, :, sl], reads=[XB0], writes=[xb])
            for (w1, u) in ((s_skw1, s_qu), (s_svw1, s_vu)):
                for kc in range(8):
                    mk.emit("tensor", lambda t, w1=w1, xb=xb, kc=kc: t.matmul(
                        p0[0:16, 0:512], w1[:, kc, :], xb[:, kc, :],
                        start=(kc == 0), stop=(kc == 7)),
                        reads=[w1, xb], writes=[p0], tag="sk_u")
                mk.emit("vector", lambda v, u=u, tt=tt: v.tensor_copy(
                    u[:, tt * 512:(tt + 1) * 512], p0[0:16, 0:512]),
                    reads=[p0], writes=[u], tag="sk_uev")
        for tile in range(16):
            tsl = slice(tile * 128, (tile + 1) * 128)
            for (u, binj, dst) in ((s_qu, s_skb2, s_wkt), (s_vu, s_svb2, s_wvt)):
                w2 = s_skw2 if dst is s_wkt else s_svw2
                mk.emit("tensor", lambda t, u=u, w2=w2, tsl=tsl: t.matmul(
                    p1[:, 0:256], u[:, tsl], w2[:], start=True, stop=False),
                    reads=[u, w2], writes=[p1], tag="wk_mm")
                mk.emit("tensor", lambda t, binj=binj: t.matmul(
                    p1[:, 0:256], s_row1f[0:1, 0:128], binj[:],
                    start=False, stop=True, skip_group_check=True),
                    reads=[s_row1f, binj], writes=[p1], tag="wk_b")
                mk.emit("vector", lambda v: v.tensor_scalar(
                    s_t1[:, 0:256], p1[:, 0:256], 32.0, -32.0, ALU.min, ALU.max),
                    reads=[p1], writes=[s_t1], tag="wk_clip")
                mk.emit("scalar", lambda s, dst=dst, tile=tile: s.activation(
                    dst[:, tile, :], s_t1[:, 0:256], AF.Exp),
                    reads=[s_t1], writes=[dst], tag="wk_exp")
        for hh in range(4):
            for tt in range(4):
                tsl = slice(tt * 512, (tt + 1) * 512)
                mk.emit("tensor", lambda t, hh=hh, tsl=tsl: t.matmul(
                    p1[0:64, 0:512], s_svw2[:, hh * 64:(hh + 1) * 64], s_vu[:, tsl],
                    start=True, stop=False),
                    reads=[s_svw2, s_vu], writes=[p1], tag="wvf_mm")
                mk.emit("tensor", lambda t, hh=hh: t.matmul(
                    p1[0:64, 0:512], s_svb2[0:1, hh * 64:(hh + 1) * 64], s_row1f[0:1, :],
                    start=False, stop=True, skip_group_check=True),
                    reads=[s_svb2, s_row1f], writes=[p1], tag="wvf_b")
                mk.emit("vector", lambda v: v.tensor_scalar(
                    s_t1[0:64, 0:512], p1[0:64, 0:512], 32.0, -32.0, ALU.min, ALU.max),
                    reads=[p1], writes=[s_t1], tag="wvf_clip")
                mk.emit("scalar", lambda s, hh=hh, tsl=tsl: s.activation(
                    s_wvf[:, hh, tsl], s_t1[0:64, 0:512], AF.Exp),
                    reads=[s_t1], writes=[s_wvf], tag="wvf_exp")

        for h in range(H0):
            load_head_weights(h)
            msl = slice(h * 64, (h + 1) * 64)
            wvf_l = s_wvf[0:64, h, :]

            for tt in range(4):
                xb = s_xb[tt % 2]
                sl = slice(t0seq + tt * 512, t0seq + (tt + 1) * 512)
                lsl = slice(tt * 512, (tt + 1) * 512)
                mk.dma("sync", xb[:], xb_view[:, :, sl], reads=[XB0], writes=[xb])
                for (wsl, dst) in ((s_wq, s_q), (s_wk, s_k)):
                    for mc in range(2):
                        pp = (p0, p1)[mc]
                        for kc in range(8):
                            mk.emit("tensor", lambda t, wsl=wsl, xb=xb, kc=kc, mc=mc, pp=pp:
                                    t.matmul(pp[:, 0:512], wsl[:, kc, mc * 128:(mc + 1) * 128],
                                             xb[:, kc, :], start=(kc == 0), stop=(kc == 7)),
                                    reads=[wsl, xb], writes=[pp], tag="qk_mm")
                    csl = s_cos[:, lsl]
                    ssl = s_sin[:, lsl]
                    mk.emit("vector", lambda v, csl=csl: v.tensor_tensor(
                        s_t1[:], p0[:, 0:512], csl, ALU.mult),
                        reads=[p0, s_cos], writes=[s_t1], tag="rope1")
                    mk.emit("vector", lambda v, ssl=ssl: v.tensor_tensor(
                        s_t2[:], p1[:, 0:512], ssl, ALU.mult),
                        reads=[p1, s_sin], writes=[s_t2], tag="rope2")
                    mk.emit("vector", lambda v, csl=csl: v.tensor_tensor(
                        s_t3[:], p1[:, 0:512], csl, ALU.mult),
                        reads=[p1, s_cos], writes=[s_t3], tag="rope3")
                    mk.emit("vector", lambda v, ssl=ssl: v.tensor_tensor(
                        s_t4[:], p0[:, 0:512], ssl, ALU.mult),
                        reads=[p0, s_sin], writes=[s_t4], tag="rope4")
                    mk.emit("vector", lambda g, dst=dst, lsl=lsl: g.tensor_sub(
                        dst[:, 0, lsl], s_t1[:], s_t2[:]),
                        reads=[s_t1, s_t2], writes=[dst], tag="rope5")
                    mk.emit("vector", lambda g, dst=dst, lsl=lsl: g.tensor_add(
                        dst[:, 1, lsl], s_t3[:], s_t4[:]),
                        reads=[s_t3, s_t4], writes=[dst], tag="rope6")
                for vt in range(4):
                    gt = tt * 4 + vt
                    for kc in range(8):
                        mk.emit("tensor", lambda t, xb=xb, kc=kc, vt=vt: t.matmul(
                            p7[:, 0:256], xb[:, kc, vt * 128:(vt + 1) * 128],
                            s_wv[:, kc, :], start=(kc == 0), stop=(kc == 7)),
                            reads=[xb, s_wv], writes=[p7], tag="v_mm")
                    mk.emit("vector", lambda v, gt=gt: v.tensor_copy(
                        s_v[:, gt, :], p7[:, 0:256]),
                        reads=[p7], writes=[s_v], tag="v_ev")

            for buf in (s_Kst, s_Kstb[0], s_Kstb[1], s_Vst, s_Vstb[0], s_Vstb[1],
                        s_cyf, s_cyb[0], s_cyb[1]):
                mk.emit("vector", lambda v, buf=buf: v.memset(buf[:], 0.0),
                        writes=[buf], memset=True)

            for c in range(16):
                tsl = slice(c * 128, (c + 1) * 128)
                gt0 = seq
                for mc in range(2):
                    mk.emit("tensor", lambda t, mc=mc, tsl=tsl: t.matmul(
                        p0[:, 0:128], s_k[:, mc, tsl], s_q[:, mc, tsl],
                        start=(mc == 0), stop=(mc == 1)),
                        reads=[s_k, s_q], writes=[p0], tag="scanS")
                mk.emit("vector", lambda v: v.scalar_tensor_tensor(
                    s_S[:], p0[:, 0:128], 1.0, s_ut32[:], ALU.mult, ALU.mult),
                    reads=[p0, s_ut32], writes=[s_S], tag="scanSev")
                for (wtm, cof, dst0) in ((s_wkt, 0, 64), (s_wvt, 64, 128)):
                    mk.emit("tensor", lambda t, wtm=wtm, dst0=dst0, c=c: t.matmul(
                        p1[:, dst0:dst0 + 64], s_ut16[:], wtm[:, c, msl],
                        start=True, stop=False),
                        reads=[s_ut16, wtm], writes=[p1], tag="scanZ")
                    cyp = s_cyb[(c + 1) % 2]
                    mk.emit("tensor", lambda t, cof=cof, dst0=dst0, cyp=cyp: t.matmul(
                        p1[:, dst0:dst0 + 64], s_row1[0:1, 0:128],
                        cyp[0:1, cof:cof + 64], start=False, stop=True,
                        skip_group_check=True),
                        reads=[s_row1, cyp], writes=[p1], tag="scanZc")
                _m1 = constcol(-1.0)[0:128, :]
                mk.emit("scalar", lambda s: s.activation(
                    s_lnt[:], p1[:, 64:128], AF.Ln),
                    reads=[p1], writes=[s_lnt], tag="lzk")
                mk.emit("scalar", lambda s, _m1=_m1: s.activation(
                    s_rzk[:], s_lnt[:], AF.Exp, scale=_m1),
                    reads=[s_lnt, ctx["s_const"]], writes=[s_rzk], tag="rzk")
                mk.emit("scalar", lambda s: s.activation(
                    s_lnt[:], p1[:, 128:192], AF.Ln),
                    reads=[p1], writes=[s_lnt], tag="lzv")
                mk.emit("scalar", lambda s, _m1=_m1: s.activation(
                    s_rzv[:], s_lnt[:], AF.Exp, scale=_m1),
                    reads=[s_lnt, ctx["s_const"]], writes=[s_rzv], tag="rzv")
                mk.emit("tensor", lambda t, c=c: t.matmul(
                    p3[0:1, 0:64], s_col1[:], s_wkt[:, c, msl],
                    start=(c == 0), stop=False, skip_group_check=True),
                    reads=[s_col1, s_wkt], writes=[p3], tag="cyk")
                mk.emit("tensor", lambda t, c=c: t.matmul(
                    p3[0:1, 64:128], s_col1[:], s_wvt[:, c, msl],
                    start=False, stop=False, skip_group_check=True),
                    reads=[s_col1, s_wvt], writes=[p3], tag="cyv")
                cyc = s_cyb[c % 2]
                mk.emit("vector", lambda v: v.tensor_copy(s_cyf[:], p3[0:1, 0:128]),
                        reads=[p3], writes=[s_cyf], tag="cysnapf")
                mk.emit("vector", lambda v, cyc=cyc: v.tensor_copy(cyc[:], s_cyf[:]),
                        reads=[s_cyf], writes=[cyc], tag="cysnapb")
                p4b = p4[:].bitcast(BF16)
                for mc in range(2):
                    mk.emit("tensor", lambda t, mc=mc, tsl=tsl, p4b=p4b: t.transpose(
                        p4b[:, 256 + mc * 128:256 + (mc + 1) * 128],
                        s_k[:, mc, tsl], eye16[:]),
                        reads=[s_k, eye16], writes=[p4], tag="ktmT")
                mk.emit("vector", lambda v, p4b=p4b: v.tensor_copy(
                    s_ktm[:], p4b[:, 256:512]),
                    reads=[p4], writes=[s_ktm], tag="ktmEv")
                for dt in range(2):
                    mk.emit("tensor", lambda t, dt=dt, c=c: t.matmul(
                        p5[:, dt * 64:(dt + 1) * 64],
                        s_ktm[:, dt * 128:(dt + 1) * 128], s_wkt[:, c, msl],
                        start=True, stop=True),
                        reads=[s_ktm, s_wkt], writes=[p5], tag="kstU")
                mk.emit("tensor", lambda t, c=c: t.matmul(
                    p5[0:64, 128:384], s_wvt[:, c, msl], s_v[:, c, :],
                    start=True, stop=True),
                    reads=[s_wvt, s_v], writes=[p5], tag="vstU")
                mk.emit("vector", lambda v: v.tensor_add(
                    s_Kst[:].rearrange("p d m -> p (d m)"),
                    s_Kst[:].rearrange("p d m -> p (d m)"), p5[:, 0:128]),
                    reads=[p5, s_Kst], writes=[s_Kst], tag="kstA")
                mk.emit("vector", lambda v: v.tensor_add(
                    s_Vst[:], s_Vst[:], p5[0:64, 128:384]),
                    reads=[p5, s_Vst], writes=[s_Vst], tag="vstA")
                kbc = s_Kstb[c % 2]
                vbc = s_Vstb[c % 2]
                mk.emit("vector", lambda v, kbc=kbc: v.tensor_copy(
                    kbc[:].rearrange("p d m -> p (d m)"),
                    s_Kst[:].rearrange("p d m -> p (d m)")),
                    reads=[s_Kst], writes=[kbc], tag="kstC")
                mk.emit("vector", lambda v, vbc=vbc: v.tensor_copy(vbc[:], s_Vst[:]),
                        reads=[s_Vst], writes=[vbc], tag="vstC")

                mk.emit("tensor", lambda t, c=c: t.matmul(
                    p1[:, 0:64], s_S[:], s_wkt[:, c, msl], start=True, stop=False),
                    reads=[s_S, s_wkt], writes=[p1], tag="logI")
                kbp = s_Kstb[(c + 1) % 2]
                for mc in range(2):
                    mk.emit("tensor", lambda t, mc=mc, tsl=tsl, kbp=kbp: t.matmul(
                        p1[:, 0:64], s_q[:, mc, tsl], kbp[:, mc, :],
                        start=False, stop=(mc == 1), skip_group_check=True),
                        reads=[s_q, kbp], writes=[p1], tag="logX")
                mk.emit("vector", lambda v: v.scalar_tensor_tensor(
                    s_log[:], p1[:, 0:64], scale0, s_rzk[:], ALU.mult, ALU.mult),
                    reads=[p1, s_rzk], writes=[s_log], tag="logEv")
                mk.emit("vector", lambda v: v.reduce_max(
                    s_nmax[:], s_log[:], AX.X, negate=True),
                    reads=[s_log], writes=[s_nmax], tag="smax")
                mk.emit("scalar", lambda s: s.activation(
                    s_exp[:], s_log[:], AF.Exp, bias=s_nmax[:]),
                    reads=[s_log, s_nmax], writes=[s_exp], tag="sexp")
                mk.emit("vector", lambda v: v.reduce_sum(s_sum[:], s_exp[:], AX.X),
                        reads=[s_exp], writes=[s_sum], tag="ssum")
                mk.emit("vector", lambda v: v.reciprocal(s_rrec[:], s_sum[:]),
                        reads=[s_sum], writes=[s_rrec], tag="srec")
                mk.emit("vector", lambda v: v.tensor_scalar(
                    s_pz0[:], s_exp[:], s_rrec[:], None, ALU.mult),
                    reads=[s_exp, s_rrec], writes=[s_pz0], tag="spz0")
                mk.emit("vector", lambda v: v.tensor_tensor(
                    s_pz[:], s_pz0[:], s_rzv[:], ALU.mult),
                    reads=[s_pz0, s_rzv], writes=[s_pz], tag="spz")
                mk.emit("tensor", lambda t: t.transpose(
                    p4[0:64, 0:128], s_pz[:], eye32[:]),
                    reads=[s_pz, eye32], writes=[p4], tag="pzT")
                mk.emit("vector", lambda v: v.tensor_copy(s_pzT[:], p4[0:64, 0:128]),
                        reads=[p4], writes=[s_pzT], tag="pzTev")
                mk.emit("tensor", lambda t, wvf_l=wvf_l, tsl=tsl: t.matmul(
                    p2[:, 0:128], wvf_l[:, tsl], s_pzT[:], start=True, stop=True),
                    reads=[s_wvf, s_pzT], writes=[p2], tag="scanA")
                mk.emit("vector", lambda v: v.scalar_tensor_tensor(
                    s_A[:], p2[:, 0:128], 1.0, s_ut32[:], ALU.mult, ALU.mult),
                    reads=[p2, s_ut32], writes=[s_A], tag="scanAev")
                for dt in range(2):
                    mk.emit("tensor", lambda t, dt=dt, c=c: t.matmul(
                        p2[:, 128 + dt * 128:128 + (dt + 1) * 128],
                        s_v[:, c, dt * 128:(dt + 1) * 128], s_A[:],
                        start=True, stop=False),
                        reads=[s_v, s_A], writes=[p2], tag="oIntra")
                    vbp = s_Vstb[(c + 1) % 2]
                    mk.emit("tensor", lambda t, dt=dt, vbp=vbp: t.matmul(
                        p2[:, 128 + dt * 128:128 + (dt + 1) * 128],
                        vbp[:, dt * 128:(dt + 1) * 128], s_pzT[:],
                        start=False, stop=True, skip_group_check=True),
                        reads=[vbp, s_pzT], writes=[p2], tag="oInter")
                mk.emit("vector", lambda v, tsl=tsl: v.tensor_copy(
                    s_o[:, :, tsl], p2[:, 128:384].rearrange("p (d t) -> p d t", d=2)),
                    reads=[p2], writes=[s_o], tag="oEv")
                if ctx.get("dbg_chunk") == (seq, h, c):
                    for nm, buf in (("DCS", s_S), ("DCLOG", s_log), ("DCNM", s_nmax),
                                    ("DCEXP", s_exp), ("DCSUM", s_sum), ("DCRREC", s_rrec),
                                    ("DCPZ", s_pz), ("DCPZT", s_pzT), ("DCA", s_A),
                                    ("DCRZK", s_rzk), ("DCRZV", s_rzv), ("DCKTM", s_ktm)):
                        shp = [int(sz) for sz in buf[:].shape]
                        dd = nc.dram_tensor(nm, shp, buf[:].dtype)
                        ctx[nm] = dd
                        mk.dma("sync", dd[:], buf[:], reads=[buf], writes=[dd], tag="dcdump")
            for tt in range(4):
                xb = s_xb[tt % 2]
                sl = slice(t0seq + tt * 512, t0seq + (tt + 1) * 512)
                lsl = slice(tt * 512, (tt + 1) * 512)
                mk.dma("sync", xb[:], xb_view[:, :, sl], reads=[XB0], writes=[xb])
                for mc in range(2):
                    for kc in range(8):
                        mk.emit("tensor", lambda t, xb=xb, kc=kc, mc=mc: t.matmul(
                            p0[:, 0:512], s_wg[:, kc, mc * 128:(mc + 1) * 128],
                            xb[:, kc, :], start=(kc == 0), stop=(kc == 7)),
                            reads=[s_wg, xb], writes=[p0], tag="g_mm")
                    mk.emit("vector", lambda v, mc=mc, lsl=lsl: v.tensor_copy(
                        s_g[:, mc, lsl], p0[:, 0:512]),
                        reads=[p0], writes=[s_g], tag="g_ev")

            for tt in range(4):
                lsl = slice(tt * 512, (tt + 1) * 512)
                for mc in range(2):
                    mk.emit("scalar", lambda s, mc=mc, lsl=lsl: s.activation(
                        s_t1[:], s_o[:, mc, lsl], AF.Square),
                        reads=[s_o], writes=[s_t1], tag="rmsSq")
                    mk.emit("tensor", lambda t, mc=mc: t.matmul(
                        p6[0:1, 0:512], s_col1f[:], s_t1[:],
                        start=(mc == 0), stop=(mc == 1)),
                        reads=[s_col1f, s_t1], writes=[p6], tag="rmsMM")
                _b = constcol(1e-5)[0:1, :]
                _s = constcol(1.0 / 256.0)[0:1, :]
                mk.emit("scalar", lambda s, _b=_b, _s=_s: s.activation(
                    s_rstd[0:1, :], p6[0:1, 0:512], AF.Sqrt, bias=_b, scale=_s),
                    reads=[p6, ctx["s_const"]], writes=[s_rstd], tag="rmsSB")
                mk.emit("vector", lambda v: v.reciprocal(
                    s_rms[0:1, :], s_rstd[0:1, :]),
                    reads=[s_rstd], writes=[s_rms], tag="rmsRec")
                mk.emit("tensor", lambda t: t.matmul(
                    p6[:, 0:512], s_row1f[0:1, 0:128], s_rms[0:1, :],
                    start=True, stop=True),
                    reads=[s_row1f, s_rms], writes=[p6], tag="rmsBc")
                for mc in range(2):
                    mk.emit("scalar", lambda s, mc=mc, lsl=lsl: s.activation(
                        s_t1[:], s_g[:, mc, lsl], AF.Silu),
                        reads=[s_g], writes=[s_t1], tag="gSilu")
                    mk.emit("vector", lambda v, mc=mc: v.tensor_scalar(
                        s_t2[:], s_t1[:], s_gnc[:, mc:mc + 1], None, ALU.mult),
                        reads=[s_t1, s_gnc], writes=[s_t2], tag="gGn")
                    mk.emit("vector", lambda v, mc=mc, lsl=lsl: v.tensor_tensor(
                        s_t3[:], s_o[:, mc, lsl], p6[:, 0:512], ALU.mult),
                        reads=[s_o, p6], writes=[s_t3], tag="gOn")
                    mk.emit("vector", lambda v: v.tensor_tensor(
                        s_ogst[:], s_t3[:], s_t2[:], ALU.mult),
                        reads=[s_t3, s_t2], writes=[s_ogst], tag="gOg")
                    gsl = slice(t0seq + tt * 512, t0seq + (tt + 1) * 512)
                    mk.dma("sync", og_view[:, h * 2 + mc, gsl], s_ogst[:],
                           reads=[s_ogst], writes=[OG], tag="gOgSt")

            if ctx.get("dbg_head") == (seq, h):
                for nm, buf in (("DBGQ", s_q), ("DBGK", s_k), ("DBGV", s_v),
                                ("DBGO", s_o), ("DBGG", s_g), ("DBGWKT", s_wkt),
                                ("DBGWVT", s_wvt), ("DBGWVF", s_wvf),
                                ("DBGKST", s_Kst), ("DBGVST", s_Vst)):
                    shp = [int(s) for s in buf[:].shape]
                    dt = buf[:].dtype
                    dd = nc.dram_tensor(nm, shp, dt)
                    ctx[nm] = dd
                    mk.dma("sync", dd[:], buf[:], reads=[buf], writes=[dd],
                           tag="dbgdump")

        for tt in range(4):
            sl = slice(t0seq + tt * 512, t0seq + (tt + 1) * 512)
            lsl = slice(tt * 512, (tt + 1) * 512)
            ogb = s_xb[tt % 2]
            mk.dma("sync", ogb[:], og_view[:, :, sl], reads=[OG], writes=[ogb],
                   tag="wo_ogld")
            for fc in range(8):
                ps = (p0, p1)[fc % 2]
                xr = (s_xr, s_t4)[fc % 2]
                for kc in range(8):
                    mk.emit("tensor", lambda t, fc=fc, kc=kc, ogb=ogb, ps=ps: t.matmul(
                        ps[:, 0:512], s_wo[:, kc, fc * 128:(fc + 1) * 128],
                        ogb[:, kc, :], start=(kc == 0), stop=(kc == 7)),
                        reads=[s_wo, ogb], writes=[ps], tag="wo_mm")
                mk.dma("scalar", xr[:], xf_view[:, fc, sl], reads=[XF0],
                       writes=[xr], tag="wo_xf")
                mk.emit("vector", lambda v, fc=fc, ps=ps, xr=xr: v.tensor_add(
                    s_res8[:, fc, :], ps[:, 0:512], xr[:]),
                    reads=[ps, xr], writes=[s_res8], tag="wo_res")
            for fc in range(8):
                t2 = (s_t1, s_t2)[fc % 2]
                mk.emit("scalar", lambda s, fc=fc, t2=t2: s.activation(
                    t2[:], s_res8[:, fc, :], AF.Square),
                    reads=[s_res8], writes=[t2], tag="lnSq")
                mk.emit("tensor", lambda t, fc=fc: t.matmul(
                    p6[0:1, 0:512], s_col1f[:], s_res8[:, fc, :],
                    start=(fc == 0), stop=(fc == 7), skip_group_check=True),
                    reads=[s_col1f, s_res8], writes=[p6], tag="lnS1")
                mk.emit("tensor", lambda t, fc=fc, t2=t2: t.matmul(
                    p5[0:1, 0:512], s_col1f[:], t2[:],
                    start=(fc == 0), stop=(fc == 7), skip_group_check=True),
                    reads=[s_col1f, t2], writes=[p5], tag="lnS2")
            mk.emit("vector", lambda v: v.tensor_scalar(
                s_mean[0:1, :], p6[0:1, 0:512], 1.0 / D, None, ALU.mult),
                reads=[p6], writes=[s_mean], tag="lnMean")
            mk.emit("scalar", lambda s: s.activation(s_mr[0:1, :], s_mean[0:1, :],
                                                     AF.Square),
                    reads=[s_mean], writes=[s_mr], tag="lnMs")
            mk.emit("vector", lambda v: v.scalar_tensor_tensor(
                s_rstd[0:1, :], p5[0:1, 0:512], 1.0 / D, s_mr[0:1, :],
                ALU.mult, ALU.subtract),
                reads=[p5, s_mr], writes=[s_rstd], tag="lnVar")
            _b2 = constcol(1e-5)[0:1, :]
            mk.emit("scalar", lambda s, _b2=_b2: s.activation(
                s_rstd[0:1, :], s_rstd[0:1, :], AF.Sqrt, bias=_b2),
                reads=[s_rstd, ctx["s_const"]], writes=[s_rstd], tag="lnSqrt")
            mk.emit("vector", lambda v: v.reciprocal(s_rstd[0:1, :], s_rstd[0:1, :]),
                    reads=[s_rstd], writes=[s_rstd], tag="lnRec")
            mk.emit("vector", lambda v: v.tensor_mul(
                s_mr[0:1, :], s_mean[0:1, :], s_rstd[0:1, :]),
                reads=[s_mean, s_rstd], writes=[s_mr], tag="lnMr")
            mk.emit("tensor", lambda t: t.matmul(
                p6[:, 0:512], s_row1f[0:1, 0:128], s_rstd[0:1, 0:512],
                start=True, stop=True),
                reads=[s_row1f, s_rstd], writes=[p6], tag="lnBc1")
            mk.emit("tensor", lambda t: t.matmul(
                p5[:, 0:512], s_row1f[0:1, 0:128], s_mr[0:1, 0:512],
                start=True, stop=True),
                reads=[s_row1f, s_mr], writes=[p5], tag="lnBc2")
            for fc in range(8):
                t1 = (s_t1, s_t2)[fc % 2]
                xb16 = (s_x1b, s_x1b2)[fc % 2]
                mk.emit("vector", lambda v, fc=fc, t1=t1: v.tensor_tensor(
                    t1[:], s_res8[:, fc, :], p6[:, 0:512], ALU.mult),
                    reads=[s_res8, p6], writes=[t1], tag="ln_ap1")
                mk.emit("vector", lambda v, t1=t1: v.tensor_tensor(
                    t1[:], t1[:], p5[:, 0:512], ALU.subtract),
                    reads=[t1, p5], writes=[t1], tag="ln_ap2")
                mk.dma("sync", xf1_view[:, fc, sl], t1[:], reads=[t1],
                       writes=[XF1], tag="ln_st32")
                mk.emit("scalar", lambda s, t1=t1, xb16=xb16: s.copy(xb16[:], t1[:]),
                        reads=[t1], writes=[xb16], tag="ln_cp16")
                mk.dma("scalar", xb1_view[:, fc, sl], xb16[:],
                       reads=[xb16], writes=[XB1], tag="ln_st16")
                mk.emit("vector", lambda v, t1=t1: v.reduce_max(
                    s_t3[:, 0:1], t1[:], AX.X, apply_absolute_value=True),
                    reads=[t1], writes=[s_t3], tag="amax1")
                mk.emit("vector", lambda v: v.tensor_max(
                    s_amax[:], s_amax[:], s_t3[:, 0:1]),
                    reads=[s_t3, s_amax], writes=[s_amax], tag="amax2")

    mk.emit("tensor", lambda t: t.transpose(ps_small[0:1, 0:128], s_amax[:], eye32[:]),
            reads=[s_amax, eye32], writes=[ps_small], tag="amaxT")
    mk.emit("vector", lambda v: v.reduce_max(
        s_amax1[:], ps_small[0:1, 0:128], AX.X),
        reads=[ps_small], writes=[s_amax1], tag="amaxR")
    mk.dma("sync", AMAXL[:], s_amax1[:], reads=[s_amax1], writes=[AMAXL], tag="amaxSt")


def emit_ln_stage(ctx, bufs, feed_pre, feed, XFres, XRs, XFout, XBout,
                  amax_buf=None, bias_cols=None, tt_range=None):
    nc, mk = ctx["nc"], ctx["mk"]
    constcol = ctx["constcol"]
    e = bufs.get("e") or ctx["stack"].enter_context
    pp, p5, p6 = bufs["pp"], bufs["p5"], bufs["p6"]
    s_mean, s_mr, s_rstd = bufs["s_mean"], bufs["s_mr"], bufs["s_rstd"]
    s_col1f, s_fones = bufs["s_col1f"], bufs["s_fones"]
    ck = bufs.get("cache_key")
    cache = ctx.setdefault("_ln_cache", {})
    if ck is not None and ck in cache:
        s_res, s_xr, s_t1, s_t2, s_x1b = cache[ck]
    else:
        ln_uid = ctx.get("_ln_uid", 0)
        ctx["_ln_uid"] = ln_uid + 1
        s_res = e(nc.sbuf_tensor(f"lnres{ln_uid}", [128, 8, 512], F32))
        s_xr = [e(nc.sbuf_tensor(f"lnxr{ln_uid}_{i}", [128, 512], F32)) for i in range(2)]
        s_t1 = [e(nc.sbuf_tensor(f"lnt1{ln_uid}_{i}", [128, 512], F32)) for i in range(2)]
        s_t2 = [e(nc.sbuf_tensor(f"lnt2{ln_uid}_{i}", [128, 512], F32)) for i in range(2)]
        s_x1b = [e(nc.sbuf_tensor(f"lnxb{ln_uid}_{i}", [128, 512], BF16)) for i in range(2)]
        if ck is not None:
            cache[ck] = (s_res, s_xr, s_t1, s_t2, s_x1b)
    xfres_v = XFres[:].rearrange("(c p) t -> p c t", p=128)
    xfo_v = XFout[:].rearrange("(c p) t -> p c t", p=128)
    if XBout is not None:
        xbo_v = XBout[:].rearrange("(c p) t -> p c t", p=128)
    _beps = constcol(1e-5)[0:1, :]

    for tt in (tt_range if tt_range is not None else range(R // 512)):
        sl = slice(tt * 512, (tt + 1) * 512)
        if feed_pre is not None:
            feed_pre(tt)
        for fc in range(8):
            ps = pp[fc % 2]
            yap = feed(tt, fc, ps)
            if yap is not None:
                psap, psbuf = yap
            else:
                psap, psbuf = ps[:, 0:512], ps
            xr = s_xr[fc % 2]
            mk.dma("scalar", xr[:], xfres_v[:, fc, sl], reads=[XFres],
                   writes=[xr], tag="lnh_xf")
            if bias_cols is not None:
                bc = bias_cols[:, fc:fc + 1]
                mk.emit("vector", lambda v, psap=psap, bc=bc, xr=xr, fc=fc:
                        v.scalar_tensor_tensor(
                            s_res[:, fc, :], psap, bc, xr[:], ALU.add, ALU.add),
                        reads=[psbuf, xr, bufs["s_biascols"]], writes=[s_res],
                        tag="lnh_res")
            else:
                mk.emit("vector", lambda v, psap=psap, xr=xr, fc=fc: v.tensor_add(
                    s_res[:, fc, :], psap, xr[:]),
                    reads=[psbuf, xr], writes=[s_res], tag="lnh_res")
        for fc in range(8):
            t2 = s_t2[fc % 2]
            mk.emit("scalar", lambda s, fc=fc, t2=t2: s.activation(
                t2[:], s_res[:, fc, :], AF.Square),
                reads=[s_res], writes=[t2], tag="lnh_sq")
            mk.emit("tensor", lambda t, fc=fc: t.matmul(
                p6[0:1, 0:512], s_col1f[:], s_res[:, fc, :],
                start=(fc == 0), stop=(fc == 7), skip_group_check=True),
                reads=[s_col1f, s_res], writes=[p6], tag="lnh_S1")
            mk.emit("tensor", lambda t, fc=fc, t2=t2: t.matmul(
                p5[0:1, 0:512], s_col1f[:], t2[:],
                start=(fc == 0), stop=(fc == 7), skip_group_check=True),
                reads=[s_col1f, t2], writes=[p5], tag="lnh_S2")
        mk.emit("vector", lambda v: v.tensor_scalar(
            s_mean[0:1, :], p6[0:1, 0:512], 1.0 / D, None, ALU.mult),
            reads=[p6], writes=[s_mean], tag="lnh_mean")
        mk.emit("scalar", lambda s: s.activation(s_mr[0:1, :], s_mean[0:1, :],
                                                 AF.Square),
                reads=[s_mean], writes=[s_mr], tag="lnh_ms")
        mk.emit("vector", lambda v: v.scalar_tensor_tensor(
            s_rstd[0:1, :], p5[0:1, 0:512], 1.0 / D, s_mr[0:1, :],
            ALU.mult, ALU.subtract),
            reads=[p5, s_mr], writes=[s_rstd], tag="lnh_var")
        mk.emit("scalar", lambda s, _beps=_beps: s.activation(
            s_rstd[0:1, :], s_rstd[0:1, :], AF.Sqrt, bias=_beps),
            reads=[s_rstd, ctx["s_const"]], writes=[s_rstd], tag="lnh_sqrt")
        mk.emit("vector", lambda v: v.reciprocal(s_rstd[0:1, :], s_rstd[0:1, :]),
                reads=[s_rstd], writes=[s_rstd], tag="lnh_rec")
        mk.emit("vector", lambda v: v.tensor_mul(
            s_mr[0:1, :], s_mean[0:1, :], s_rstd[0:1, :]),
            reads=[s_mean, s_rstd], writes=[s_mr], tag="lnh_mr")
        mk.emit("tensor", lambda t: t.matmul(
            p6[:, 0:512], s_fones[0:1, 0:128], s_rstd[0:1, 0:512],
            start=True, stop=True),
            reads=[s_fones, s_rstd], writes=[p6], tag="lnh_bc1")
        mk.emit("tensor", lambda t: t.matmul(
            p5[:, 0:512], s_fones[0:1, 0:128], s_mr[0:1, 0:512],
            start=True, stop=True),
            reads=[s_fones, s_mr], writes=[p5], tag="lnh_bc2")
        for fc in range(8):
            t1 = s_t1[fc % 2]
            t2 = s_t2[fc % 2]
            xb = s_x1b[fc % 2]
            mk.emit("vector", lambda v, fc=fc, t1=t1: v.tensor_tensor(
                t1[:], s_res[:, fc, :], p6[:, 0:512], ALU.mult),
                reads=[s_res, p6], writes=[t1], tag="lnh_ap1")
            mk.emit("vector", lambda v, t1=t1, t2=t2: v.tensor_tensor(
                t2[:], t1[:], p5[:, 0:512], ALU.subtract),
                reads=[t1, p5], writes=[t2], tag="lnh_ap2")
            mk.dma("sync", xfo_v[:, fc, sl], t2[:], reads=[t2], writes=[XFout],
                   tag="lnh_st32")
            if XBout is not None:
                mk.emit("scalar", lambda s, t2=t2, xb=xb: s.copy(xb[:], t2[:]),
                        reads=[t2], writes=[xb], tag="lnh_cp16")
                mk.dma("scalar", xbo_v[:, fc, sl], xb[:], reads=[xb],
                       writes=[XBout], tag="lnh_st16")
            if amax_buf is not None:
                mk.emit("vector", lambda v, t1=t1, t2=t2: v.reduce_max(
                    t1[:, 0:1], t2[:], AX.X, apply_absolute_value=True),
                    reads=[t2], writes=[t1], tag="lnh_am1")
                mk.emit("vector", lambda v, t1=t1: v.tensor_max(
                    amax_buf[:], amax_buf[:], t1[:, 0:1]),
                    reads=[t1, amax_buf], writes=[amax_buf], tag="lnh_am2")


def emit_partition_amax(ctx, s_amax, s_out1, ps, eye32, dram_out):
    nc, mk = ctx["nc"], ctx["mk"]
    mk.emit("tensor", lambda t: t.transpose(ps[0:1, 0:128], s_amax[:], eye32[:]),
            reads=[s_amax, eye32], writes=[ps], tag="pam_T")
    mk.emit("vector", lambda v: v.reduce_max(s_out1[:], ps[0:1, 0:128], AX.X),
            reads=[ps], writes=[s_out1], tag="pam_R")
    mk.dma("sync", dram_out[:], s_out1[:], reads=[s_out1], writes=[dram_out],
           tag="pam_st")


def phase_l0_ffn(ctx):
    nc, mk = ctx["nc"], ctx["mk"]
    W = ctx["W"]
    XF1, XB1, XF2, XB2 = ctx["XF1"], ctx["XB1"], ctx["XF2"], ctx["XB2"]
    AMAXL = ctx["AMAXL"]
    e = ctx["stack"].enter_context
    eye32, eye16 = ctx["s_eye32"], ctx["s_eye16"]
    constcol = ctx["constcol"]
    QMAX = 32767.0

    UPRE = nc.dram_tensor("UPRE", [DFF, R], BF16)
    GPRE = nc.dram_tensor("GPRE", [DFF, R], BF16)
    GS = nc.dram_tensor("GS", [DFF, R], BF16)
    OPRE = nc.dram_tensor("OPRE", [DFF, R], BF16)
    XQ = nc.dram_tensor("XQ", [D, R], BF16)
    XR2 = nc.dram_tensor("XR2", [D, R], F32)
    AMG = nc.dram_tensor("AMG", [1, 8], F32)
    AMGo = nc.dram_tensor("AMGo", [1, 8], F32)
    for nm in ("UPRE", "GPRE", "GS", "OPRE", "XQ"):
        ctx[nm] = locals()[nm]

    s_fones = e(nc.sbuf_tensor("f_ones", [1, 512], F32))
    s_col1f = e(nc.sbuf_tensor("f_col1f", [128, 1], F32))
    s_scl = e(nc.sbuf_tensor("f_scl", [1, 8], F32))
    s_cols = e(nc.sbuf_tensor("f_cols", [128, 8], F32))
    s_am_u = e(nc.sbuf_tensor("f_amu", [128, 1], F32))
    s_am_g = e(nc.sbuf_tensor("f_amg", [128, 1], F32))
    s_am_s = e(nc.sbuf_tensor("f_ams", [128, 1], F32))
    s_am1 = e(nc.sbuf_tensor("f_am1", [1, 2], F32))
    s_wdT = e(nc.sbuf_tensor("f_wdT", [128, 24, 1024], BF16))

    p0 = e(nc.psum_tensor("f_p0", [128, 512], F32))
    p1 = e(nc.psum_tensor("f_p1", [128, 512], F32))
    p4 = e(nc.psum_tensor("f_p4", [128, 256], F32))
    p5 = e(nc.psum_tensor("f_p5", [128, 512], F32))
    p6 = e(nc.psum_tensor("f_p6", [128, 512], F32))

    mk.emit("vector", lambda v: v.memset(s_fones[:], 1.0), writes=[s_fones], memset=True)
    mk.emit("vector", lambda v: v.memset(s_col1f[:], 1.0), writes=[s_col1f], memset=True)
    for b in (s_am_u, s_am_g, s_am_s):
        mk.emit("vector", lambda v, b=b: v.memset(b[:], 0.0), writes=[b], memset=True)

    p4b = p4[:].bitcast(BF16)

    mk.dma("sync", AMG[0:1, 0:1], AMAXL[:], reads=[AMAXL], writes=[AMG], tag="am_cp")
    mk.collective("AllReduce", ALU.max, [AMG[:]], [AMGo[:]],
                  reads=[AMG], writes=[AMGo])
    mk.dma("sync", s_scl[0:1, 0:1], AMGo[0:1, 0:1], reads=[AMGo], writes=[s_scl],
           tag="am_ld")

    def emit_scale_pair(src_slice, dst_scl_col, dst_inv_col, qmax):
        mk.emit("vector", lambda v: v.tensor_scalar(
            s_scl[0:1, 2:3], src_slice, 1.0 / qmax, 1e-12, ALU.mult, ALU.max),
            reads=[s_scl], writes=[s_scl], tag="scl_s")
        mk.emit("vector", lambda v: v.reciprocal(s_scl[0:1, 3:4], s_scl[0:1, 2:3]),
                reads=[s_scl], writes=[s_scl], tag="scl_i")
        mk.emit("tensor", lambda t: t.matmul(
            p4[0:128, 0:2], s_fones[0:1, 0:128], s_scl[0:1, 2:4],
            start=True, stop=True),
            reads=[s_fones, s_scl], writes=[p4], tag="scl_bc")
        mk.emit("vector", lambda v: v.tensor_copy(dst_scl_col, p4[0:128, 0:1]),
                reads=[p4], writes=[s_cols], tag="scl_c1")
        mk.emit("vector", lambda v: v.tensor_copy(dst_inv_col, p4[0:128, 1:2]),
                reads=[p4], writes=[s_cols], tag="scl_c2")

    emit_scale_pair(s_scl[0:1, 0:1], s_cols[:, 0:1], s_cols[:, 1:2], QMAX)
    if ctx.get("ffn_stop") == "scales":
        return


    from contextlib import ExitStack as _ES
    xf1_v = XF1[:].rearrange("(c p) t -> p c t", p=128)
    with _ES() as sub:
        se = sub.enter_context
        s_xq = se(nc.sbuf_tensor("f_xq", [128, 8, R], BF16))
        s_x1t = se(nc.sbuf_tensor("f_x1t", [128, 8, 512], F32))
        s_xi = se(nc.sbuf_tensor("f_xi", [128, 8, 512], I32))
        s_wt = [se(nc.sbuf_tensor(f"f_wt{i}", [128, 8, 128], BF16)) for i in range(2)]
        s_sp2 = [se(nc.sbuf_tensor(f"f_sp{i}", [128, 512], BF16)) for i in range(2)]
        s_wraw = [se(nc.sbuf_tensor(f"f_wraw{i}", [128, DFF], F32)) for i in range(2)]
        s_am32 = se(nc.sbuf_tensor("f_am32", [128, 96], F32))
        s_rs32 = se(nc.sbuf_tensor("f_rs32", [128, 96], F32))
        s_wi = se(nc.sbuf_tensor("f_wi", [128, DFF], I32))
        s_wf = se(nc.sbuf_tensor("f_wf", [128, DFF], F32))
        s_wq16 = se(nc.sbuf_tensor("f_wq16", [128, DFF], BF16))

        def dequant(tile_f32, width):
            nblk = width // 32
            v3 = tile_f32.rearrange("p (b w) -> p b w", w=32)
            mk.emit("vector", lambda v, v3=v3, nblk=nblk: v.tensor_reduce(
                s_am32[:, 0:nblk], v3, AX.X, ALU.max, apply_absolute_value=True),
                reads=[s_wraw[0], s_wraw[1]], writes=[s_am32], tag="dq_am")
            mk.emit("vector", lambda v, nblk=nblk: v.tensor_scalar(
                s_rs32[:, 0:nblk], s_am32[:, 0:nblk], 1.0 / 7.0, 1e-12, ALU.mult, ALU.max),
                reads=[s_am32], writes=[s_rs32], tag="dq_s")
            mk.emit("vector", lambda v, nblk=nblk: v.reciprocal(
                s_am32[:, 0:nblk], s_rs32[:, 0:nblk]),
                reads=[s_rs32], writes=[s_am32], tag="dq_rs")
            mk.emit("vector", lambda v, v3=v3, nblk=nblk, width=width: v.tensor_tensor(
                s_wi[:, 0:width].rearrange("p (b w) -> p b w", w=32), v3,
                s_am32[:, 0:nblk].unsqueeze(2).broadcast_to((128, nblk, 32)), ALU.mult),
                reads=[s_wraw[0], s_wraw[1], s_am32], writes=[s_wi], tag="dq_r")
            mk.emit("vector", lambda g, width=width: g.tensor_copy(
                s_wf[:, 0:width], s_wi[:, 0:width]),
                reads=[s_wi], writes=[s_wf], tag="dq_f")
            mk.emit("vector", lambda g, nblk=nblk, width=width: g.tensor_tensor(
                s_wq16[:, 0:width].rearrange("p (b w) -> p b w", w=32),
                s_wf[:, 0:width].rearrange("p (b w) -> p b w", w=32),
                s_rs32[:, 0:nblk].unsqueeze(2).broadcast_to((128, nblk, 32)), ALU.mult),
                reads=[s_wf, s_rs32], writes=[s_wq16], tag="dq_q")

        for ot in range(8):
            wr = s_wraw[ot % 2]
            mk.dma("gpsimd", wr[:], W["L0_w_down"][ot * 128:(ot + 1) * 128, :],
                   reads=[W["L0_w_down"]], writes=[wr], tag="wd_ld")
            dequant(wr[:], DFF)
            for kc in range(24):
                mk.emit("tensor", lambda t, kc=kc, p4b=p4b: t.transpose(
                    p4b[:, 256:384], s_wq16[:, kc * 128:(kc + 1) * 128], eye16[:]),
                    reads=[s_wq16, eye16], writes=[p4], tag="wd_T")
                mk.emit("vector", lambda v, kc=kc, ot=ot, p4b=p4b: v.tensor_copy(
                    s_wdT[:, kc, ot * 128:(ot + 1) * 128], p4b[:, 256:384]),
                    reads=[p4], writes=[s_wdT], tag="wd_ev")

        if ctx.get("ffn_stop") == "wdown":
            return
        for tt in range(R // 512):
            sl = slice(tt * 512, (tt + 1) * 512)
            mk.dma("sync", s_x1t[:], xf1_v[:, :, sl], reads=[XF1], writes=[s_x1t],
                   tag="xq_ld")
            mk.emit("vector", lambda v: v.tensor_scalar(
                s_xi[:], s_x1t[:], s_cols[:, 1:2], None, ALU.mult),
                reads=[s_x1t, s_cols], writes=[s_xi], tag="xq_i")
            mk.emit("vector", lambda v: v.tensor_copy(s_x1t[:], s_xi[:]),
                    reads=[s_xi], writes=[s_x1t], tag="xq_f")
            mk.emit("vector", lambda v, sl=sl: v.tensor_scalar(
                s_xq[:, :, sl], s_x1t[:], s_cols[:, 0:1], None, ALU.mult),
                reads=[s_x1t, s_cols], writes=[s_xq], tag="xq_q")

        if ctx.get("ffn_stop") == "xq":
            return
        for (wname, PRE, amcol) in (("L0_w_up", UPRE, s_am_u), ("L0_w_gate", GPRE, s_am_g)):
            pre_v = PRE[:].rearrange("(c p) t -> p c t", p=128)
            for ot in range(24):
                wr = s_wraw[ot % 2]
                wt = s_wt[ot % 2]
                mk.dma("gpsimd", wr[:, 0:1024], W[wname][ot * 128:(ot + 1) * 128, :],
                       reads=[W[wname]], writes=[wr], tag="w_ld")
                dequant(wr[:, 0:1024], 1024)
                for kc in range(8):
                    mk.emit("tensor", lambda t, kc=kc, p4b=p4b: t.transpose(
                        p4b[:, 256:384], s_wq16[:, kc * 128:(kc + 1) * 128], eye16[:]),
                        reads=[s_wq16, eye16], writes=[p4], tag="w_T")
                    mk.emit("vector", lambda v, kc=kc, wt=wt, p4b=p4b: v.tensor_copy(
                        wt[:, kc, :], p4b[:, 256:384]),
                        reads=[p4], writes=[wt], tag="w_ev")
                for tt in range(R // 512):
                    sl = slice(tt * 512, (tt + 1) * 512)
                    ps = (p0, p1)[tt % 2]
                    if not ctx.get("w_nomm"):
                        for kc in range(8):
                            mk.emit("tensor", lambda t, kc=kc, sl=sl, ps=ps, wt=wt: t.matmul(
                                ps[:, 0:512], wt[:, kc, :], s_xq[:, kc, sl],
                                start=(kc == 0), stop=(kc == 7)),
                                reads=[wt, s_xq], writes=[ps], tag="w_mm")
                    if not ctx.get("w_noamax"):
                        mk.emit("vector", lambda v, ps=ps: v.reduce_max(
                            s_x1t[:, 0, 0:1], ps[:, 0:512], AX.X, apply_absolute_value=True),
                            reads=[ps], writes=[s_x1t], tag="w_am1")
                        mk.emit("vector", lambda v, amcol=amcol: v.tensor_max(
                            amcol[:], amcol[:], s_x1t[:, 0, 0:1]),
                            reads=[s_x1t, amcol], writes=[amcol], tag="w_am2")
                    if not ctx.get("w_nosp"):
                        sp = s_sp2[tt % 2]
                        mk.emit("vector", lambda v, ps=ps, sp=sp: v.tensor_copy(sp[:], ps[:, 0:512]),
                                reads=[ps], writes=[sp], tag="w_sp")
                        mk.dma("sync", pre_v[:, ot, sl], sp[:], reads=[sp],
                               writes=[PRE], tag="w_st")
                if ctx.get("ffn_stop") == f"W{ot}_{wname}":
                    return
    if ctx.get("ffn_stop") == "W":
        return
    mk.barrier()

    emit_partition_amax(ctx, s_am_u, s_am1[0:1, 0:1], p4, eye32, AMG[0:1, 0:1])
    emit_partition_amax(ctx, s_am_g, s_am1[0:1, 1:2], p4, eye32, AMG[0:1, 1:2])
    mk.collective("AllReduce", ALU.max, [AMG[:]], [AMGo[:]],
                  reads=[AMG], writes=[AMGo])
    mk.dma("sync", s_scl[0:1, 4:6], AMGo[0:1, 0:2], reads=[AMGo], writes=[s_scl], tag="m2_ld")
    emit_scale_pair(s_scl[0:1, 4:5], s_cols[:, 2:3], s_cols[:, 3:4], QMAX)
    emit_scale_pair(s_scl[0:1, 5:6], s_cols[:, 4:5], s_cols[:, 5:6], QMAX)

    gpre_v = GPRE[:].rearrange("(c p) t -> p c t", p=128)
    upre_v = UPRE[:].rearrange("(c p) t -> p c t", p=128)
    gs_v = GS[:].rearrange("(c p) t -> p c t", p=128)
    opre_v = OPRE[:].rearrange("(c p) t -> p c t", p=128)
    with _ES() as sub:
        se = sub.enter_context
        s_in1 = [se(nc.sbuf_tensor(f"f_in1{i}", [128, 4, 512], BF16)) for i in range(2)]
        s_in2 = [se(nc.sbuf_tensor(f"f_in2{i}", [128, 4, 512], BF16)) for i in range(2)]
        s_out = [se(nc.sbuf_tensor(f"f_out{i}", [128, 4, 512], BF16)) for i in range(2)]
        s_f1 = [se(nc.sbuf_tensor(f"f_f1{i}", [128, 4, 512], F32)) for i in range(2)]
        s_f2 = [se(nc.sbuf_tensor(f"f_f2{i}", [128, 4, 512], F32)) for i in range(2)]
        s_i1 = [se(nc.sbuf_tensor(f"f_i1{i}", [128, 4, 512], I32)) for i in range(2)]

        def qdq_from(dstbuf, srcbuf, inv_col, scl_col, i1):
            mk.emit("vector", lambda v, srcbuf=srcbuf, inv_col=inv_col, i1=i1:
                    v.tensor_scalar(i1[:], srcbuf[:], inv_col, None, ALU.mult),
                    reads=[srcbuf, s_cols], writes=[i1], tag="qd_i")
            mk.emit("vector", lambda v, dstbuf=dstbuf, i1=i1: v.tensor_copy(
                dstbuf[:], i1[:]),
                reads=[i1], writes=[dstbuf], tag="qd_f")
            mk.emit("vector", lambda v, dstbuf=dstbuf, scl_col=scl_col: v.tensor_scalar(
                dstbuf[:], dstbuf[:], scl_col, None, ALU.mult),
                reads=[dstbuf, s_cols], writes=[dstbuf], tag="qd_q")

        for it in range(48):
            gt, tt = divmod(it, 8)
            sl = slice(tt * 512, (tt + 1) * 512)
            ocs = slice(gt * 4, (gt + 1) * 4)
            pi = it % 2
            in1, f1, f2, i1, out = s_in1[pi], s_f1[pi], s_f2[pi], s_i1[pi], s_out[pi]
            mk.dma("sync", in1[:], gpre_v[:, ocs, sl], reads=[GPRE],
                   writes=[in1], tag="x_ld")
            qdq_from(f1, in1, s_cols[:, 5:6], s_cols[:, 4:5], i1)
            mk.emit("scalar", lambda s, f1=f1, f2=f2: s.activation(
                f2[:], f1[:], AF.Sigmoid),
                reads=[f1], writes=[f2], tag="x_sig")
            mk.emit("vector", lambda v, f2=f2, i1=i1: v.tensor_scalar(
                i1[:], f2[:], 65536.0, None, ALU.mult),
                reads=[f2], writes=[i1], tag="x_si")
            mk.emit("vector", lambda v, f2=f2, i1=i1: v.tensor_copy(f2[:], i1[:]),
                    reads=[i1], writes=[f2], tag="x_sf")
            mk.emit("vector", lambda v, f2=f2: v.tensor_scalar(
                f2[:], f2[:], 65535.0, None, ALU.min),
                reads=[f2], writes=[f2], tag="x_sc")
            mk.emit("vector", lambda v, f1=f1, f2=f2: v.tensor_tensor(
                f2[:], f1[:], f2[:], ALU.mult),
                reads=[f1, f2], writes=[f2], tag="x_gs1")
            mk.emit("vector", lambda v, f2=f2, out=out: v.tensor_scalar(
                out[:], f2[:], float(2.0 ** -16), None, ALU.mult),
                reads=[f2], writes=[out], tag="x_gs2")
            mk.emit("vector", lambda v, f1=f1, out=out: v.reduce_max(
                f1[:, 0, 0:1], out[:].rearrange("p a b -> p (a b)"),
                AX.X, apply_absolute_value=True),
                reads=[out], writes=[f1], tag="x_am1")
            mk.emit("vector", lambda v, f1=f1: v.tensor_max(
                s_am_s[:], s_am_s[:], f1[:, 0, 0:1]),
                reads=[f1, s_am_s], writes=[s_am_s], tag="x_am2")
            mk.dma("scalar", gs_v[:, ocs, sl], out[:], reads=[out],
                   writes=[GS], tag="x_st")

        if ctx.get("ffn_stop") == "X":
            return
        mk.barrier()
        emit_partition_amax(ctx, s_am_s, s_am1[0:1, 0:1], p4, eye32, AMG[0:1, 0:1])
        mk.collective("AllReduce", ALU.max, [AMG[:]], [AMGo[:]],
                      reads=[AMG], writes=[AMGo])
        mk.dma("sync", s_scl[0:1, 6:7], AMGo[0:1, 0:1], reads=[AMGo], writes=[s_scl],
               tag="m3_ld")
        emit_scale_pair(s_scl[0:1, 6:7], s_cols[:, 6:7], s_cols[:, 7:8], QMAX)
        mk.emit("vector", lambda v: v.memset(s_am_s[:], 0.0), writes=[s_am_s],
                memset=True)

        for it in range(48):
            gt, tt = divmod(it, 8)
            sl = slice(tt * 512, (tt + 1) * 512)
            ocs = slice(gt * 4, (gt + 1) * 4)
            pi = it % 2
            in1, in2, f1, f2, i1, out = (s_in1[pi], s_in2[pi], s_f1[pi],
                                         s_f2[pi], s_i1[pi], s_out[pi])
            mk.dma("sync", in1[:], gs_v[:, ocs, sl], reads=[GS],
                   writes=[in1], tag="y_ld1")
            mk.dma("scalar", in2[:], upre_v[:, ocs, sl], reads=[UPRE],
                   writes=[in2], tag="y_ld2")
            qdq_from(f1, in1, s_cols[:, 7:8], s_cols[:, 6:7], i1)
            qdq_from(f2, in2, s_cols[:, 3:4], s_cols[:, 2:3], i1)
            mk.emit("vector", lambda v, f1=f1, f2=f2: v.tensor_tensor(
                f1[:], f1[:], f2[:], ALU.mult),
                reads=[f1, f2], writes=[f1], tag="y_op")
            mk.emit("vector", lambda v, f1=f1, out=out: v.tensor_copy(out[:], f1[:]),
                    reads=[f1], writes=[out], tag="y_cp")
            mk.emit("vector", lambda v, f1=f1, f2=f2: v.reduce_max(
                f2[:, 0, 0:1], f1[:].rearrange("p a b -> p (a b)"),
                AX.X, apply_absolute_value=True),
                reads=[f1], writes=[f2], tag="y_am1")
            mk.emit("vector", lambda v, f2=f2: v.tensor_max(
                s_am_s[:], s_am_s[:], f2[:, 0, 0:1]),
                reads=[f2, s_am_s], writes=[s_am_s], tag="y_am2")
            mk.dma("scalar", opre_v[:, ocs, sl], out[:], reads=[out],
                   writes=[OPRE], tag="y_st")

        if ctx.get("ffn_stop") == "Y":
            return
        mk.barrier()
        emit_partition_amax(ctx, s_am_s, s_am1[0:1, 0:1], p4, eye32, AMG[0:1, 0:1])
        mk.collective("AllReduce", ALU.max, [AMG[:]], [AMGo[:]],
                      reads=[AMG], writes=[AMGo])
        mk.dma("sync", s_scl[0:1, 7:8], AMGo[0:1, 0:1], reads=[AMGo], writes=[s_scl],
               tag="m4_ld")
        emit_scale_pair(s_scl[0:1, 7:8], s_cols[:, 6:7], s_cols[:, 7:8], QMAX)

        s_o3 = se(nc.sbuf_tensor("f_o3", [128, 24, 512], BF16))
        bufs = dict(pp=(p0, p1), p5=p5, p6=p6, e=se,
                    s_mean=se(nc.sbuf_tensor("f_lmean", [1, 512], F32)),
                    s_mr=se(nc.sbuf_tensor("f_lmr", [1, 512], F32)),
                    s_rstd=se(nc.sbuf_tensor("f_lrstd", [1, 512], F32)),
                    s_col1f=s_col1f, s_fones=s_fones)

        def feed_pre(tt):
            sl = slice(tt * 512, (tt + 1) * 512)
            for gt in range(6):
                ocs = slice(gt * 4, (gt + 1) * 4)
                pi = gt % 2
                in1, f1, i1 = s_in1[pi], s_f1[pi], s_i1[pi]
                mk.dma("sync", in1[:], opre_v[:, ocs, sl], reads=[OPRE],
                       writes=[in1], tag="z_ld")
                qdq_from(f1, in1, s_cols[:, 7:8], s_cols[:, 6:7], i1)
                mk.emit("vector", lambda v, ocs=ocs, f1=f1: v.tensor_copy(
                    s_o3[:, ocs, :], f1[:]),
                    reads=[f1], writes=[s_o3], tag="z_cp")

        def feed(tt, fc, ps):
            for kc in range(24):
                mk.emit("tensor", lambda t, kc=kc, fc=fc, ps=ps: t.matmul(
                    ps[:, 0:512], s_wdT[:, kc, fc * 128:(fc + 1) * 128],
                    s_o3[:, kc, :], start=(kc == 0), stop=(kc == 23)),
                    reads=[s_wdT, s_o3], writes=[ps], tag="z_mm")

        emit_ln_stage(ctx, bufs, feed_pre, feed, XF1, XR2, XF2, XB2)


def phase_l1_attn(ctx):
    nc, mk = ctx["nc"], ctx["mk"]
    W = ctx["W"]
    XF2, XB2, XF3, XB3 = ctx["XF2"], ctx["XB2"], ctx["XF3"], ctx["XB3"]
    e = ctx["stack"].enter_context
    eye32, eye16 = ctx["s_eye32"], ctx["s_eye16"]
    constcol = ctx["constcol"]
    TWO_PI = float(2.0 * np.pi)

    ZPRE = nc.dram_tensor("ZPRE", [D, R], F32)
    ZB = nc.dram_tensor("ZB", [D, R], BF16)
    QB = nc.dram_tensor("QB", [D, R], BF16)
    RB = nc.dram_tensor("RB", [D, R], BF16)
    HB = nc.dram_tensor("HB", [D, R], BF16)
    QC = nc.dram_tensor("QC", [2 * D, R], BF16)
    RC = nc.dram_tensor("RC", [2 * D, R], BF16)
    OH = nc.dram_tensor("OH", [D, R], BF16)
    XR3 = nc.dram_tensor("XR3", [D, R], F32)
    for nm in ("ZPRE", "ZB", "QB", "RB", "HB", "QC", "RC", "OH"):
        ctx[nm] = locals()[nm]

    xb2_v = XB2[:].rearrange("(c p) t -> p c t", p=128)
    zpre_v = ZPRE[:].rearrange("(c p) t -> p c t", p=128)
    zb_v = ZB[:].rearrange("(c p) t -> p c t", p=128)

    s_fones = e(nc.sbuf_tensor("l_ones", [1, 512], F32))
    s_col1f = e(nc.sbuf_tensor("l_col1f", [128, 1], F32))
    s_w = e(nc.sbuf_tensor("l_w", [128, 8, 1024], BF16))
    s_bc = e(nc.sbuf_tensor("l_bc", [128, 8], F32))
    s_lng = e(nc.sbuf_tensor("l_lng", [128, 8], F32))
    s_lnb = e(nc.sbuf_tensor("l_lnb", [128, 8], F32))
    s_v8 = e(nc.sbuf_tensor("l_v8", [8, 128], F32))
    s_xt = [e(nc.sbuf_tensor(f"l_xt{i}", [128, 8, 512], BF16)) for i in range(2)]
    s_t1 = e(nc.sbuf_tensor("l_t1", [128, 512], F32))
    s_t2 = e(nc.sbuf_tensor("l_t2", [128, 512], F32))
    s_t3 = e(nc.sbuf_tensor("l_t3", [128, 512], F32))
    s_i1 = e(nc.sbuf_tensor("l_i1", [128, 512], I32))
    s_b16 = e(nc.sbuf_tensor("l_b16", [128, 512], BF16))
    s_b16b = e(nc.sbuf_tensor("l_b16b", [128, 512], BF16))
    s_mean = e(nc.sbuf_tensor("l_mean", [1, 512], F32))
    s_mr = e(nc.sbuf_tensor("l_mr", [1, 512], F32))
    s_rstd = e(nc.sbuf_tensor("l_rstd", [1, 512], F32))
    s_xr = e(nc.sbuf_tensor("l_xr", [128, 512], F32))
    s_x1b = e(nc.sbuf_tensor("l_x1b", [128, 512], BF16))
    s_wphi = e(nc.sbuf_tensor("l_wphi", [128, 64], BF16))
    s_bphic = e(nc.sbuf_tensor("l_bphic", [128, 1], F32))

    p0 = e(nc.psum_tensor("l_p0", [128, 512], F32))
    p1 = e(nc.psum_tensor("l_p1", [128, 512], F32))
    p4 = e(nc.psum_tensor("l_p4", [128, 256], F32))
    p5 = e(nc.psum_tensor("l_p5", [128, 512], F32))
    p6 = e(nc.psum_tensor("l_p6", [128, 512], F32))
    p7 = e(nc.psum_tensor("l_p7", [128, 512], F32))
    p4b = p4[:].bitcast(BF16)

    mk.emit("vector", lambda v: v.memset(s_fones[:], 1.0), writes=[s_fones], memset=True)
    mk.emit("vector", lambda v: v.memset(s_col1f[:], 1.0), writes=[s_col1f], memset=True)

    def load_cols(wname, dst):
        mk.dma("sync", s_v8[:], W[wname][:].rearrange("(a b) -> a b", a=8),
               reads=[W[wname]], writes=[s_v8], tag="lc_ld")
        mk.emit("tensor", lambda t: t.transpose(p4[0:128, 0:8], s_v8[:], eye32[0:8, 0:8]),
                reads=[s_v8, eye32], writes=[p4], tag="lc_T")
        mk.emit("vector", lambda v, dst=dst: v.tensor_copy(dst[:], p4[0:128, 0:8]),
                reads=[p4], writes=[dst], tag="lc_ev")

    def load_weight(wname):
        mk.dma("gpsimd", s_w[:], W[wname][:].rearrange("(c p) n -> p c n", p=128),
               reads=[W[wname]], writes=[s_w], tag="lw_" + wname)

    load_cols("L1_ln_g", s_lng)
    load_cols("L1_ln_b", s_lnb)

    load_weight("L1_Wz")
    load_cols("L1_bz", s_bc)
    s_zres = e(nc.sbuf_tensor("l_zres", [128, 8, 512], F32))
    _beps = constcol(1e-5)[0:1, :]
    for tt in range(R // 512):
        sl = slice(tt * 512, (tt + 1) * 512)
        xt = s_xt[tt % 2]
        mk.dma("sync", xt[:], xb2_v[:, :, sl], reads=[XB2], writes=[xt], tag="z_xld")
        for fc in range(8):
            ps = (p0, p1)[fc % 2]
            for kc in range(8):
                mk.emit("tensor", lambda t, kc=kc, fc=fc, ps=ps, xt=xt: t.matmul(
                    ps[:, 0:512], s_w[:, kc, fc * 128:(fc + 1) * 128], xt[:, kc, :],
                    start=(kc == 0), stop=(kc == 7)),
                    reads=[s_w, xt], writes=[ps], tag="z_mm")
            mk.emit("vector", lambda v, fc=fc, ps=ps: v.tensor_scalar(
                s_zres[:, fc, :], ps[:, 0:512], s_bc[:, fc:fc + 1], None, ALU.add),
                reads=[ps, s_bc], writes=[s_zres], tag="z_bias")
        for fc in range(8):
            t2 = (s_t2, s_t3)[fc % 2]
            mk.emit("scalar", lambda s, fc=fc, t2=t2: s.activation(
                t2[:], s_zres[:, fc, :], AF.Square),
                reads=[s_zres], writes=[t2], tag="z_sq")
            mk.emit("tensor", lambda t, fc=fc: t.matmul(
                p6[0:1, 0:512], s_col1f[:], s_zres[:, fc, :],
                start=(fc == 0), stop=(fc == 7), skip_group_check=True),
                reads=[s_col1f, s_zres], writes=[p6], tag="z_S1")
            mk.emit("tensor", lambda t, fc=fc, t2=t2: t.matmul(
                p5[0:1, 0:512], s_col1f[:], t2[:],
                start=(fc == 0), stop=(fc == 7), skip_group_check=True),
                reads=[s_col1f, t2], writes=[p5], tag="z_S2")
        mk.emit("vector", lambda v: v.tensor_scalar(
            s_mean[0:1, :], p6[0:1, 0:512], 1.0 / D, None, ALU.mult),
            reads=[p6], writes=[s_mean], tag="z_mean")
        mk.emit("scalar", lambda s: s.activation(s_mr[0:1, :], s_mean[0:1, :], AF.Square),
                reads=[s_mean], writes=[s_mr], tag="z_ms")
        mk.emit("vector", lambda v: v.scalar_tensor_tensor(
            s_rstd[0:1, :], p5[0:1, 0:512], 1.0 / D, s_mr[0:1, :], ALU.mult, ALU.subtract),
            reads=[p5, s_mr], writes=[s_rstd], tag="z_var")
        mk.emit("scalar", lambda s, _beps=_beps: s.activation(
            s_rstd[0:1, :], s_rstd[0:1, :], AF.Sqrt, bias=_beps),
            reads=[s_rstd, ctx["s_const"]], writes=[s_rstd], tag="z_sqrt")
        mk.emit("vector", lambda v: v.reciprocal(s_rstd[0:1, :], s_rstd[0:1, :]),
                reads=[s_rstd], writes=[s_rstd], tag="z_rec")
        mk.emit("vector", lambda v: v.tensor_mul(
            s_mr[0:1, :], s_mean[0:1, :], s_rstd[0:1, :]),
            reads=[s_mean, s_rstd], writes=[s_mr], tag="z_mr")
        mk.emit("tensor", lambda t: t.matmul(
            p6[:, 0:512], s_fones[0:1, 0:128], s_rstd[0:1, 0:512], start=True, stop=True),
            reads=[s_fones, s_rstd], writes=[p6], tag="z_bc1")
        mk.emit("tensor", lambda t: t.matmul(
            p5[:, 0:512], s_fones[0:1, 0:128], s_mr[0:1, 0:512], start=True, stop=True),
            reads=[s_fones, s_mr], writes=[p5], tag="z_bc2")
        for fc in range(8):
            t1 = (s_t1, s_t2)[fc % 2]
            xb = (s_x1b, s_b16b)[fc % 2]
            mk.emit("vector", lambda v, fc=fc, t1=t1: v.tensor_tensor(
                t1[:], s_zres[:, fc, :], p6[:, 0:512], ALU.mult),
                reads=[s_zres, p6], writes=[t1], tag="z_ap1")
            mk.emit("vector", lambda v, t1=t1: v.tensor_tensor(
                t1[:], t1[:], p5[:, 0:512], ALU.subtract),
                reads=[t1, p5], writes=[t1], tag="z_ap2")
            mk.emit("vector", lambda v, fc=fc, t1=t1, xb=xb: v.tensor_scalar(
                xb[:], t1[:], s_lng[:, fc:fc + 1], s_lnb[:, fc:fc + 1],
                ALU.mult, ALU.add),
                reads=[t1, s_lng, s_lnb], writes=[xb], tag="z_ap3")
            mk.dma("sync", zb_v[:, fc, sl], xb[:], reads=[xb], writes=[ZB],
                   tag="z_st2")

    for (wname, bname, DST) in (("L1_Wq", "L1_bq", QB), ("L1_Wr", "L1_br", RB),
                                ("L1_Wh", "L1_bh", HB)):
        load_weight(wname)
        load_cols(bname, s_bc)
        dst_v = DST[:].rearrange("(c p) t -> p c t", p=128)
        for tt in range(R // 512):
            sl = slice(tt * 512, (tt + 1) * 512)
            xt = s_xt[tt % 2]
            mk.dma("sync", xt[:], zb_v[:, :, sl], reads=[ZB], writes=[xt],
                   tag="q_xld")
            for fc in range(8):
                ps = (p0, p1)[fc % 2]
                for kc in range(8):
                    mk.emit("tensor", lambda t, kc=kc, fc=fc, ps=ps, xt=xt: t.matmul(
                        ps[:, 0:512], s_w[:, kc, fc * 128:(fc + 1) * 128], xt[:, kc, :],
                        start=(kc == 0), stop=(kc == 7)),
                        reads=[s_w, xt], writes=[ps], tag="q_mm")
                mk.emit("vector", lambda v, fc=fc, ps=ps: v.tensor_scalar(
                    s_b16[:], ps[:, 0:512], s_bc[:, fc:fc + 1], None, ALU.add),
                    reads=[ps, s_bc], writes=[s_b16], tag="q_ev")
                mk.dma("sync", dst_v[:, fc, sl], s_b16[:], reads=[s_b16], writes=[DST],
                       tag="q_st")

    mk.dma("gpsimd", s_wphi[0:64, :], W["L1_Wphi"][:], reads=[W["L1_Wphi"]],
           writes=[s_wphi], tag="phi_w")
    mk.dma("gpsimd", s_wphi[64:128, :], W["L1_Wphi"][:], reads=[W["L1_Wphi"]],
           writes=[s_wphi], tag="phi_w2")
    mk.dma("sync", s_v8[0:1, 0:64], W["L1_bphi"][:].rearrange("(a b) -> a b", a=1),
           reads=[W["L1_bphi"]], writes=[s_v8], tag="phib_ld")
    mk.emit("tensor", lambda t: t.transpose(p4[0:64, 0:1], s_v8[0:1, 0:64],
                                            eye32[0:1, 0:1]),
            reads=[s_v8, eye32], writes=[p4], tag="phib_T")
    mk.emit("vector", lambda v: v.tensor_copy(s_bphic[0:64, :], p4[0:64, 0:1]),
            reads=[p4], writes=[s_bphic], tag="phib_c1")
    mk.emit("tensor", lambda t: t.transpose(p4[0:64, 1:2], s_v8[0:1, 0:64],
                                            eye32[0:1, 0:1]),
            reads=[s_v8, eye32], writes=[p4], tag="phib_T2")
    mk.emit("vector", lambda v: v.tensor_copy(s_bphic[64:128, :], p4[0:64, 1:2]),
            reads=[p4], writes=[s_bphic], tag="phib_c2")

    qb_v = QB[:].rearrange("(c p) t -> p c t", p=128)
    rb_v = RB[:].rearrange("(c p) t -> p c t", p=128)
    qc_v = QC[:].rearrange("(c x p) t -> x c p t", x=2, p=64)
    rc_v = RC[:].rearrange("(c x p) t -> x c p t", x=2, p=64)
    s_b16_2 = e(nc.sbuf_tensor("l_b16_2", [128, 512], BF16))
    s_t1_2 = e(nc.sbuf_tensor("l_t1_2", [128, 512], F32))
    s_t2_2 = e(nc.sbuf_tensor("l_t2_2", [128, 512], F32))
    s_t3_2 = e(nc.sbuf_tensor("l_t3_2", [128, 512], F32))
    s_i1_2 = e(nc.sbuf_tensor("l_i1_2", [128, 512], I32))
    s_b16b_2 = e(nc.sbuf_tensor("l_b16b_2", [128, 512], BF16))
    qc_sets = [(s_b16, s_t1, s_t2, s_t3, s_i1, s_b16b),
               (s_b16_2, s_t1_2, s_t2_2, s_t3_2, s_i1_2, s_b16b_2)]
    qit = 0
    for (SRC_v, SRCbuf, DST_v, DSTbuf) in ((qb_v, QB, qc_v, QC), (rb_v, RB, rc_v, RC)):
        for c in range(8):
            for tt in range(R // 512):
                qit += 1
                b16, t1, t2, t3, i1, b16b = qc_sets[qit % 2]
                pphi = (p0, p7)[qit % 2]
                sl = slice(tt * 512, (tt + 1) * 512)
                mk.dma("sync", b16[:], SRC_v[:, c, sl], reads=[SRCbuf],
                       writes=[b16], tag="qc_ld")
                for half in range(2):
                    mk.emit("tensor", lambda t, half=half, b16=b16, pphi=pphi: t.matmul(
                        pphi[64 * half:64 * half + 64, 0:512],
                        s_wphi[64 * half:64 * half + 64, :],
                        b16[64 * half:64 * half + 64, :], start=True, stop=True),
                        reads=[s_wphi, b16], writes=[pphi], tag="qc_phi")
                mk.emit("vector", lambda v, t1=t1, pphi=pphi: v.tensor_scalar(
                    t1[:], pphi[:, 0:512], s_bphic[:, 0:1], None, ALU.add),
                    reads=[pphi, s_bphic], writes=[t1], tag="qc_phib")
                mk.emit("vector", lambda v, t1=t1, i1=i1: v.tensor_scalar(
                    i1[:], t1[:], 1.0 / TWO_PI, 0.25, ALU.mult, ALU.add),
                    reads=[t1], writes=[i1], tag="qc_k2")
                mk.emit("vector", lambda v, t2=t2, i1=i1: v.tensor_copy(t2[:], i1[:]),
                        reads=[i1], writes=[t2], tag="qc_k2f")
                mk.emit("vector", lambda v, t2=t2, t3=t3, t1=t1: v.scalar_tensor_tensor(
                    t3[:], t2[:], -TWO_PI, t1[:], ALU.mult, ALU.add),
                    reads=[t2, t1], writes=[t3], tag="qc_carg")
                _bhalfpi = constcol(float(np.pi / 2.0))
                mk.emit("scalar", lambda s, _b=_bhalfpi, t2=t2, t3=t3: s.activation(
                    t2[:], t3[:], AF.Sin, bias=_b),
                    reads=[t3, ctx["s_const"]], writes=[t2], tag="qc_cos")
                mk.emit("vector", lambda v, b16b=b16b, b16=b16, t2=t2: v.tensor_tensor(
                    b16b[:], b16[:], t2[:], ALU.mult),
                    reads=[b16, t2], writes=[b16b], tag="qc_mc")
                mk.dma("sync", DST_v[0, 2 * c, :, sl], b16b[0:64, :],
                       reads=[b16b], writes=[DSTbuf], tag="qc_stc0")
                mk.dma("sync", DST_v[0, 2 * c + 1, :, sl], b16b[64:128, :],
                       reads=[b16b], writes=[DSTbuf], tag="qc_stc1")
                mk.emit("vector", lambda v, t1=t1, i1=i1: v.tensor_scalar(
                    i1[:], t1[:], 1.0 / TWO_PI, None, ALU.mult),
                    reads=[t1], writes=[i1], tag="qc_k1")
                mk.emit("vector", lambda v, t2=t2, i1=i1: v.tensor_copy(t2[:], i1[:]),
                        reads=[i1], writes=[t2], tag="qc_k1f")
                mk.emit("vector", lambda v, t2=t2, t3=t3, t1=t1: v.scalar_tensor_tensor(
                    t3[:], t2[:], -TWO_PI, t1[:], ALU.mult, ALU.add),
                    reads=[t2, t1], writes=[t3], tag="qc_sarg")
                mk.emit("scalar", lambda s, t2=t2, t3=t3: s.activation(
                    t2[:], t3[:], AF.Sin),
                    reads=[t3], writes=[t2], tag="qc_sin")
                mk.emit("vector", lambda v, b16b=b16b, b16=b16, t2=t2: v.tensor_tensor(
                    b16b[:], b16[:], t2[:], ALU.mult),
                    reads=[b16, t2], writes=[b16b], tag="qc_ms")
                mk.dma("scalar", DST_v[1, 2 * c, :, sl], b16b[0:64, :],
                       reads=[b16b], writes=[DSTbuf], tag="qc_sts0")
                mk.dma("scalar", DST_v[1, 2 * c + 1, :, sl], b16b[64:128, :],
                       reads=[b16b], writes=[DSTbuf], tag="qc_sts1")

    s_qc = [e(nc.sbuf_tensor(f"l_qc{i}", [128, T], BF16)) for i in range(2)]
    s_rc = [e(nc.sbuf_tensor(f"l_rc{i}", [128, T], BF16)) for i in range(2)]
    s_hh = [e(nc.sbuf_tensor(f"l_hh{i}", [64, T], BF16)) for i in range(2)]
    s_rct = [e(nc.sbuf_tensor(f"l_rct{i}", [128, 16, 128], BF16)) for i in range(2)]
    s_hht = [e(nc.sbuf_tensor(f"l_hht{i}", [128, 16, 64], BF16)) for i in range(2)]
    s_G = [e(nc.sbuf_tensor(f"l_G{i}", [128, 128], BF16)) for i in range(2)]
    s_M = [e(nc.sbuf_tensor(f"l_M{i}", [128, 64], BF16)) for i in range(2)]
    s_inrm = [e(nc.sbuf_tensor(f"l_inrm{i}", [1, T], F32)) for i in range(2)]
    s_qcs = [e(nc.sbuf_tensor(f"l_qcs{i}", [128, 512], BF16)) for i in range(2)]
    oh_v = OH[:].rearrange("(h p) t -> p h t", p=64)
    hb_hv = HB[:].rearrange("(h p) t -> p h t", p=64)
    qc_hv = QC[:].rearrange("(h p) t -> p h t", p=128)
    rc_hv = RC[:].rearrange("(h p) t -> p h t", p=128)

    for seq in range(SEQ_PER_CORE):
        t0seq = seq * T
        ssl = slice(t0seq, t0seq + T)
        for h in range(H1):
            hp = (seq * H1 + h) % 2
            qc, rc, hh = s_qc[hp], s_rc[hp], s_hh[hp]
            rct, hht, G, M = s_rct[hp], s_hht[hp], s_G[hp], s_M[hp]
            inrm, qcs = s_inrm[hp], s_qcs[hp]
            pG = (p5, p6)[hp]
            mk.dma("sync", qc[:], qc_hv[:, h, ssl], reads=[QC], writes=[qc],
                   tag="gh_qld")
            mk.dma("sync", rc[:], rc_hv[:, h, ssl], reads=[RC], writes=[rc],
                   tag="gh_rld")
            mk.dma("scalar", hh[:], hb_hv[:, h, ssl], reads=[HB], writes=[hh],
                   tag="gh_hld")
            for c in range(16):
                tsl = slice(c * 128, (c + 1) * 128)
                pt = p4 if hp == 0 else p1
                ptb = pt[:].bitcast(BF16)
                mk.emit("tensor", lambda t, tsl=tsl, ptb=ptb, rc=rc: t.transpose(
                    ptb[:, 0:128], rc[:, tsl], eye16[:]),
                    reads=[rc, eye16], writes=[pt], tag="gh_rT")
                mk.emit("vector", lambda v, c=c, ptb=ptb, rct=rct: v.tensor_copy(
                    rct[:, c, :], ptb[:, 0:128]),
                    reads=[pt], writes=[rct], tag="gh_rTe")
                mk.emit("tensor", lambda t, tsl=tsl, ptb=ptb, hh=hh: t.transpose(
                    ptb[:, 128:192], hh[:, tsl], eye16[0:64, 0:64]),
                    reads=[hh, eye16], writes=[pt], tag="gh_hT")
                mk.emit("vector", lambda v, c=c, ptb=ptb, hht=hht: v.tensor_copy(
                    hht[:, c, :], ptb[:, 128:192]),
                    reads=[pt], writes=[hht], tag="gh_hTe")
            for c in range(16):
                mk.emit("tensor", lambda t, c=c, rct=rct, pG=pG: t.matmul(
                    pG[:, 0:128], rct[:, c, :], rct[:, c, :],
                    start=(c == 0), stop=(c == 15), skip_group_check=True),
                    reads=[rct], writes=[pG], tag="gh_G")
                mk.emit("tensor", lambda t, c=c, rct=rct, hht=hht, pG=pG: t.matmul(
                    pG[:, 128:192], rct[:, c, :], hht[:, c, :],
                    start=False, stop=(c == 15), skip_group_check=True),
                    reads=[rct, hht], writes=[pG], tag="gh_M")
            mk.emit("vector", lambda v, G=G, pG=pG: v.tensor_copy(G[:], pG[:, 0:128]),
                    reads=[pG], writes=[G], tag="gh_Ge")
            mk.emit("vector", lambda v, M=M, pG=pG: v.tensor_copy(M[:, 0:64], pG[:, 128:192]),
                    reads=[pG], writes=[M], tag="gh_Me")
            for tq in range(4):
                qsl = slice(tq * 512, (tq + 1) * 512)
                pW = (p0, p7)[hp]
                mk.emit("tensor", lambda t, qsl=qsl, G=G, qc=qc, pW=pW: t.matmul(
                    pW[:, 0:512], G[:], qc[:, qsl], start=True, stop=True),
                    reads=[G, qc], writes=[pW], tag="gh_W")
                t1l = (s_t1, s_t2)[hp]
                mk.emit("vector", lambda v, qsl=qsl, qc=qc, pW=pW, t1l=t1l: v.tensor_tensor(
                    t1l[:], pW[:, 0:512], qc[:, qsl], ALU.mult),
                    reads=[pW, qc], writes=[t1l], tag="gh_WQ")
                mk.emit("tensor", lambda t, t1l=t1l, pW=pW: t.matmul(
                    pW[0:1, 0:512], s_col1f[:], t1l[:], start=True, stop=True),
                    reads=[s_col1f, t1l], writes=[pW], tag="gh_n2")
                mrow = (s_mean, s_mr)[hp]
                mk.emit("scalar", lambda s, mrow=mrow, pW=pW: s.activation(
                    mrow[0:1, :], pW[0:1, 0:512], AF.Sqrt),
                    reads=[pW], writes=[mrow], tag="gh_sq")
                mk.emit("vector", lambda v, mrow=mrow: v.tensor_scalar(
                    mrow[0:1, :], mrow[0:1, :], 1e-12, None, ALU.max),
                    reads=[mrow], writes=[mrow], tag="gh_mx")
                mk.emit("vector", lambda v, tq=tq, qsl=qsl, mrow=mrow, inrm=inrm:
                        v.reciprocal(inrm[0:1, qsl], mrow[0:1, :]),
                        reads=[mrow], writes=[inrm], tag="gh_in")
                mk.emit("tensor", lambda t, qsl=qsl, inrm=inrm, pW=pW: t.matmul(
                    pW[:, 0:512], s_fones[0:1, 0:128], inrm[0:1, qsl],
                    start=True, stop=True),
                    reads=[s_fones, inrm], writes=[pW], tag="gh_bc")
                mk.emit("vector", lambda v, qsl=qsl, qc=qc, qcs=qcs, pW=pW: v.tensor_tensor(
                    qcs[:], qc[:, qsl], pW[:, 0:512], ALU.mult),
                    reads=[qc, pW], writes=[qcs], tag="gh_qs")
                mk.emit("tensor", lambda t, M=M, qcs=qcs, pW=pW: t.matmul(
                    pW[0:64, 0:512], M[:, 0:64], qcs[:], start=True, stop=True),
                    reads=[M, qcs], writes=[pW], tag="gh_o")
                ob = (s_b16, s_b16b)[hp]
                mk.emit("vector", lambda v, ob=ob, pW=pW: v.tensor_copy(
                    ob[0:64, :], pW[0:64, 0:512]),
                    reads=[pW], writes=[ob], tag="gh_oe")
                mk.dma("sync", oh_v[:, h, t0seq + tq * 512:t0seq + (tq + 1) * 512],
                       ob[0:64, :], reads=[ob], writes=[OH], tag="gh_ost")

    load_weight("L1_Wo")
    load_cols("L1_bo", s_bc)
    oh_cv = OH[:].rearrange("(c p) t -> p c t", p=128)
    bufs = dict(pp=(p0, p1), p5=p5, p6=p6,
                s_mean=s_mean, s_mr=s_mr, s_rstd=s_rstd,
                s_col1f=s_col1f, s_fones=s_fones, s_biascols=s_bc)

    def feed_pre(tt):
        sl = slice(tt * 512, (tt + 1) * 512)
        xt = s_xt[tt % 2]
        mk.dma("sync", xt[:], oh_cv[:, :, sl], reads=[OH], writes=[xt],
               tag="wo_old")

    def feed(tt, fc, ps):
        xt = s_xt[tt % 2]
        for kc in range(8):
            mk.emit("tensor", lambda t, kc=kc, fc=fc, ps=ps, xt=xt: t.matmul(
                ps[:, 0:512], s_w[:, kc, fc * 128:(fc + 1) * 128], xt[:, kc, :],
                start=(kc == 0), stop=(kc == 7)),
                reads=[s_w, xt], writes=[ps], tag="wo_mm")

    emit_ln_stage(ctx, bufs, feed_pre, feed, XF2, XR3, XF3, XB3, bias_cols=s_bc)


def phase_l1_ffn(ctx):
    nc, mk = ctx["nc"], ctx["mk"]
    W = ctx["W"]
    XF3, XB3, XF4 = ctx["XF3"], ctx["XB3"], ctx["XF4"]
    e = ctx["stack"].enter_context
    eye32, eye16 = ctx["s_eye32"], ctx["s_eye16"]
    constcol = ctx["constcol"]
    CAP = float(B * T // NEXP)

    RLOC = nc.dram_tensor("RLOC", [R, NEXP], F32)
    RGL = nc.dram_tensor("RGL", [B * T, NEXP], F32)
    W1T = nc.dram_tensor("W1T", [NEXP, D, DEXP], BF16)
    W2T = nc.dram_tensor("W2T", [NEXP, DEXP, D], BF16)
    XR4 = nc.dram_tensor("XR4", [D, R], F32)
    SSEL = nc.dram_tensor("SSEL", [R, NEXP], F32)
    ctx["RLOC"] = RLOC
    ctx["RGL"] = RGL
    ctx["SSEL"] = SSEL

    tle = np.zeros((128, 8, 8), np.float32)
    tlt = np.zeros((128, 8, 8), np.float32)
    for j in range(8):
        tle[:, j, j + 1:] = 1.0
        tlt[:, j, :j] = 1.0
    d_tle = nc.inline_tensor(tle.reshape(128, 64), name="tmpl_le")
    d_tlt = nc.inline_tensor(tlt.reshape(128, 64), name="tmpl_lt")
    sel1h = np.zeros((8, 8 * 128), np.float32)
    for ex in range(8):
        sel1h[ex, ex * 128:(ex + 1) * 128] = 1.0
    d_sel1h = nc.inline_tensor(sel1h.astype(ml_dtypes.bfloat16), name="sel1h")

    s_tle = e(nc.sbuf_tensor("m_tle", [128, 8, 8], F32))
    s_tlt = e(nc.sbuf_tensor("m_tlt", [128, 8, 8], F32))
    s_sel1h = e(nc.sbuf_tensor("m_sel1h", [8, 8 * 128], BF16))
    mk.dma("sync", s_tle[:].rearrange("p a b -> p (a b)"), d_tle[:],
           reads=[d_tle], writes=[s_tle])
    mk.dma("sync", s_tlt[:].rearrange("p a b -> p (a b)"), d_tlt[:],
           reads=[d_tlt], writes=[s_tlt])
    mk.dma("sync", s_sel1h[:], d_sel1h[:], reads=[d_sel1h], writes=[s_sel1h])

    s_fones = e(nc.sbuf_tensor("m_ones", [1, 512], F32))
    s_col1f = e(nc.sbuf_tensor("m_col1f", [128, 1], F32))
    mk.emit("vector", lambda v: v.memset(s_fones[:], 1.0), writes=[s_fones], memset=True)
    mk.emit("vector", lambda v: v.memset(s_col1f[:], 1.0), writes=[s_col1f], memset=True)

    s_neg = e(nc.sbuf_tensor("m_neg", [128, 1], F32))
    s_t1 = e(nc.sbuf_tensor("m_t1", [128, 512], F32))
    s_t2 = e(nc.sbuf_tensor("m_t2", [128, 512], F32))
    s_selT = e(nc.sbuf_tensor("m_selT", [8, R], BF16))
    p0 = e(nc.psum_tensor("m_p0", [128, 512], F32))
    p1 = e(nc.psum_tensor("m_p1", [128, 512], F32))
    p4 = e(nc.psum_tensor("m_p4", [128, 256], F32))
    p5 = e(nc.psum_tensor("m_p5", [128, 512], F32))
    p6 = e(nc.psum_tensor("m_p6", [128, 512], F32))
    p4b = p4[:].bitcast(BF16)
    s_w1T = e(nc.sbuf_tensor("m_w1T", [128, 8, 1024], BF16))
    s_w2T = e(nc.sbuf_tensor("m_w2T", [128, 8, 1024], BF16))
    s_wl = s_w1T
    for ex in range(NEXP):
        for (WSRC, DSTT) in ((W["L1_w1"], W1T), (W["L1_w2"], W2T)):
            mk.dma("gpsimd", s_wl[:],
                   WSRC[ex].rearrange("(c p) n -> p c n", p=128),
                   reads=[WSRC], writes=[s_wl], tag="wt_ld")
            for oc in range(8):
                for kc in range(8):
                    pt = (p4, p5)[kc % 2]
                    ptb = pt[:].bitcast(BF16)
                    mk.emit("tensor", lambda t, oc=oc, kc=kc, ptb=ptb: t.transpose(
                        ptb[:, 256:384], s_wl[:, oc, kc * 128:(kc + 1) * 128], eye16[:]),
                        reads=[s_wl, eye16], writes=[pt], tag="wt_T")
                    mk.emit("vector", lambda v, oc=oc, kc=kc, ptb=ptb: v.tensor_copy(
                        s_w2T[:, kc, oc * 128:(oc + 1) * 128], ptb[:, 256:384]),
                        reads=[pt], writes=[s_w2T], tag="wt_e")
            mk.dma("scalar", DSTT[ex].rearrange("(c p) n -> p c n", p=128), s_w2T[:],
                   reads=[s_w2T], writes=[DSTT], tag="wt_st")

    from contextlib import ExitStack as _ES2
    with _ES2() as rsub:
        re_ = rsub.enter_context
        s_gwT = re_(nc.sbuf_tensor("m_gwT", [128, 8, 8], F32))
        s_gw = re_(nc.sbuf_tensor("m_gw", [8, D], F32))
        s_probs = re_(nc.sbuf_tensor("m_probs", [128, 32, 8], F32))
        s_S = re_(nc.sbuf_tensor("m_S", [128, 32, 8], F32))
        s_cnt = re_(nc.sbuf_tensor("m_cnt", [128, 32, 8], F32))
        s_R = re_(nc.sbuf_tensor("m_R", [128, 32, 8], F32))
        s_asg = re_(nc.sbuf_tensor("m_asg", [128, 32, 8], F32))
        s_m1 = re_(nc.sbuf_tensor("m_m1", [128, 32, 8], F32))
        s_m2 = re_(nc.sbuf_tensor("m_m2", [128, 32, 8], F32))


        mk.emit("vector", lambda v: v.memset(s_neg[:], -1e9), writes=[s_neg], memset=True)

        mk.dma("sync", s_gw[:], W["L1_gate_w"][:], reads=[W["L1_gate_w"]], writes=[s_gw])
        for kc in range(8):
            mk.emit("tensor", lambda t, kc=kc: t.transpose(
                p4[0:128, 0:8], s_gw[:, kc * 128:(kc + 1) * 128], eye32[0:8, 0:8]),
                reads=[s_gw, eye32], writes=[p4], tag="gl_T")
            mk.emit("vector", lambda v, kc=kc: v.tensor_copy(s_gwT[:, kc, :], p4[0:128, 0:8]),
                    reads=[p4], writes=[s_gwT], tag="gl_Te")
        xf3_v = XF3[:].rearrange("(c p) t -> p c t", p=128)
        s_xf8 = [re_(nc.sbuf_tensor(f"m_xf8{i}", [128, 8, 128], F32)) for i in range(2)]
        for tile in range(32):
            tsl = slice(tile * 128, (tile + 1) * 128)
            xf8 = s_xf8[tile % 2]
            mk.dma("sync", xf8[:], xf3_v[:, :, tsl], reads=[XF3], writes=[xf8],
                   tag="gl_x")
            for kc in range(8):
                mk.emit("tensor", lambda t, kc=kc, xf8=xf8: t.matmul(
                    p0[0:128, 0:8], xf8[:, kc, :], s_gwT[:, kc, :],
                    start=(kc == 0), stop=(kc == 7)),
                    reads=[xf8, s_gwT], writes=[p0], tag="gl_mm")
            mk.emit("vector", lambda v: v.reduce_max(
                s_t1[:, 0:1], p0[0:128, 0:8], AX.X, negate=True),
                reads=[p0], writes=[s_t1], tag="gl_mx")
            mk.emit("scalar", lambda s: s.activation(
                s_t2[:, 0:8], p0[0:128, 0:8], AF.Exp, bias=s_t1[:, 0:1]),
                reads=[p0, s_t1], writes=[s_t2], tag="gl_ex")
            mk.emit("vector", lambda v: v.reduce_sum(s_t1[:, 0:1], s_t2[:, 0:8], AX.X),
                    reads=[s_t2], writes=[s_t1], tag="gl_sm")
            mk.emit("vector", lambda v: v.reciprocal(s_t1[:, 1:2], s_t1[:, 0:1]),
                    reads=[s_t1], writes=[s_t1], tag="gl_rc")
            mk.emit("vector", lambda v, tile=tile: v.tensor_scalar(
                s_probs[:, tile, :], s_t2[:, 0:8], s_t1[:, 1:2], None, ALU.mult),
                reads=[s_t2, s_t1], writes=[s_probs], tag="gl_p")
        mk.emit("vector", lambda v: v.memset(s_S[:], 0.0), writes=[s_S], memset=True)
        mk.emit("vector", lambda v: v.memset(s_cnt[:], 0.0), writes=[s_cnt], memset=True)
        for j in range(8):
            pj = s_probs[:, :, j:j + 1].broadcast_to((128, 32, 8))
            mk.emit("vector", lambda v, pj=pj: v.tensor_tensor(
                s_m1[:], s_probs[:], pj, ALU.is_le),
                reads=[s_probs], writes=[s_m1], tag="rt_le")
            mk.emit("vector", lambda v, pj=pj: v.tensor_tensor(
                s_m2[:], s_probs[:], pj, ALU.is_lt),
                reads=[s_probs], writes=[s_m2], tag="rt_lt")
            tleb = s_tle[:, j:j + 1, :].broadcast_to((128, 32, 8))
            tltb = s_tlt[:, j:j + 1, :].broadcast_to((128, 32, 8))
            mk.emit("vector", lambda v, tleb=tleb: v.tensor_tensor(
                s_m1[:], s_m1[:], tleb, ALU.mult),
                reads=[s_m1, s_tle], writes=[s_m1], tag="rt_c1")
            mk.emit("vector", lambda g, tltb=tltb: g.tensor_tensor(
                s_m2[:], s_m2[:], tltb, ALU.mult),
                reads=[s_m2, s_tlt], writes=[s_m2], tag="rt_c2")
            mk.emit("vector", lambda v: v.tensor_add(s_m1[:], s_m1[:], s_m2[:]),
                    reads=[s_m1, s_m2], writes=[s_m1], tag="rt_c3")
            mk.emit("vector", lambda v, pj=pj: v.tensor_tensor(
                s_m2[:], s_m1[:], pj, ALU.mult),
                reads=[s_m1, s_probs], writes=[s_m2], tag="rt_sp")
            mk.emit("vector", lambda v: v.tensor_add(s_S[:], s_S[:], s_m2[:]),
                    reads=[s_S, s_m2], writes=[s_S], tag="rt_sa")
            mk.emit("vector", lambda g: g.tensor_add(s_cnt[:], s_cnt[:], s_m1[:]),
                    reads=[s_cnt, s_m1], writes=[s_cnt], tag="rt_ca")
        mk.emit("vector", lambda v: v.tensor_scalar(
            s_asg[:], s_S[:], 0.9, None, ALU.is_lt),
            reads=[s_S], writes=[s_asg], tag="rt_as")
        mk.emit("vector", lambda v: v.tensor_sub(s_R[:], s_probs[:], s_cnt[:]),
                reads=[s_probs, s_cnt], writes=[s_R], tag="rt_r1")
        mk.emit("vector", lambda v: v.tensor_scalar(
            s_R[:], s_R[:], -1.0, None, ALU.add),
            reads=[s_R], writes=[s_R], tag="rt_r2")
        mk.emit("vector", lambda v: v.tensor_scalar(
            s_m2[:], s_asg[:], 1e9, -1e9, ALU.mult, ALU.add),
            reads=[s_asg], writes=[s_m2], tag="rt_r3a")
        mk.emit("vector", lambda v: v.tensor_add(s_R[:], s_R[:], s_m2[:]),
                reads=[s_R, s_m2], writes=[s_R], tag="rt_r3b")
        rloc_v = RLOC[:].rearrange("(k p) e -> p k e", p=128)
        mk.dma("sync", rloc_v, s_R[:], reads=[s_R], writes=[RLOC], tag="rt_st")
        mk.collective("AllGather", ALU.bypass, [RLOC[:]], [RGL[:]],
                      reads=[RLOC], writes=[RGL])

        s_rgl = re_(nc.sbuf_tensor("m_rgl", [128, 256, 8], F32))
        s_lo = re_(nc.sbuf_tensor("m_lo", [1, 8], F32))
        s_hi = re_(nc.sbuf_tensor("m_hi", [1, 8], F32))
        s_mid = re_(nc.sbuf_tensor("m_mid", [1, 8], F32))
        s_cnt8 = re_(nc.sbuf_tensor("m_cnt8", [1, 8], F32))
        s_pred = re_(nc.sbuf_tensor("m_pred", [1, 8], F32))
        s_midc = re_(nc.sbuf_tensor("m_midc", [128, 8], F32))
        s_cmp = re_(nc.sbuf_tensor("m_cmp", [128, 256], F32))
        s_c256 = re_(nc.sbuf_tensor("m_c256", [1, 256], F32))
        rgl_v = RGL[:].rearrange("(k p) e -> p k e", p=128)
        mk.dma("sync", s_rgl[:], rgl_v, reads=[RGL], writes=[s_rgl], tag="bi_ld")
        mk.emit("vector", lambda v: v.memset(s_lo[:], -9.0), writes=[s_lo], memset=True)
        mk.emit("vector", lambda v: v.memset(s_hi[:], 0.0), writes=[s_hi], memset=True)
        for it in range(28):
            mk.emit("vector", lambda v: v.tensor_add(s_mid[:], s_lo[:], s_hi[:]),
                    reads=[s_lo, s_hi], writes=[s_mid], tag="bi_m1")
            mk.emit("vector", lambda v: v.tensor_scalar(
                s_mid[:], s_mid[:], 0.5, None, ALU.mult),
                reads=[s_mid], writes=[s_mid], tag="bi_m2")
            mk.emit("tensor", lambda t: t.matmul(
                p4[0:128, 0:8], s_fones[0:1, 0:128], s_mid[:], start=True, stop=True),
                reads=[s_fones, s_mid], writes=[p4], tag="bi_bc")
            mk.emit("vector", lambda v: v.tensor_copy(s_midc[:], p4[0:128, 0:8]),
                    reads=[p4], writes=[s_midc], tag="bi_bce")
            for ex in range(8):
                mk.emit("vector", lambda v, ex=ex: v.tensor_scalar(
                    s_cmp[:], s_rgl[:, :, ex], s_midc[:, ex:ex + 1], None, ALU.is_ge),
                    reads=[s_rgl, s_midc], writes=[s_cmp], tag="bi_cmp")
                mk.emit("tensor", lambda t: t.matmul(
                    p6[0:1, 0:256], s_col1f[:], s_cmp[:], start=True, stop=True),
                    reads=[s_col1f, s_cmp], writes=[p6], tag="bi_mm")
                mk.emit("vector", lambda v, ex=ex: v.reduce_sum(
                    s_cnt8[0:1, ex:ex + 1], p6[0:1, 0:256], AX.X),
                    reads=[p6], writes=[s_cnt8], tag="bi_rs")
            mk.emit("vector", lambda v: v.tensor_scalar(
                s_pred[:], s_cnt8[:], CAP, None, ALU.is_ge),
                reads=[s_cnt8], writes=[s_pred], tag="bi_pr")
            mk.emit("vector", lambda v: v.tensor_sub(s_cnt8[:], s_mid[:], s_lo[:]),
                    reads=[s_mid, s_lo], writes=[s_cnt8], tag="bi_d1")
            mk.emit("vector", lambda v: v.tensor_mul(s_cnt8[:], s_cnt8[:], s_pred[:]),
                    reads=[s_cnt8, s_pred], writes=[s_cnt8], tag="bi_d2")
            mk.emit("vector", lambda v: v.tensor_add(s_lo[:], s_lo[:], s_cnt8[:]),
                    reads=[s_lo, s_cnt8], writes=[s_lo], tag="bi_lo")
            mk.emit("vector", lambda v: v.tensor_sub(s_cnt8[:], s_hi[:], s_mid[:]),
                    reads=[s_hi, s_mid], writes=[s_cnt8], tag="bi_d3")
            mk.emit("vector", lambda v: v.tensor_mul(s_cnt8[:], s_cnt8[:], s_pred[:]),
                    reads=[s_cnt8, s_pred], writes=[s_cnt8], tag="bi_d4")
            mk.emit("vector", lambda v: v.tensor_add(s_hi[:], s_mid[:], s_cnt8[:]),
                    reads=[s_mid, s_cnt8], writes=[s_hi], tag="bi_hi")
        mk.emit("tensor", lambda t: t.matmul(
            p4[0:128, 0:8], s_fones[0:1, 0:128], s_lo[:], start=True, stop=True),
            reads=[s_fones, s_lo], writes=[p4], tag="kp_bc")
        mk.emit("vector", lambda v: v.tensor_copy(s_midc[:], p4[0:128, 0:8]),
                reads=[p4], writes=[s_midc], tag="kp_bce")
        mk.emit("vector", lambda v: v.tensor_tensor(
            s_m1[:], s_R[:], s_midc[:, 0:8].unsqueeze(1).broadcast_to((128, 32, 8)),
            ALU.is_ge),
            reads=[s_R, s_midc], writes=[s_m1], tag="kp_ge")
        mk.emit("vector", lambda v: v.tensor_mul(s_m1[:], s_m1[:], s_asg[:]),
                reads=[s_m1, s_asg], writes=[s_m1], tag="kp_a")
        mk.emit("vector", lambda v: v.tensor_mul(s_m1[:], s_m1[:], s_probs[:]),
                reads=[s_m1, s_probs], writes=[s_m1], tag="kp_p")
        for tile in range(32):
            mk.emit("tensor", lambda t, tile=tile: t.transpose(
                p4[0:8, 0:128], s_m1[:, tile, :], eye32[:]),
                reads=[s_m1, eye32], writes=[p4], tag="sl_T")
            mk.emit("vector", lambda v, tile=tile: v.tensor_copy(
                s_selT[:, tile * 128:(tile + 1) * 128], p4[0:8, 0:128]),
                reads=[p4], writes=[s_selT], tag="sl_Te")


    mk.barrier()

    s_xb = [e(nc.sbuf_tensor(f"m_xb{i}", [128, 8, 512], BF16)) for i in range(2)]
    s_h = [e(nc.sbuf_tensor(f"m_h{i}", [128, 8, 512], BF16)) for i in range(2)]
    psel = [e(nc.psum_tensor(f"m_psel{i}", [128, 512], F32)) for i in range(2)]
    s_y = e(nc.sbuf_tensor("m_y", [128, 8, R // 2], F32))
    xb3_v = XB3[:].rearrange("(c p) t -> p c t", p=128)
    for half in range(2):
        mk.emit("vector", lambda v: v.memset(s_y[:], 0.0), writes=[s_y], memset=True)
        for ex in range(NEXP):
            mk.dma("gpsimd", s_w1T[:], W1T[ex].rearrange("(c p) n -> p c n", p=128),
                   reads=[W1T], writes=[s_w1T], tag="de_w1")
            mk.dma("scalar", s_w2T[:], W2T[ex].rearrange("(c p) n -> p c n", p=128),
                   reads=[W2T], writes=[s_w2T], tag="de_w2")
            for tt in range(4):
                gtt = half * 4 + tt
                sl = slice(gtt * 512, (gtt + 1) * 512)
                lsl = slice(tt * 512, (tt + 1) * 512)
                xb = s_xb[tt % 2]
                sh = s_h[tt % 2]
                pse = psel[tt % 2]
                tg = s_t1 if tt % 2 == 0 else s_t2
                mk.dma("sync", xb[:], xb3_v[:, :, sl], reads=[XB3], writes=[xb],
                       tag="de_x")
                mk.emit("tensor", lambda t, ex=ex, sl=sl, pse=pse: t.matmul(
                    pse[:, 0:512], s_sel1h[:, ex * 128:(ex + 1) * 128], s_selT[:, sl],
                    start=True, stop=True),
                    reads=[s_sel1h, s_selT], writes=[pse], tag="de_sb")
                for fc in range(8):
                    ps = (p0, p1)[fc % 2]
                    for kc in range(8):
                        mk.emit("tensor", lambda t, kc=kc, fc=fc, ps=ps, xb=xb: t.matmul(
                            ps[:, 0:512], s_w1T[:, kc, fc * 128:(fc + 1) * 128],
                            xb[:, kc, :], start=(kc == 0), stop=(kc == 7)),
                            reads=[s_w1T, xb], writes=[ps], tag="de_h")
                    mk.emit("scalar", lambda s, ps=ps, tg=tg: s.activation(
                        tg[:], ps[:, 0:512], AF.Gelu),
                        reads=[ps], writes=[tg], tag="de_g")
                    mk.emit("vector", lambda v, fc=fc, sh=sh, pse=pse, tg=tg:
                            v.tensor_tensor(
                                sh[:, fc, :], tg[:], pse[:, 0:512], ALU.mult),
                            reads=[tg, pse], writes=[sh], tag="de_hs")
                for fc in range(8):
                    ps = (p0, p1)[fc % 2]
                    for kc in range(8):
                        mk.emit("tensor", lambda t, kc=kc, fc=fc, ps=ps, sh=sh: t.matmul(
                            ps[:, 0:512], s_w2T[:, kc, fc * 128:(fc + 1) * 128],
                            sh[:, kc, :], start=(kc == 0), stop=(kc == 7)),
                            reads=[s_w2T, sh], writes=[ps], tag="de_y")
                    mk.emit("vector", lambda v, fc=fc, lsl=lsl, ps=ps: v.tensor_add(
                        s_y[:, fc, lsl], s_y[:, fc, lsl], ps[:, 0:512]),
                        reads=[s_y, ps], writes=[s_y], tag="de_ya")
        bufs = dict(pp=(p0, p1), p5=p5, p6=p6, e=e, cache_key="moe",
                    s_mean=e(nc.sbuf_tensor(f"m_me{half}", [1, 512], F32)),
                    s_mr=e(nc.sbuf_tensor(f"m_mr{half}", [1, 512], F32)),
                    s_rstd=e(nc.sbuf_tensor(f"m_rs{half}", [1, 512], F32)),
                    s_col1f=s_col1f, s_fones=s_fones)

        def feed(tt, fc, ps, half=half):
            lsl = slice((tt - half * 4) * 512, (tt - half * 4 + 1) * 512)
            return (s_y[:, fc, lsl], s_y)

        emit_ln_stage(ctx, bufs, None, feed, XF3, XR4, XF4, None,
                      tt_range=range(half * 4, (half + 1) * 4))


def shard_inputs(inputs):
    x = f32c(inputs["L0_attn_hidden_states"]).reshape(B * T, D)
    wmap = {
        "L0_Wq": "L0_attn_Wq", "L0_Wk": "L0_attn_Wk", "L0_Wv": "L0_attn_Wv",
        "L0_Wg": "L0_attn_Wg", "L0_Wo": "L0_attn_Wo",
        "L0_sk_w1": "L0_attn_sk_w1", "L0_sk_w2": "L0_attn_sk_w2",
        "L0_sk_b2": "L0_attn_sk_b2", "L0_sv_w1": "L0_attn_sv_w1",
        "L0_sv_w2": "L0_attn_sv_w2", "L0_sv_b2": "L0_attn_sv_b2",
        "L0_gnorm_w": "L0_attn_gnorm_w",
        "L0_w_gate": "L0_ffn_w_gate", "L0_w_up": "L0_ffn_w_up",
        "L0_w_down": "L0_ffn_w_down",
        "L1_Wz": "L1_attn_Wz", "L1_bz": "L1_attn_bz",
        "L1_ln_g": "L1_attn_ln_g", "L1_ln_b": "L1_attn_ln_b",
        "L1_Wq": "L1_attn_Wq", "L1_bq": "L1_attn_bq",
        "L1_Wr": "L1_attn_Wr", "L1_br": "L1_attn_br",
        "L1_Wh": "L1_attn_Wh", "L1_bh": "L1_attn_bh",
        "L1_Wphi": "L1_attn_Wphi", "L1_bphi": "L1_attn_bphi",
        "L1_Wo": "L1_attn_Wo", "L1_bo": "L1_attn_bo",
        "L1_gate_w": "L1_ffn_gate_w", "L1_w1": "L1_ffn_w1", "L1_w2": "L1_ffn_w2",
    }
    weights = {k: f32c(inputs[v]) for k, v in wmap.items()}
    in_maps = []
    for c in range(NCORES):
        m = dict(weights)
        m["x0"] = x[c * R:(c + 1) * R]
        in_maps.append(m)
    return in_maps


def kernel(**inputs):
    assert inputs["L1_attn_key_in"].shape[-1] == D
    assert inputs["L1_attn_value"].shape[-1] == D
    in_maps = shard_inputs(inputs)
    nc = build_program()
    res = run_bass_kernel_spmd(nc, in_maps, core_ids=list(range(NCORES)))
    out = np.concatenate([np.asarray(r["y"]) for r in res.results], axis=0)
    return out.reshape(B, T, D)
